# version 20
# baseline (speedup 1.0000x reference)
import numpy as np
import concourse.bass as bass
import concourse.mybir as mybir
from concourse.bass_utils import run_bass_kernel_spmd
from concourse.aot_env import allow_ici_all_to_all

F32 = mybir.dt.float32
BF16 = mybir.dt.bfloat16
AF = mybir.ActivationFunctionType
ALU = mybir.AluOpType
AX = mybir.AxisListType

NCORE = 8
HD = 128
EPS = 1e-5
ALPHA = 1.0 / float(np.sqrt(HD))


def make_cfg(S=8192, D=2048, NH=16, INTER=8192):
    c = dict(S=S, D=D, NH=NH, INTER=INTER)
    c["HPC"] = NH // NCORE
    c["OCW"] = c["HPC"] * HD
    c["TOK"] = S // NCORE
    c["NTT"] = c["TOK"] // 128
    c["TB"] = min(512, c["TOK"])
    c["NB"] = S // c["TB"]
    c["NST"] = S // 512
    c["MC"] = D // 128
    c["NOC"] = D // 512
    c["SEG"] = 512
    c["NSEG"] = INTER // c["SEG"]
    c["TS"] = min(512, c["TOK"])
    c["NTS"] = c["TOK"] // c["TS"]
    c["OFF_WO"] = 0
    c["OFF_GU"] = D * D
    c["OFF_WD"] = D * D + 2 * INTER * D
    c["BLOB"] = D * D + 2 * INTER * D + INTER * D
    assert c["BLOB"] % NCORE == 0
    c["BW"] = c["BLOB"] // NCORE
    assert D == NH * HD
    return c


class _Sem:
    __slots__ = ("h", "n")

    def __init__(self, h):
        self.h = h
        self.n = 0


def build_program(cfg, debug=False):
    S, D, NH, INTER = cfg["S"], cfg["D"], cfg["NH"], cfg["INTER"]
    HPC, OCW, TOK, NTT = cfg["HPC"], cfg["OCW"], cfg["TOK"], cfg["NTT"]
    TB, NB, NST, MC, NOC = cfg["TB"], cfg["NB"], cfg["NST"], cfg["MC"], cfg["NOC"]
    SEG, NSEG, TS, NTS = cfg["SEG"], cfg["NSEG"], cfg["TS"], cfg["NTS"]
    BW = cfg["BW"]
    NVT = S // 128
    VTPB = TB // 128
    ICS = SEG // 128

    nc = bass.Bass("TRN2", target_bir_lowering=False, debug=False)

    x_own = nc.dram_tensor("x_own", [TOK, D], F32, kind="ExternalInput")
    x_full = nc.dram_tensor("x_full", [S, D], F32, kind="ExternalInput")
    wq_t = nc.dram_tensor("wq_t", [D, OCW], F32, kind="ExternalInput")
    wk_t = nc.dram_tensor("wk_t", [D, OCW], F32, kind="ExternalInput")
    wv_t = nc.dram_tensor("wv_t", [D, OCW], F32, kind="ExternalInput")
    bq_c = nc.dram_tensor("bq_c", [OCW, 1], F32, kind="ExternalInput")
    bk_c = nc.dram_tensor("bk_c", [OCW, 1], F32, kind="ExternalInput")
    bv_c = nc.dram_tensor("bv_c", [1, OCW], F32, kind="ExternalInput")
    bo_c = nc.dram_tensor("bo_c", [1, D], F32, kind="ExternalInput")
    wblob = nc.dram_tensor("wblob", [1, BW], F32, kind="ExternalInput")
    out_own = nc.dram_tensor("out_own", [TOK, D], F32, kind="ExternalOutput")

    xn_own_dram = nc.dram_tensor("xn_own_dram", [D, TOK], BF16)
    ag_w_in = nc.dram_tensor("ag_w_in", [1, BW], BF16)
    ag_w_out = nc.dram_tensor("ag_w_out", [NCORE, BW], BF16, addr_space="Shared")
    PH = 2 if TOK >= 1024 else 1
    a2a_in_p = [nc.dram_tensor("a2a_in%d" % p, [NCORE * OCW, TOK // PH], BF16)
                for p in range(PH)]
    a2a_out_p = [nc.dram_tensor("a2a_out%d" % p, [NCORE * OCW, TOK // PH], BF16)
                 for p in range(PH)]

    dbg = {}
    if debug:
        dbg["xn"] = nc.dram_tensor("dbg_xn", [D, TOK], BF16, kind="ExternalOutput")
        dbg["q"] = nc.dram_tensor("dbg_q", [OCW, S], BF16, kind="ExternalOutput")
        dbg["k"] = nc.dram_tensor("dbg_k", [OCW, S], BF16, kind="ExternalOutput")
        dbg["v"] = nc.dram_tensor("dbg_v", [S, OCW], BF16, kind="ExternalOutput")
        dbg["bc"] = nc.dram_tensor("dbg_bc", [128, HPC], F32, kind="ExternalOutput")

    from contextlib import ExitStack
    stack = ExitStack()
    en = stack.enter_context

    K_ = 1024
    AR_BYTES = 192 * K_
    arena = en(nc.sbuf_tensor("arena", [128, AR_BYTES // 4], F32))
    arf = arena[:, :]
    arb = arf.bitcast(BF16)

    def bfv(off, cols, parts=128):
        assert off % 32 == 0 and off + 2 * cols <= AR_BYTES, (off, cols)
        return arb[0:parts, off // 2: off // 2 + cols]

    def f32v(off, cols, parts=128):
        assert off % 32 == 0 and off + 4 * cols <= AR_BYTES, (off, cols)
        return arf[0:parts, off // 4: off // 4 + cols]

    o = 0
    qw_sb = bfv(o, MC * OCW); o += 2 * MC * OCW
    kw_sb = bfv(o, MC * OCW); o += 2 * MC * OCW
    vw_sb = bfv(o, MC * OCW); o += 2 * MC * OCW
    _qkv_end = o
    Q_sb = bfv(o, HPC * S); o += 2 * HPC * S
    K_sb = bfv(o, HPC * S); o += 2 * HPC * S
    V_sb = bfv(o, NVT * OCW); o += 2 * NVT * OCW
    _xns = o
    xn_blk = [bfv(_xns + i * 2 * MC * TB, MC * TB) for i in range(2)]
    o += 2 * 2 * MC * TB
    _attn_hi = o
    TAIL0 = AR_BYTES - 24 * K_
    assert _attn_hi <= TAIL0
    p_buf = [bfv(TAIL0 + i * K_, 512) for i in range(4)]
    masks = [bfv(TAIL0 + 4 * K_ + i * K_, 512) for i in range(4)]
    ho_buf = [bfv(TAIL0 + 8 * K_ + i * K_, 512) for i in range(2)]
    r_sb2 = [f32v(TAIL0 + 10 * K_ + i * 2 * K_, 512) for i in range(2)]

    NXT = 2
    xt = [f32v(_attn_hi + i * 4 * D, D) for i in range(NXT)]
    xnb = [bfv(TAIL0 + 14 * K_ + i * 2 * D, D) for i in range(2)]
    assert _attn_hi + NXT * 4 * D <= TAIL0
    assert 14 * K_ + 4 * D <= 24 * K_

    accum = f32v(0, NTT * D)
    _A = 4 * NTT * D
    ho_own = bfv(_A, MC * TOK)
    xn_own = bfv(_A, MC * TOK)
    _B = _A + 2 * MC * TOK
    gu_buf = [bfv(_B, MC * 2 * SEG),
              bfv(AR_BYTES - 4 * MC * SEG, MC * 2 * SEG)]
    _C = _B + 2 * MC * K_
    wo_buf = [bfv(_xns + i * MC * K_, MC * 512) for i in range(2)]
    _wo_end = _xns + 2 * MC * K_
    _x_off = _B if _B + 8 * D <= _xns else _wo_end
    x_t4 = [f32v(_x_off + i * 4 * D, D) for i in range(2)]
    bo_f = f32v(0, D, parts=1)
    wd_buf = bfv(_C, ICS * D)
    _E = _C + 8 * D
    h_sb = bfv(_E, ICS * TOK)
    _F = max(_E + 2 * ICS * TOK, _wo_end,
             _x_off + 8 * D if _x_off >= _wo_end else 0)
    bo_hi = bfv(_F, D, parts=1)
    bo_lo = bfv(_F + 2 * D, D, parts=1)
    assert _F + 4 * D <= AR_BYTES - 4 * MC * SEG, ("P4 map overflow", _F, AR_BYTES)

    gs_sb = en(nc.sbuf_tensor("gs_sb", [128, TS], BF16))[:, :]
    consts = en(nc.sbuf_tensor("consts", [128, 1280], BF16))
    ident = consts[:, 0:128]
    ones_col = consts[:, 128:129]
    ones_row = consts[0:1, 160:288]
    ones_mat = consts[:, 1024:1152]
    bv_hi = consts[0:1, 288:288 + OCW]
    bv_lo = consts[0:1, 544:544 + OCW]
    fsm = en(nc.sbuf_tensor("fsm", [128, 3072], F32))
    ones_f32 = fsm[0:1, 0:128]
    stats = fsm[0:1, 1664:1728]
    sc1 = fsm[0:1, 1728:1736]
    bq_sb = fsm[:, 1744:1744 + HPC]
    bk_sb = fsm[:, 1752:1752 + HPC]
    biasC = fsm[:, 1760:1760 + HPC]
    mst = fsm[:, 1776:1792]
    bv_f = fsm[0:1, 1792:1792 + OCW]
    bv_t1 = fsm[0:1, 384:384 + OCW]
    bnst = [fsm[:, 640 + 24 * s_:640 + 24 * s_ + 24] for s_ in range(4)]
    mvs = fsm[:, 736:752]
    veps = [fsm[:, 752 + 2 * q_:754 + 2 * q_] for q_ in range(2)]
    lnR2 = [fsm[:, 760 + 2 * q_:762 + 2 * q_] for q_ in range(2)]
    lnM2 = [fsm[:, 768 + 2 * q_:770 + 2 * q_] for q_ in range(2)]
    identF = fsm[:, 2304:2432]
    bo_t1 = fsm[0:1, 2432:2432 + 512]

    psA = en(nc.psum_tensor("psA", [128, 1024], F32))
    psB = en(nc.psum_tensor("psB", [128, 1024], F32))
    psm = [en(nc.psum_tensor("psm%d" % i, [128, 512], F32)) for i in range(4)]

    NCH = 6
    dch = {"y": [_Sem(en(nc.semaphore("sY%d" % i))) for i in range(NCH)],
           "g": [_Sem(en(nc.semaphore("sG%d" % i))) for i in range(NCH)]}
    _rr = {"y": 0, "g": 0}
    tE = _Sem(en(nc.semaphore("tE")))
    vE = _Sem(en(nc.semaphore("vE")))
    scE = _Sem(en(nc.semaphore("scE")))
    gE = _Sem(en(nc.semaphore("gE")))
    ccs = [_Sem(en(nc.semaphore("cc%d" % i))) for i in range(3)]

    ops = {k: [] for k in "tvsyg"}
    _wmax = {}

    def em(e, f):
        ops[e].append(f)

    def wait(e, tok):
        if not tok:
            return
        sem, val = tok
        if val <= 0:
            return
        key = (e, id(sem))
        if _wmax.get(key, 0) >= val:
            return
        _wmax[key] = val
        em(e, lambda eng, _s=sem.h, _t=val: eng.wait_ge(_s, _t))

    def tmax(*toks):
        best = 0
        for t in toks:
            if t and (not best or t[1] > best[1]):
                if best:
                    assert t[0] is best[0]
                best = t
        return best

    def dma(e, out, in_, **kw):
        ch = _rr[e] % NCH
        _rr[e] += 1
        sem = dch[e][ch]
        wait(e, (sem, sem.n))
        sem.n += 16
        em(e, lambda eng, _o=out, _i=in_, _s=sem.h, _k=dict(kw):
           eng.dma_start(out=_o, in_=_i, **_k).then_inc(_s, 16))
        return (sem, sem.n)

    def run(e, sem, meth, *a, **kw):
        sem.n += 1
        em(e, lambda eng, _m=meth, _a=a, _k=dict(kw), _s=sem.h:
           getattr(eng, _m)(*_a, **_k).then_inc(_s, 1))
        return (sem, sem.n)

    def run0(e, meth, *a, **kw):
        em(e, lambda eng, _m=meth, _a=a, _k=dict(kw): getattr(eng, _m)(*_a, **_k))

    def mm(out, lhsT, rhs, start, stop):
        run0("t", "matmul", out, lhsT, rhs, start=start, stop=stop,
             skip_group_check=True)

    def mm_i(out, lhsT, rhs, start, stop):
        return run("t", tE, "matmul", out, lhsT, rhs, start=start, stop=stop,
                   skip_group_check=True)


    def fence(wait_engine, q):
        for sem in dch[q]:
            wait(wait_engine, (sem, sem.n))

    qkv_toks = []
    for (w_in, w_sb) in ((wq_t, qw_sb), (wk_t, kw_sb), (wv_t, vw_sb)):
        qkv_toks.append(dma("g", w_sb.rearrange("p (c o) -> p c o", c=MC),
                            w_in.ap().rearrange("(c p) o -> p c o", p=128)))
    b_toks = [dma("y", bq_sb[:, :], bq_c.ap().rearrange("(c p) one -> p (c one)", p=128),
                  allow_slow_non_contiguous=True),
              dma("y", bk_sb[:, :], bk_c.ap().rearrange("(c p) one -> p (c one)", p=128),
                  allow_slow_non_contiguous=True),
              dma("y", bv_f[:, :], bv_c[:, :])]

    run("g", gE, "memset", ident, 1.0)
    run("g", gE, "memset", identF, 1.0)
    run("g", gE, "memset", consts[:, 128:160], 1.0)
    run("g", gE, "memset", consts[0:1, 160:288], 1.0)
    run("g", gE, "memset", ones_mat, 1.0)
    run("g", gE, "memset", fsm[0:1, 0:128], 1.0)
    for j in range(4):
        run("g", gE, "memset", masks[j], 1.0)
    wait("g", (gE, gE.n))
    run("g", gE, "affine_select", out=ident, in_=ident, compare_op=ALU.is_equal,
        fill=0.0, base=0, pattern=[[-1, 128]], channel_multiplier=1)
    run("g", gE, "affine_select", out=identF, in_=identF, compare_op=ALU.is_equal,
        fill=0.0, base=0, pattern=[[-1, 128]], channel_multiplier=1)
    for j in range(4):
        run("g", gE, "affine_select", out=masks[j], in_=masks[j],
            compare_op=ALU.is_ge, fill=0.0, base=-128 * j,
            pattern=[[1, 512]], channel_multiplier=-1)
    g_const = (gE, gE.n)

    for t in b_toks:
        wait("v", t)
    run0("v", "tensor_copy", bv_hi, bv_f)
    run0("v", "drain")
    run0("v", "tensor_copy", bv_t1, bv_hi)
    run0("v", "drain")
    run0("v", "tensor_sub", bv_t1, bv_f, bv_t1)
    run0("v", "drain")
    run0("v", "tensor_copy", bv_lo, bv_t1)
    bv_split = run("v", vE, "drain")

    NTB = TB // 128
    NG = D // 512
    NGR = (MC + 7) // 8
    assert NG <= 4 and MC % 8 == 0
    psAb = psA[:, :].bitcast(BF16)
    psBb = psB[:, :].bitcast(BF16)
    grans = [psAb[:, 1024:2048], psBb[:, 1024:2048]]
    gran_ev = [0, 0]
    pr_nrm = {}
    pr_agg = {}
    pr_pair = {}
    pr_phB_rd = {0: 0, 1: 0}
    ship_war = {0: [], 1: []}
    xnb_free = {0: 0, 1: 0}
    xnb_war = {}
    blk_ready = {}
    _gt = [0]
    _pq = [0]
    _gu = [0]
    _pend = []
    _xd_tok = {}

    def _flush_pair():
        if not _pend:
            return
        pend = list(_pend)
        _pend.clear()
        w = len(pend)
        q = _pq[0] % 2
        _pq[0] += 1
        s0 = pend[0][0] % 4
        mv2 = mvs[:, 2 * s0:2 * s0 + 2 * w].rearrange("p (s two) -> p two s", two=2)
        wait("v", (vE, vE.n))
        wait("v", pr_phB_rd[q])
        run0("v", "tensor_scalar_add", veps[q][:, 0:w], mv2[:, 1, :], float(EPS))
        vt = run("v", vE, "drain")
        wait("s", vt)
        st = run("s", scE, "activation", veps[q][:, 0:w], veps[q][:, 0:w], AF.Sqrt)
        wait("v", st)
        run0("v", "reciprocal", lnR2[q][:, 0:w], veps[q][:, 0:w])
        run0("v", "drain")
        ptok = run("v", vE, "scalar_tensor_tensor", lnM2[q][:, 0:w],
                   mv2[:, 0, :], -1.0, lnR2[q][:, 0:w], ALU.mult, ALU.mult)
        pr_pair[pend[0][0] // 2] = ptok
        for j, (t, slot, par, toff, bkey) in enumerate(pend):
            xb = t % 2
            wait("s", ptok)
            wait("s", _xd_tok.pop(t))
            wait("s", xnb_war.get(xb, 0))
            nt_ = run("s", scE, "activation", xnb[xb], xt[slot], AF.Identity,
                      scale=lnR2[q][:, j:j + 1], bias=lnM2[q][:, j:j + 1])
            pr_nrm[t] = nt_
            wait("t", g_const)
            wait("t", nt_)
            for gr in range(NGR):
                cw = min(8, MC - 8 * gr)
                g2 = _gu[0] % 2
                _gu[0] += 1
                wait("t", gran_ev[g2])
                tt2 = 0
                for c8 in range(cw):
                    c = 8 * gr + c8
                    tt2 = run("t", tE, "transpose",
                              grans[g2][:, 128 * c8:128 * (c8 + 1)],
                              xnb[xb][:, 128 * c:128 * (c + 1)], ident)
                xnb_war[xb] = tt2
                wait("v", tt2)
                wait("v", xnb_free[par])
                for stok in ship_war[par]:
                    wait("v", stok)
                dstv = xn_blk[par].rearrange("p (c t) -> p c t", c=MC)[
                    :, 8 * gr:8 * gr + cw, toff:toff + 128]
                srcv = grans[g2].rearrange("p (c t) -> p c t", c=8)[:, 0:cw, :]
                gran_ev[g2] = run("v", vE, "tensor_copy", dstv, srcv)
            blk_ready[bkey] = gran_ev[(_gu[0] - 1) % 2]
        pr_phB_rd[q] = pr_nrm[pend[-1][0]]

    def produce_tile(dq, src_rows, par, toff, bkey):
        t = _gt[0]
        _gt[0] += 1
        slot, s4 = t % NXT, t % 4
        wait(dq, pr_nrm.get(t - NXT, 0))
        wait(dq, pr_agg.get(t - NXT, 0))
        xd = dma(dq, xt[slot], src_rows)
        _xd_tok[t] = xd
        wait("v", xd)
        wait("v", pr_agg.get(t - 4, 0))
        wait("v", pr_pair.get((t - 4) // 2, 0))
        for g in range(NG):
            run("v", vE, "bn_stats", bnst[s4][:, 6 * g:6 * g + 6],
                xt[slot][:, 512 * g:512 * (g + 1)])
        wait("v", (vE, vE.n))
        pr_agg[t] = run("v", vE, "bn_aggr", mvs[:, 2 * s4:2 * s4 + 2],
                        bnst[s4][:, 0:6 * NG])
        _pend.append((t, slot, par, toff, bkey))
        if len(_pend) == 2:
            _flush_pair()

    def produce_block(gb, src_t, row0, bkey):
        par = gb % 2
        for j in range(NTB):
            produce_tile(("y", "g")[_gt[0] % 2],
                         src_t.ap()[row0 + 128 * j:row0 + 128 * (j + 1), :],
                         par, 128 * j, bkey)
        return par

    NOB = TOK // TB
    for ob in range(NOB):
        par = produce_block(ob, x_own, ob * TB, ("own", ob))
        _flush_pair()
        if _gt[0] % 2:
            _gt[0] += 1
        wait("y", blk_ready[("own", ob)])
        shv = xn_blk[par].rearrange("p (c t) -> p c t", c=MC)
        toks = [dma("y", xn_own_dram.ap()[:, ob * TB:(ob + 1) * TB]
                    .rearrange("(c p) t -> p c t", p=128), shv)]
        if debug:
            toks.append(dma("y", dbg["xn"].ap()[:, ob * TB:(ob + 1) * TB]
                            .rearrange("(c p) t -> p c t", p=128), shv))
        ship_war[par] = toks
    qk_ev = {0: 0, 1: 0}
    v_ev = {0: 0, 1: 0}
    last_q_ev = 0
    last_v_ev = 0
    sq_free = {0: 0, 1: 0, 2: 0, 3: 0}
    nrm_free = {0: 0, 1: 0}
    nsq = [0]
    npp = [0]
    pending_norm = []

    def flush_norms():
        while pending_norm:
            sqv, idx, sqt = pending_norm.pop(0)
            pp2 = npp[0] % 2
            npp[0] += 1
            wait("t", sqt)
            wait("t", nrm_free[pp2])
            psN = (psA, psB)[pp2][0:1, 0:512]
            mt2 = mm_i(psN[:, 0:TB] if TB < 512 else psN, ones_col, sqv,
                       start=True, stop=True)
            sq_free[nsq[0] % 4] = mt2
            nsq[0] += 1
            wait("v", mt2)
            nrm_free[pp2] = run("v", vE, "reduce_max", stats[0:1, idx:idx + 1],
                                (psA, psB)[pp2][0:1, 0:TB], axis=AX.X)
    def emit_qkv(b):
        nonlocal last_q_ev, last_v_ev
        bi = b
        par = (NOB + b) % 2
        flush_norms()
        wait("t", blk_ready[b])
        if bi < 2:
            for t in qkv_toks:
                wait("t", t)
            wait("t", g_const)
            for t in b_toks:
                wait("t", t)
        xv = xn_blk[par].rearrange("p (c t) -> p c t", c=MC)
        for which, (wsb, bcol, dst) in enumerate(((qw_sb, bq_sb, Q_sb),
                                                  (kw_sb, bk_sb, K_sb))):
            wv = wsb.rearrange("p (c o) -> p c o", c=MC)
            for h in range(HPC):
                pp = (bi * 2 * HPC + which * HPC + h) % 2
                ps = psm[pp]
                wait("t", qk_ev[pp])
                mt = 0
                for m in range(MC):
                    if m < MC - 1:
                        mm(ps[:, 0:TB], wv[:, m, 128 * h:128 * (h + 1)], xv[:, m, :],
                           start=(m == 0), stop=False)
                    else:
                        mt = mm_i(ps[:, 0:TB], wv[:, m, 128 * h:128 * (h + 1)],
                                  xv[:, m, :], start=(m == 0), stop=True)
                wait("s", mt)
                qk_ev[pp] = run("s", scE, "activation",
                                dst[:, S * h + TB * b:S * h + TB * (b + 1)],
                                ps[:, 0:TB], AF.Identity, bias=bcol[:, h:h + 1])
                last_q_ev = qk_ev[pp]
                sqp = (nsq[0] + len(pending_norm)) % 4
                wait("s", qk_ev[pp])
                wait("s", sq_free[sqp])
                sqt = run("s", scE, "activation", p_buf[sqp][:, 0:TB],
                          dst[:, S * h + TB * b:S * h + TB * (b + 1)], AF.Square)
                pending_norm.append((p_buf[sqp][:, 0:TB],
                                     (2 * h + which) * 16 + b, sqt))
        wv = vw_sb.rearrange("p (c o) -> p c o", c=MC)
        npair = (VTPB + 1) // 2
        for pair in range(npair):
            pp = 2 + (bi * npair + pair) % 2
            ps = psm[pp]
            wait("t", v_ev[pp - 2])
            wait("t", bv_split)
            nj = min(2, VTPB - 2 * pair)
            mt = 0
            for j in range(nj):
                t4 = 2 * pair + j
                for m in range(MC):
                    mm(ps[:, OCW * j:OCW * (j + 1)], xv[:, m, 128 * t4:128 * (t4 + 1)],
                       wv[:, m, :], start=(m == 0), stop=False)
                mm(ps[:, OCW * j:OCW * (j + 1)], ones_row, bv_hi, start=False, stop=False)
                mt = mm_i(ps[:, OCW * j:OCW * (j + 1)], ones_row, bv_lo,
                          start=False, stop=True)
            wait("v", mt)
            base = OCW * (VTPB * b + 2 * pair)
            v_ev[pp - 2] = run("v", vE, "tensor_copy",
                               V_sb[:, base:base + nj * OCW], ps[:, 0:nj * OCW])
            last_v_ev = v_ev[pp - 2]
        xnb_free[par] = (tE, tE.n)

    for b in range(NB):
        produce_block(NOB + b, x_full, b * TB, b)
        if b >= 1:
            emit_qkv(b - 1)
    _flush_pair()
    emit_qkv(NB - 1)

    t_blob = dma("g", ag_w_in[:, :], wblob[:, :])
    wait("g", t_blob)
    _wag_hole = [None]
    em("g", lambda g: g.wait_ge(*_wag_hole[0]))
    ccs[1].n = 1
    cc_w = (ccs[1], 1)
    em("g", lambda g: g.collective_compute(
        "AllGather", ALU.bypass, replica_groups=[list(range(NCORE))],
        ins=[ag_w_in.ap().opt()], outs=[ag_w_out.ap().opt()]).then_inc(ccs[1].h, 1))

    p2_t_end = (tE, tE.n)
    blobv = ag_w_out.ap().rearrange("r w -> (r w)")

    def blob_slice(off, rows, cols):
        return blobv[off:off + rows * cols].rearrange("(c p o) -> p c o", p=128, o=cols)

    OFF_WO, OFF_GU, OFF_WD = cfg["OFF_WO"], cfg["OFF_GU"], cfg["OFF_WD"]
    wait("g", cc_w)
    wait("g", p2_t_end)
    wo_tok = {0: dma("g", wo_buf[0].rearrange("p (c o) -> p c o", c=MC),
                     blob_slice(OFF_WO, D, D)[:, :, 0:512])}
    if NOC > 1:
        wo_tok[1] = dma("g", wo_buf[1].rearrange("p (c o) -> p c o", c=MC),
                        blob_slice(OFF_WO, D, D)[:, :, 512:1024])

    if debug:
        wait("y", last_q_ev)
        wait("y", last_v_ev)
        dma("y", dbg["q"].ap().rearrange("(h p) t -> p h t", p=128),
            Q_sb.rearrange("p (h t) -> p h t", h=HPC))
        dma("y", dbg["k"].ap().rearrange("(h p) t -> p h t", p=128),
            K_sb.rearrange("p (h t) -> p h t", h=HPC))
        dma("y", dbg["v"].ap().rearrange("(n p) o -> p n o", p=128),
            V_sb.rearrange("p (n o) -> p n o", n=NVT))

    flush_norms()
    wait("v", tmax(nrm_free[0], nrm_free[1]))
    run0("v", "drain")
    biasC_tok = 0
    for h in range(HPC):
        run0("v", "reduce_max", sc1[0:1, 0:1],
             stats[0:1, 2 * h * 16:2 * h * 16 + NB], axis=AX.X)
        run0("v", "drain")
        run0("v", "reduce_max", sc1[0:1, 1:2],
             stats[0:1, (2 * h + 1) * 16:(2 * h + 1) * 16 + NB], axis=AX.X)
        run0("v", "drain")
        vt = run("v", vE, "tensor_tensor", sc1[0:1, 2:3], sc1[0:1, 0:1],
                 sc1[0:1, 1:2], ALU.mult)
        wait("s", vt)
        st = run("s", scE, "activation", sc1[0:1, 3:4], sc1[0:1, 2:3], AF.Sqrt)
        wait("v", st)
        vt = run("v", vE, "tensor_scalar_mul", sc1[0:1, 4:5], sc1[0:1, 3:4], -ALPHA)
        wait("t", vt)
        wait("t", last_v_ev)
        mt = mm_i(psm[2][:, 0:1], ones_f32, sc1[0:1, 4:5], start=True, stop=True)
        wait("v", mt)
        biasC_tok = run("v", vE, "tensor_copy", biasC[:, h:h + 1], psm[2][:, 0:1])
        run0("v", "drain")
    p2norm_t = (tE, tE.n)
    if debug:
        wait("y", biasC_tok)
        dma("y", dbg["bc"].ap(), biasC)

    wait("y", p2_t_end)
    bo_tok0 = dma("y", bo_f[:, :], bo_c[:, :])
    wait("v", bo_tok0)
    wait("v", pr_nrm[_gt[0] - 1])
    wait("v", (vE, vE.n))
    run0("v", "tensor_copy", bo_hi, bo_f)
    run0("v", "drain")
    for q in range(D // 512):
        run0("v", "tensor_copy", bo_t1, bo_hi[0:1, 512 * q:512 * (q + 1)])
        run0("v", "drain")
        run0("v", "tensor_sub", bo_t1, bo_f[0:1, 512 * q:512 * (q + 1)], bo_t1)
        run0("v", "drain")
        run0("v", "tensor_copy", bo_lo[0:1, 512 * q:512 * (q + 1)], bo_t1)
        run0("v", "drain")
    bo_split = run("v", vE, "drain")

    scq = [psA[:, 0:512], psA[:, 512:1024], psB[:, 0:512], psB[:, 512:1024]]
    exp_tok = {q: 0 for q in range(4)}
    pbuf_cons = {q: p2norm_t for q in range(4)}
    sc_cons = {q: nrm_free[q // 2] for q in range(4)}
    sc_cons[1] = tmax(sc_cons[1], gran_ev[0])
    sc_cons[3] = tmax(sc_cons[3], gran_ev[1])
    av_free = {0: last_v_ev, 1: last_v_ev}
    sum_free = {0: biasC_tok, 1: biasC_tok}
    ho_free = {0: [], 1: []}
    wait("s", biasC_tok)
    wait("t", last_q_ev)
    wait("t", last_v_ev)
    a2a_ship = {p: [] for p in range(PH)}
    TOKC = min(TOK // PH, 512) if PH == 2 else min(TOK, 512)
    LA = 3
    TLA = 4

    if PH == 2:
        strip_order = [i for i in range(NST - 2, -1, -2)] + \
                      [i for i in range(NST - 1, 0, -2)]
    else:
        strip_order = list(range(NST))
    jobs = []
    for i in strip_order:
        nt = 4 * (i + 1)
        for h in range(HPC):
            for kt in range(nt):
                jobs.append((i, kt, h))
    njobs = len(jobs)
    avt_h = {}
    tails = {}

    def emit_sc(g):
        i, kt, h = jobs[g]
        q4 = g % 4
        qs = Q_sb[:, S * h + 512 * i:S * h + 512 * (i + 1)]
        wait("t", exp_tok[q4])
        wait("t", sc_cons[q4])
        mt = mm_i(scq[q4], K_sb[:, S * h + 128 * kt:S * h + 128 * (kt + 1)],
                  qs, start=True, stop=True)
        if g % 2 == 0:
            return
        qe, qo = (g - 1) % 4, q4
        pj = bfv(TAIL0 + ((g - 1) % 4) * K_, 1024)
        psrc = (psA, psB)[(g % 4) // 2][:, 0:1024]
        wait("s", mt)
        wait("s", pbuf_cons[qe])
        wait("s", pbuf_cons[qo])
        et = run("s", scE, "activation", pj, psrc, AF.Exp,
                 bias=biasC[:, h:h + 1], scale=ALPHA)
        exp_tok[qe] = exp_tok[qo] = et
        sc_cons[qe] = sc_cons[qo] = 0
        k0 = jobs[g - 1][1]
        if kt >= 4 * i or k0 >= 4 * i:
            wait("v", et)
            for (qq, kk) in ((qe, k0), (qo, kt)):
                if kk >= 4 * i:
                    run0("v", "tensor_tensor", p_buf[qq], p_buf[qq],
                         masks[kk - 4 * i], ALU.mult)
            gt = run("v", vE, "drain")
            exp_tok[qe] = exp_tok[qo] = gt

    def emit_av(g):
        i, kt, h = jobs[g]
        q4 = g % 4
        nt = 4 * (i + 1)
        wait("t", exp_tok[q4])
        if kt == 0:
            wait("t", av_free[h])
            wait("t", sum_free[h])
        mm(psm[h][:, 0:512],
           V_sb[:, OCW * kt + 128 * h:OCW * kt + 128 * (h + 1)],
           p_buf[q4], start=(kt == 0), stop=(kt == nt - 1))
        avt = mm_i(psm[2 + h][:, 0:512], ones_mat, p_buf[q4],
                   start=(kt == 0), stop=(kt == nt - 1))
        pbuf_cons[q4] = avt
        if kt == nt - 1:
            avt_h[(i, h)] = avt
            tails[g + TLA] = (i, h)

    def emit_tail(i, h):
        hb = (i * HPC + h) % 2
        wait("v", avt_h[(i, h)])
        for t in ho_free[hb]:
            wait("v", t)
        vtr = run("v", vE, "reciprocal", r_sb2[hb], psm[2 + h][:, 0:512])
        sum_free[h] = vtr
        run0("v", "drain")
        vt = run("v", vE, "tensor_tensor", ho_buf[hb], psm[h][:, 0:512],
                 r_sb2[hb], ALU.mult)
        av_free[h] = vt
        wait("y", vt)
        ho_free[hb] = []
        for u in range(512 // TOKC):
            g_t = 512 * i + u * TOKC
            rows0 = OCW * (g_t // TOK) + 128 * h
            hp = (g_t % TOK) // (TOK // PH)
            col0 = g_t % (TOK // PH)
            tok = dma("y", a2a_in_p[hp].ap()[rows0:rows0 + 128, col0:col0 + TOKC],
                      ho_buf[hb][:, u * TOKC:(u + 1) * TOKC])
            ho_free[hb].append(tok)
            a2a_ship[hp].append(tok)

    wag_gate = [None]
    xfree_gate = [None]
    XSAFE = 4
    _remaining = set(strip_order)
    _done_ct = {}
    for i in strip_order:
        _done_ct[i] = 4 * (i + 1) * HPC
    _per_strip_left = dict(_done_ct)
    for g in range(njobs + TLA + 1):
        if g in tails:
            i_, h_ = tails.pop(g)
            emit_tail(i_, h_)
        if g < njobs:
            emit_sc(g)
        if g - LA >= 0 and g - LA < njobs:
            emit_av(g - LA)
            i_av = jobs[g - LA][0]
            _per_strip_left[i_av] -= 1
            if _per_strip_left[i_av] == 0:
                _remaining.discard(i_av)
                if xfree_gate[0] is None and all(i2 < XSAFE for i2 in _remaining):
                    xfree_gate[0] = (tE, tE.n)
        if g == njobs // 3:
            wag_gate[0] = (tE, tE.n)
    p3_t_end = (tE, tE.n)
    if xfree_gate[0] is None:
        xfree_gate[0] = p3_t_end
    _wag_hole[0] = (wag_gate[0][0].h, wag_gate[0][1])

    cc_a2a_p = []
    for p in range(PH):
        for t in a2a_ship[p]:
            wait("g", t)
        ccs[2].n += 1
        cc_a2a_p.append((ccs[2], ccs[2].n))

        def _emit_a2a(g, _p=p):
            with allow_ici_all_to_all():
                g.collective_compute(
                    "AllToAll", ALU.bypass, replica_groups=[list(range(NCORE))],
                    ins=[a2a_in_p[_p].ap().opt()],
                    outs=[a2a_out_p[_p].ap().opt()]).then_inc(ccs[2].h, 1)
        em("g", _emit_a2a)
    cc_a2a = cc_a2a_p[-1]

    wait("g", p3_t_end)
    gu_tok = {0: dma("g", gu_buf[1].rearrange("p (c f) -> p c f", c=MC),
                     blob_slice(OFF_GU, D, 2 * SEG))}
    gu_home = {0: 1}
    wait("y", p3_t_end)
    ho_own_toks = []
    for p in range(PH):
        wait("y", cc_a2a_p[p])
        ho_own_toks.append(
            dma("y", ho_own.rearrange("p (c t) -> p c t", c=MC)[
                :, :, (TOK // PH) * p:(TOK // PH) * (p + 1)],
                a2a_out_p[p].ap().rearrange("(c p) t -> p c t", p=128)))

    x4_done = {}
    acc_done = {}
    for tt in range(NTT):
        par = tt % 2
        wait("y", x4_done.get(tt - 2, 0))
        wait("y", xfree_gate[0])
        x4_tok = dma("y", x_t4[par], x_own.ap()[128 * tt:128 * (tt + 1), :])
        wait("v", x4_tok)
        wait("v", p3_t_end)
        if tt == 0:
            wait("v", bo_split)
        acc_done[tt] = run("v", vE, "tensor_copy",
                           accum[:, D * tt:D * (tt + 1)], x_t4[par])
        x4_done[tt] = acc_done[tt]
    run0("v", "drain")
    acc_init = acc_done[NTT - 1]

    hov = ho_own.rearrange("p (c t) -> p c t", c=MC)
    op_ev = {0: acc_init, 1: acc_init}
    NHLF = min(PH, 2)
    tt_half = [[tt for tt in range(NTT) if (tt * 128) // max(1, TOK // NHLF) == hf]
               for hf in range(NHLF)]
    use_seq = [(hf, oc) for hf in range(NHLF) for oc in range(NOC)]
    wo_use_tok = {}
    wo_cons = {}
    for ui, (hf, oc) in enumerate(use_seq):
        if ui < 2:
            wo_use_tok[ui] = wo_tok[min(oc, max(wo_tok))]
    for ui, (hf, oc) in enumerate(use_seq):
        par = ui % 2
        wov = wo_buf[par].rearrange("p (c o) -> p c o", c=MC)
        for tt in tt_half[hf]:
            pp = (ui * NTT + tt) % 2
            ps = psm[pp]
            wait("t", wo_use_tok[ui])
            wait("t", ho_own_toks[min(len(ho_own_toks) - 1, hf)])
            wait("t", op_ev[pp])
            wait("t", bo_split)
            for c in range(MC):
                mm(ps[:, 0:512], hov[:, c, 128 * tt:128 * (tt + 1)], wov[:, c, :],
                   start=(c == 0), stop=False)
            mm(ps[:, 0:512], ones_row, bo_hi[0:1, 512 * oc:512 * (oc + 1)],
               start=False, stop=False)
            mt = mm_i(ps[:, 0:512], ones_row, bo_lo[0:1, 512 * oc:512 * (oc + 1)],
                      start=False, stop=True)
            wait("v", mt)
            wait("v", acc_done[tt])
            op_ev[pp] = run("v", vE, "tensor_tensor",
                            accum[:, D * tt + 512 * oc:D * tt + 512 * (oc + 1)],
                            accum[:, D * tt + 512 * oc:D * tt + 512 * (oc + 1)],
                            ps[:, 0:512], ALU.add)
        wo_cons[par] = (tE, tE.n)
        if ui + 2 < len(use_seq):
            nhf, noc = use_seq[ui + 2]
            wait("g", wo_cons[par])
            wo_use_tok[ui + 2] = dma(
                "g", wo_buf[par].rearrange("p (c o) -> p c o", c=MC),
                blob_slice(OFF_WO, D, D)[:, :, 512 * noc:512 * (noc + 1)])
    oproj_t_end = (tE, tE.n)
    acc_guard = run("v", vE, "drain")

    wait("g", oproj_t_end)
    wait("g", acc_guard)
    xn_own_tok = dma("g", xn_own.rearrange("p (c t) -> p c t", c=MC),
                     xn_own_dram.ap().rearrange("(c p) t -> p c t", p=128))
    wd_tok = {0: dma("g", wd_buf.rearrange("p (c o) -> p c o", c=ICS),
                     blob_slice(OFF_WD, SEG, D))}
    if NSEG > 1:
        gu_tok[1] = dma("g", gu_buf[0].rearrange("p (c f) -> p c f", c=MC),
                        blob_slice(OFF_GU + D * 2 * SEG, D, 2 * SEG))
        gu_home[1] = 0

    xnv = xn_own.rearrange("p (c t) -> p c t", c=MC)
    g_free = 0
    u_free = acc_guard
    gs_free = 0
    d_ev = {0: acc_guard, 1: acc_guard, 2: acc_guard, 3: acc_guard}
    down_wm = {-1: oproj_t_end}
    acc_fin = {}
    gu_cons = {}
    hmul = 0
    for s_ in range(NSEG):
        home = gu_home[s_]
        guv = gu_buf[home].rearrange("p (c f) -> p c f", c=MC)
        wdv = wd_buf.rearrange("p (c o) -> p c o", c=ICS)
        wait("t", gu_tok[s_])
        wait("t", xn_own_tok)
        for ts in range(NTS):
            for ic in range(ICS):
                wait("t", g_free)
                wait("t", d_ev[2])
                wait("t", d_ev[3])
                mtg = 0
                for m in range(MC):
                    if m < MC - 1:
                        mm(psm[2][:, 0:TS], guv[:, m, 128 * ic:128 * (ic + 1)],
                           xnv[:, m, TS * ts:TS * (ts + 1)], start=(m == 0), stop=False)
                    else:
                        mtg = mm_i(psm[2][:, 0:TS], guv[:, m, 128 * ic:128 * (ic + 1)],
                                   xnv[:, m, TS * ts:TS * (ts + 1)], start=(m == 0),
                                   stop=True)
                wait("t", u_free)
                wait("t", d_ev[2])
                wait("t", d_ev[3])
                mtu = 0
                for m in range(MC):
                    if m < MC - 1:
                        mm(psm[3][:, 0:TS], guv[:, m, SEG + 128 * ic:SEG + 128 * (ic + 1)],
                           xnv[:, m, TS * ts:TS * (ts + 1)], start=(m == 0), stop=False)
                    else:
                        mtu = mm_i(psm[3][:, 0:TS],
                                   guv[:, m, SEG + 128 * ic:SEG + 128 * (ic + 1)],
                                   xnv[:, m, TS * ts:TS * (ts + 1)], start=(m == 0),
                                   stop=True)
                wait("s", mtg)
                wait("s", gs_free)
                g_free = run("s", scE, "activation", gs_sb, psm[2][:, 0:TS], AF.Silu)
                wait("v", g_free)
                wait("v", mtu)
                wait("v", down_wm[s_ - 1])
                hmul = run("v", vE, "tensor_tensor",
                           h_sb[:, TOK * ic + TS * ts:TOK * ic + TS * (ts + 1)],
                           gs_sb, psm[3][:, 0:TS], ALU.mult)
                u_free = hmul
                gs_free = hmul
        gu_cons[s_] = (tE, tE.n)
        hv = h_sb.rearrange("p (c t) -> p c t", c=ICS)
        wait("t", wd_tok[s_])
        wait("t", hmul)
        for tt in range(NTT):
            for oc2 in range(NOC):
                pp = (tt * NOC + oc2) % 4
                ps = psm[pp]
                wait("t", d_ev[pp])
                mt = 0
                for ic in range(ICS):
                    if ic < ICS - 1:
                        mm(ps[:, 0:512], hv[:, ic, 128 * tt:128 * (tt + 1)],
                           wdv[:, ic, 512 * oc2:512 * (oc2 + 1)],
                           start=(ic == 0), stop=False)
                    else:
                        mt = mm_i(ps[:, 0:512], hv[:, ic, 128 * tt:128 * (tt + 1)],
                                  wdv[:, ic, 512 * oc2:512 * (oc2 + 1)],
                                  start=(ic == 0), stop=True)
                wait("v", mt)
                d_ev[pp] = run("v", vE, "tensor_tensor",
                               accum[:, D * tt + 512 * oc2:D * tt + 512 * (oc2 + 1)],
                               accum[:, D * tt + 512 * oc2:D * tt + 512 * (oc2 + 1)],
                               ps[:, 0:512], ALU.add)
                if s_ == NSEG - 1 and oc2 == NOC - 1:
                    acc_fin[tt] = d_ev[pp]
        down_wm[s_] = (tE, tE.n)
        run0("v", "drain")
        if s_ + 2 < NSEG:
            wait("g", gu_cons[s_])
            gu_tok[s_ + 2] = dma("g", gu_buf[home].rearrange("p (c f) -> p c f", c=MC),
                                 blob_slice(OFF_GU + (s_ + 2) * D * 2 * SEG, D, 2 * SEG))
            gu_home[s_ + 2] = home
        if s_ + 1 < NSEG:
            wait("g", down_wm[s_])
            wd_tok[s_ + 1] = dma("g", wd_buf.rearrange("p (c o) -> p c o", c=ICS),
                                 blob_slice(OFF_WD + (s_ + 1) * SEG * D, SEG, D))

    out_toks = []
    for tt in range(NTT):
        wait("y", acc_fin[tt])
        out_toks.append(dma("y", out_own.ap()[128 * tt:128 * (tt + 1), :],
                            accum[:, D * tt:D * (tt + 1)]))
    for t in out_toks:
        wait("y", t)

    with nc.Block() as block:
        @block.tensor
        def _(eng):
            for f in ops["t"]:
                f(eng)

        @block.vector
        def _(eng):
            for f in ops["v"]:
                f(eng)

        @block.scalar
        def _(eng):
            for f in ops["s"]:
                f(eng)

        @block.sync
        def _(eng):
            for f in ops["y"]:
                f(eng)

        @block.gpsimd
        def _(eng):
            for f in ops["g"]:
                f(eng)

    stack.close()
    n_inst = sum(len(v) for v in ops.values())
    return nc, n_inst


_PROG_CACHE = {}


def _get_prog(key_cfg, debug=False):
    key = (tuple(sorted(key_cfg.items())), debug)
    if key not in _PROG_CACHE:
        _PROG_CACHE[key] = build_program(make_cfg(**key_cfg), debug=debug)
    return _PROG_CACHE[key]


def stage_inputs(cfg, x, Wq, bq, Wk, bk, Wv, bv, Wo, bo, w_gate_up, w_down):
    S, D, NH, INTER = cfg["S"], cfg["D"], cfg["NH"], cfg["INTER"]
    OCW, TOK, SEG, NSEG, BW = cfg["OCW"], cfg["TOK"], cfg["SEG"], cfg["NSEG"], cfg["BW"]
    f = np.float32
    x = np.asarray(x, dtype=f).reshape(S, D)
    WqT = np.ascontiguousarray(np.asarray(Wq, dtype=f).T)
    WkT = np.ascontiguousarray(np.asarray(Wk, dtype=f).T)
    WvT = np.ascontiguousarray(np.asarray(Wv, dtype=f).T)
    WoT = np.ascontiguousarray(np.asarray(Wo, dtype=f).T)
    WguT = np.ascontiguousarray(np.asarray(w_gate_up, dtype=f).T)
    WdT = np.ascontiguousarray(np.asarray(w_down, dtype=f).T)
    bq = np.asarray(bq, dtype=f).reshape(D)
    bk = np.asarray(bk, dtype=f).reshape(D)
    bv = np.asarray(bv, dtype=f).reshape(D)
    bo = np.asarray(bo, dtype=f).reshape(1, D)

    parts = [WoT.ravel()]
    for s_ in range(NSEG):
        seg = np.concatenate([WguT[:, SEG * s_:SEG * (s_ + 1)],
                              WguT[:, INTER + SEG * s_:INTER + SEG * (s_ + 1)]], axis=1)
        parts.append(np.ascontiguousarray(seg).ravel())
    for s_ in range(NSEG):
        parts.append(np.ascontiguousarray(WdT[SEG * s_:SEG * (s_ + 1), :]).ravel())
    blob = np.concatenate(parts).reshape(NCORE, BW)
    x_c = np.ascontiguousarray(x)

    in_maps = []
    for c in range(NCORE):
        sl = slice(OCW * c, OCW * (c + 1))
        in_maps.append({
            "x_own": np.ascontiguousarray(x[TOK * c:TOK * (c + 1)]),
            "x_full": x_c,
            "wq_t": np.ascontiguousarray(WqT[:, sl]),
            "wk_t": np.ascontiguousarray(WkT[:, sl]),
            "wv_t": np.ascontiguousarray(WvT[:, sl]),
            "bq_c": np.ascontiguousarray(bq[sl].reshape(OCW, 1)),
            "bk_c": np.ascontiguousarray(bk[sl].reshape(OCW, 1)),
            "bv_c": np.ascontiguousarray(bv[sl].reshape(1, OCW)),
            "bo_c": bo,
            "wblob": np.ascontiguousarray(blob[c:c + 1]),
        })
    return in_maps


def kernel(L0_attn_hidden_states, L0_attn_Wq, L0_attn_bq, L0_attn_Wk, L0_attn_bk,
           L0_attn_Wv, L0_attn_bv, L0_attn_Wo, L0_attn_bo,
           L0_ffn_w_gate_up, L0_ffn_w_down):
    key_cfg = dict(S=8192, D=2048, NH=16, INTER=8192)
    cfg = make_cfg(**key_cfg)
    nc, _ = _get_prog(key_cfg)
    in_maps = stage_inputs(cfg, L0_attn_hidden_states, L0_attn_Wq, L0_attn_bq,
                           L0_attn_Wk, L0_attn_bk, L0_attn_Wv, L0_attn_bv,
                           L0_attn_Wo, L0_attn_bo, L0_ffn_w_gate_up, L0_ffn_w_down)
    res = run_bass_kernel_spmd(nc, in_maps, core_ids=list(range(NCORE)))
    out = np.concatenate([res.results[c]["out_own"] for c in range(NCORE)], axis=0)
    return out.reshape(1, cfg["S"], cfg["D"]).astype(np.float32)


# version 21
# speedup vs baseline: 1.0314x; 1.0314x over previous
import numpy as np
import concourse.bass as bass
import concourse.mybir as mybir
from concourse.bass_utils import run_bass_kernel_spmd
from concourse.aot_env import allow_ici_all_to_all

F32 = mybir.dt.float32
BF16 = mybir.dt.bfloat16
AF = mybir.ActivationFunctionType
ALU = mybir.AluOpType
AX = mybir.AxisListType

NCORE = 8
HD = 128
EPS = 1e-5
ALPHA = 1.0 / float(np.sqrt(HD))


def make_cfg(S=8192, D=2048, NH=16, INTER=8192):
    c = dict(S=S, D=D, NH=NH, INTER=INTER)
    c["HPC"] = NH // NCORE
    c["OCW"] = c["HPC"] * HD
    c["TOK"] = S // NCORE
    c["NTT"] = c["TOK"] // 128
    c["TB"] = min(512 if D <= 1024 else 256, c["TOK"])
    c["NB"] = S // c["TB"]
    c["NST"] = S // 512
    c["MC"] = D // 128
    c["NOC"] = D // 512
    c["SEG"] = 512
    c["NSEG"] = INTER // c["SEG"]
    c["TS"] = min(512, c["TOK"])
    c["NTS"] = c["TOK"] // c["TS"]
    c["OFF_WO"] = 0
    c["OFF_GU"] = D * D
    c["OFF_WD"] = D * D + 2 * INTER * D
    c["BLOB"] = D * D + 2 * INTER * D + INTER * D
    assert c["BLOB"] % NCORE == 0
    c["BW"] = c["BLOB"] // NCORE
    assert D == NH * HD
    return c


class _Sem:
    __slots__ = ("h", "n")

    def __init__(self, h):
        self.h = h
        self.n = 0


def build_program(cfg, debug=False):
    S, D, NH, INTER = cfg["S"], cfg["D"], cfg["NH"], cfg["INTER"]
    HPC, OCW, TOK, NTT = cfg["HPC"], cfg["OCW"], cfg["TOK"], cfg["NTT"]
    TB, NB, NST, MC, NOC = cfg["TB"], cfg["NB"], cfg["NST"], cfg["MC"], cfg["NOC"]
    SEG, NSEG, TS, NTS = cfg["SEG"], cfg["NSEG"], cfg["TS"], cfg["NTS"]
    BW = cfg["BW"]
    NVT = S // 128
    VTPB = TB // 128
    ICS = SEG // 128

    nc = bass.Bass("TRN2", target_bir_lowering=False, debug=False)

    x_own = nc.dram_tensor("x_own", [TOK, D], F32, kind="ExternalInput")
    x_full = nc.dram_tensor("x_full", [S, D], F32, kind="ExternalInput")
    wq_t = nc.dram_tensor("wq_t", [D, OCW], F32, kind="ExternalInput")
    wk_t = nc.dram_tensor("wk_t", [D, OCW], F32, kind="ExternalInput")
    wv_t = nc.dram_tensor("wv_t", [D, OCW], F32, kind="ExternalInput")
    bq_c = nc.dram_tensor("bq_c", [OCW, 1], F32, kind="ExternalInput")
    bk_c = nc.dram_tensor("bk_c", [OCW, 1], F32, kind="ExternalInput")
    bv_c = nc.dram_tensor("bv_c", [1, OCW], F32, kind="ExternalInput")
    bo_c = nc.dram_tensor("bo_c", [1, D], F32, kind="ExternalInput")
    wblob = nc.dram_tensor("wblob", [1, BW], F32, kind="ExternalInput")
    out_own = nc.dram_tensor("out_own", [TOK, D], F32, kind="ExternalOutput")

    xn_own_dram = nc.dram_tensor("xn_own_dram", [D, TOK], BF16)
    ag_w_in = nc.dram_tensor("ag_w_in", [1, BW], BF16)
    ag_w_out = nc.dram_tensor("ag_w_out", [NCORE, BW], BF16, addr_space="Shared")
    PH = 2 if TOK >= 1024 else 1
    a2a_in_p = [nc.dram_tensor("a2a_in%d" % p, [NCORE * OCW, TOK // PH], BF16)
                for p in range(PH)]
    a2a_out_p = [nc.dram_tensor("a2a_out%d" % p, [NCORE * OCW, TOK // PH], BF16)
                 for p in range(PH)]

    dbg = {}
    if debug:
        dbg["xn"] = nc.dram_tensor("dbg_xn", [D, TOK], BF16, kind="ExternalOutput")
        dbg["q"] = nc.dram_tensor("dbg_q", [OCW, S], BF16, kind="ExternalOutput")
        dbg["k"] = nc.dram_tensor("dbg_k", [OCW, S], BF16, kind="ExternalOutput")
        dbg["v"] = nc.dram_tensor("dbg_v", [S, OCW], BF16, kind="ExternalOutput")
        dbg["bc"] = nc.dram_tensor("dbg_bc", [128, HPC], F32, kind="ExternalOutput")

    from contextlib import ExitStack
    stack = ExitStack()
    en = stack.enter_context

    K_ = 1024
    AR_BYTES = 192 * K_
    arena = en(nc.sbuf_tensor("arena", [128, AR_BYTES // 4], F32))
    arf = arena[:, :]
    arb = arf.bitcast(BF16)

    def bfv(off, cols, parts=128):
        assert off % 32 == 0 and off + 2 * cols <= AR_BYTES, (off, cols)
        return arb[0:parts, off // 2: off // 2 + cols]

    def f32v(off, cols, parts=128):
        assert off % 32 == 0 and off + 4 * cols <= AR_BYTES, (off, cols)
        return arf[0:parts, off // 4: off // 4 + cols]

    o = 0
    qw_sb = bfv(o, MC * OCW); o += 2 * MC * OCW
    kw_sb = bfv(o, MC * OCW); o += 2 * MC * OCW
    vw_sb = bfv(o, MC * OCW); o += 2 * MC * OCW
    _qkv_end = o
    Q_sb = bfv(o, HPC * S); o += 2 * HPC * S
    K_sb = bfv(o, HPC * S); o += 2 * HPC * S
    V_sb = bfv(o, NVT * OCW); o += 2 * NVT * OCW
    _xns = o
    xn_blk = [bfv(_xns + i * 2 * MC * TB, MC * TB) for i in range(2)]
    o += 2 * 2 * MC * TB
    _attn_hi = o
    TAIL0 = AR_BYTES - 24 * K_
    assert _attn_hi <= TAIL0
    p_buf = [bfv(TAIL0 + i * K_, 512) for i in range(4)]
    masks = [bfv(TAIL0 + 4 * K_ + i * K_, 512) for i in range(4)]
    ho_buf = [bfv(TAIL0 + 8 * K_ + i * K_, 512) for i in range(2)]
    r_sb2 = [f32v(TAIL0 + 10 * K_ + i * 2 * K_, 512) for i in range(2)]

    NXT = 2
    NXB = TB // 128
    xt = [f32v(_attn_hi + i * 4 * D, D) for i in range(NXT)]
    xnb = [bfv(_attn_hi + NXT * 4 * D + i * 2 * D, D) for i in range(NXB)]
    assert _attn_hi + NXT * 4 * D + NXB * 2 * D <= TAIL0

    accum = f32v(0, NTT * D)
    _A = 4 * NTT * D
    ho_own = bfv(_A, MC * TOK)
    xn_own = bfv(_A, MC * TOK)
    _B = _A + 2 * MC * TOK
    gu_buf = [bfv(_B, MC * 2 * SEG),
              bfv(AR_BYTES - 4 * MC * SEG, MC * 2 * SEG)]
    _C = _B + 2 * MC * K_
    wo_buf = [bfv(_xns + i * MC * K_, MC * 512) for i in range(2)]
    _wo_end = _xns + 2 * MC * K_
    _x_off = _B if _B + 8 * D <= _xns else _wo_end
    x_t4 = [f32v(_x_off + i * 4 * D, D) for i in range(2)]
    bo_f = f32v(0, D, parts=1)
    wd_buf = bfv(_C, ICS * D)
    _E = _C + 8 * D
    h_sb = bfv(_E, ICS * TOK)
    _F = max(_E + 2 * ICS * TOK, _wo_end,
             _x_off + 8 * D if _x_off >= _wo_end else 0)
    bo_hi = bfv(_F, D, parts=1)
    bo_lo = bfv(_F + 2 * D, D, parts=1)
    assert _F + 4 * D <= AR_BYTES - 4 * MC * SEG, ("P4 map overflow", _F, AR_BYTES)

    gs_sb = en(nc.sbuf_tensor("gs_sb", [128, TS], BF16))[:, :]
    consts = en(nc.sbuf_tensor("consts", [128, 1280], BF16))
    ident = consts[:, 0:128]
    ones_col = consts[:, 128:129]
    ones_row = consts[0:1, 160:288]
    ones_mat = consts[:, 1024:1152]
    bv_hi = consts[0:1, 288:288 + OCW]
    bv_lo = consts[0:1, 544:544 + OCW]
    fsm = en(nc.sbuf_tensor("fsm", [128, 3072], F32))
    ones_f32 = fsm[0:1, 0:128]
    stats = fsm[0:1, 1536:1664]
    sc1 = fsm[0:1, 1728:1736]
    bq_sb = fsm[:, 1744:1744 + HPC]
    bk_sb = fsm[:, 1752:1752 + HPC]
    biasC = fsm[:, 1760:1760 + HPC]
    mst = fsm[:, 1776:1792]
    bv_f = fsm[0:1, 1792:1792 + OCW]
    bv_t1 = fsm[0:1, 384:384 + OCW]
    bnst = [fsm[:, 640 + 24 * s_:640 + 24 * s_ + 24] for s_ in range(4)]
    mvs = fsm[:, 736:752]
    veps = [fsm[:, 752 + 2 * q_:754 + 2 * q_] for q_ in range(4)]
    lnR2 = [fsm[:, 760 + 2 * q_:762 + 2 * q_] for q_ in range(4)]
    lnM2 = [fsm[:, 768 + 2 * q_:770 + 2 * q_] for q_ in range(4)]
    epsc = fsm[:, 776:777]
    identF = fsm[:, 2304:2432]
    bo_t1 = fsm[0:1, 2432:2432 + 512]

    psA = en(nc.psum_tensor("psA", [128, 1024], F32))
    psB = en(nc.psum_tensor("psB", [128, 1024], F32))
    psm = [en(nc.psum_tensor("psm%d" % i, [128, 512], F32)) for i in range(4)]

    NCH = 6
    dch = {"y": [_Sem(en(nc.semaphore("sY%d" % i))) for i in range(NCH)],
           "g": [_Sem(en(nc.semaphore("sG%d" % i))) for i in range(NCH)]}
    _rr = {"y": 0, "g": 0}
    tE = _Sem(en(nc.semaphore("tE")))
    vE = _Sem(en(nc.semaphore("vE")))
    scE = _Sem(en(nc.semaphore("scE")))
    gE = _Sem(en(nc.semaphore("gE")))
    ccs = [_Sem(en(nc.semaphore("cc%d" % i))) for i in range(3)]

    ops = {k: [] for k in "tvsyg"}
    _wmax = {}

    def em(e, f):
        ops[e].append(f)

    def wait(e, tok):
        if not tok:
            return
        sem, val = tok
        if val <= 0:
            return
        key = (e, id(sem))
        if _wmax.get(key, 0) >= val:
            return
        _wmax[key] = val
        em(e, lambda eng, _s=sem.h, _t=val: eng.wait_ge(_s, _t))

    def tmax(*toks):
        best = 0
        for t in toks:
            if t and (not best or t[1] > best[1]):
                if best:
                    assert t[0] is best[0]
                best = t
        return best

    def dma(e, out, in_, **kw):
        ch = _rr[e] % NCH
        _rr[e] += 1
        sem = dch[e][ch]
        wait(e, (sem, sem.n))
        sem.n += 16
        em(e, lambda eng, _o=out, _i=in_, _s=sem.h, _k=dict(kw):
           eng.dma_start(out=_o, in_=_i, **_k).then_inc(_s, 16))
        return (sem, sem.n)

    def run(e, sem, meth, *a, **kw):
        sem.n += 1
        em(e, lambda eng, _m=meth, _a=a, _k=dict(kw), _s=sem.h:
           getattr(eng, _m)(*_a, **_k).then_inc(_s, 1))
        return (sem, sem.n)

    def run0(e, meth, *a, **kw):
        em(e, lambda eng, _m=meth, _a=a, _k=dict(kw): getattr(eng, _m)(*_a, **_k))

    def mm(out, lhsT, rhs, start, stop):
        run0("t", "matmul", out, lhsT, rhs, start=start, stop=stop,
             skip_group_check=True)

    def mm_i(out, lhsT, rhs, start, stop):
        return run("t", tE, "matmul", out, lhsT, rhs, start=start, stop=stop,
                   skip_group_check=True)


    def fence(wait_engine, q):
        for sem in dch[q]:
            wait(wait_engine, (sem, sem.n))

    qkv_toks = []
    for (w_in, w_sb) in ((wq_t, qw_sb), (wk_t, kw_sb), (wv_t, vw_sb)):
        qkv_toks.append(dma("g", w_sb.rearrange("p (c o) -> p c o", c=MC),
                            w_in.ap().rearrange("(c p) o -> p c o", p=128)))
    b_toks = [dma("y", bq_sb[:, :], bq_c.ap().rearrange("(c p) one -> p (c one)", p=128),
                  allow_slow_non_contiguous=True),
              dma("y", bk_sb[:, :], bk_c.ap().rearrange("(c p) one -> p (c one)", p=128),
                  allow_slow_non_contiguous=True),
              dma("y", bv_f[:, :], bv_c[:, :])]

    run("g", gE, "memset", ident, 1.0)
    run("g", gE, "memset", identF, 1.0)
    run("g", gE, "memset", consts[:, 128:160], 1.0)
    run("g", gE, "memset", consts[0:1, 160:288], 1.0)
    run("g", gE, "memset", ones_mat, 1.0)
    run("g", gE, "memset", fsm[0:1, 0:128], 1.0)
    run("g", gE, "memset", epsc, float(EPS))
    for j in range(4):
        run("g", gE, "memset", masks[j], 1.0)
    wait("g", (gE, gE.n))
    run("g", gE, "affine_select", out=ident, in_=ident, compare_op=ALU.is_equal,
        fill=0.0, base=0, pattern=[[-1, 128]], channel_multiplier=1)
    run("g", gE, "affine_select", out=identF, in_=identF, compare_op=ALU.is_equal,
        fill=0.0, base=0, pattern=[[-1, 128]], channel_multiplier=1)
    for j in range(4):
        run("g", gE, "affine_select", out=masks[j], in_=masks[j],
            compare_op=ALU.is_ge, fill=0.0, base=-128 * j,
            pattern=[[1, 512]], channel_multiplier=-1)
    g_const = (gE, gE.n)

    for t in b_toks:
        wait("v", t)
    run0("v", "tensor_copy", bv_hi, bv_f)
    run0("v", "drain")
    run0("v", "tensor_copy", bv_t1, bv_hi)
    run0("v", "drain")
    run0("v", "tensor_sub", bv_t1, bv_f, bv_t1)
    run0("v", "drain")
    run0("v", "tensor_copy", bv_lo, bv_t1)
    bv_split = run("v", vE, "drain")

    NTB = TB // 128
    NG = D // 512
    NGR = (MC + 7) // 8
    assert NG <= 4 and MC % 8 == 0
    psAb = psA[:, :].bitcast(BF16)
    psBb = psB[:, :].bitcast(BF16)
    grans = [psAb[:, 1024:2048], psBb[:, 1024:2048]]
    gran_ev = [0, 0]
    pr_nrm = {}
    pr_agg = {}
    pr_pair = {}
    pr_phB_rd = {0: 0, 1: 0, 2: 0, 3: 0}
    ship_war = {0: [], 1: []}
    xnb_free = {0: 0, 1: 0}
    xnb_war = {}
    blk_ready = {}
    _gt = [0]
    _pq = [0]
    _gu = [0]
    _pend = []
    _xd_tok = {}

    _tinfo = {}
    _sd_rd = {0: 0, 1: 0, 2: 0, 3: 0}

    def _flush_pair():
        if not _pend:
            return
        pend = list(_pend)
        _pend.clear()
        w = len(pend)
        q = _pq[0] % 4
        _pq[0] += 1
        s0 = pend[0] % 4
        mv2 = mvs[:, 2 * s0:2 * s0 + 2 * w].rearrange("p (s two) -> p two s", two=2)
        wait("s", pr_agg[pend[-1]])
        wait("s", _sd_rd[q])
        wait("s", g_const)
        st = run("s", scE, "activation", veps[q][:, 0:w], mv2[:, 1, :],
                 AF.Sqrt, bias=epsc)
        wait("v", st)
        wait("v", pr_phB_rd[q])
        rt = run("v", vE, "reciprocal", lnR2[q][:, 0:w], veps[q][:, 0:w])
        _sd_rd[q] = rt
        wait("v", rt)
        ptok = run("v", vE, "scalar_tensor_tensor", lnM2[q][:, 0:w],
                   mv2[:, 0, :], -1.0, lnR2[q][:, 0:w], ALU.mult, ALU.mult)
        pr_pair[pend[0] // 2] = ptok
        nrm_last = 0
        for j, t in enumerate(pend):
            slot = t % NXT
            xb = t % NXB
            wait("s", ptok)
            wait("s", _xd_tok.pop(t))
            wait("s", xnb_war.get(xb, 0))
            nrm_last = run("s", scE, "activation", xnb[xb], xt[slot], AF.Identity,
                           scale=lnR2[q][:, j:j + 1], bias=lnM2[q][:, j:j + 1])
            pr_nrm[t] = nrm_last
        pr_phB_rd[q] = nrm_last

    def stage1_tile(dq, src_rows, par, toff, bkey):
        t = _gt[0]
        _gt[0] += 1
        slot, s4 = t % NXT, t % 4
        wait(dq, pr_nrm.get(t - NXT, 0))
        wait(dq, pr_agg.get(t - NXT, 0))
        xd = dma(dq, xt[slot], src_rows)
        _xd_tok[t] = xd
        _tinfo[t] = (par, toff, bkey)
        wait("v", xd)
        wait("v", pr_agg.get(t - 4, 0))
        wait("v", pr_pair.get((t - 4) // 2, 0))
        for g in range(NG):
            run("v", vE, "bn_stats", bnst[s4][:, 6 * g:6 * g + 6],
                xt[slot][:, 512 * g:512 * (g + 1)])
        wait("v", (vE, vE.n))
        pr_agg[t] = run("v", vE, "bn_aggr", mvs[:, 2 * s4:2 * s4 + 2],
                        bnst[s4][:, 0:6 * NG])
        _pend.append(t)
        if len(_pend) == 2:
            _flush_pair()

    def stage2_tile(t):
        if t not in pr_nrm:
            _flush_pair()
        par, toff, bkey = _tinfo.pop(t)
        xb = t % NXB
        wait("t", g_const)
        wait("t", pr_nrm[t])
        for gr in range(NGR):
            cw = min(8, MC - 8 * gr)
            g2 = _gu[0] % 2
            _gu[0] += 1
            wait("t", gran_ev[g2])
            tt2 = 0
            for c8 in range(cw):
                c = 8 * gr + c8
                tt2 = run("t", tE, "transpose",
                          grans[g2][:, 128 * c8:128 * (c8 + 1)],
                          xnb[xb][:, 128 * c:128 * (c + 1)], ident)
            xnb_war[xb] = tt2
            wait("v", tt2)
            wait("v", xnb_free[par])
            for stok in ship_war[par]:
                wait("v", stok)
            dstv = xn_blk[par].rearrange("p (c t) -> p c t", c=MC)[
                :, 8 * gr:8 * gr + cw, toff:toff + 128]
            srcv = grans[g2].rearrange("p (c t) -> p c t", c=8)[:, 0:cw, :]
            gran_ev[g2] = run("v", vE, "tensor_copy", dstv, srcv)
        blk_ready[bkey] = gran_ev[(_gu[0] - 1) % 2]

    st2_q = []

    def stage1_block(gb, src_t, row0, bkey):
        par = gb % 2
        for j in range(NTB):
            st2_q.append(_gt[0])
            stage1_tile(("y", "g")[_gt[0] % 2],
                        src_t.ap()[row0 + 128 * j:row0 + 128 * (j + 1), :],
                        par, 128 * j, bkey)
        if NTB % 2 == 1:
            _flush_pair()
            if _gt[0] % 2:
                _gt[0] += 1
        return par

    def stage2_n(n):
        for _ in range(n):
            if st2_q:
                stage2_tile(st2_q.pop(0))

    NOB = TOK // TB
    for ob in range(NOB):
        par = stage1_block(ob, x_own, ob * TB, ("own", ob))
        _flush_pair()
        if _gt[0] % 2:
            _gt[0] += 1
        stage2_n(NTB)
        wait("y", blk_ready[("own", ob)])
        shv = xn_blk[par].rearrange("p (c t) -> p c t", c=MC)
        toks = [dma("y", xn_own_dram.ap()[:, ob * TB:(ob + 1) * TB]
                    .rearrange("(c p) t -> p c t", p=128), shv)]
        if debug:
            toks.append(dma("y", dbg["xn"].ap()[:, ob * TB:(ob + 1) * TB]
                            .rearrange("(c p) t -> p c t", p=128), shv))
        ship_war[par] = toks
    qk_ev = {0: 0, 1: 0}
    v_ev = {0: 0, 1: 0}
    last_q_ev = 0
    last_v_ev = 0
    sq_free = {0: 0, 1: 0, 2: 0, 3: 0}
    nrm_free = {0: 0, 1: 0}
    nsq = [0]
    npp = [0]
    pending_norm = []

    def flush_norms():
        while pending_norm:
            sqv, idx, sqt = pending_norm.pop(0)
            pp2 = npp[0] % 2
            npp[0] += 1
            wait("t", sqt)
            wait("t", nrm_free[pp2])
            psN = (psA, psB)[pp2][0:1, 0:512]
            mt2 = mm_i(psN[:, 0:TB] if TB < 512 else psN, ones_col, sqv,
                       start=True, stop=True)
            sq_free[nsq[0] % 4] = mt2
            nsq[0] += 1
            wait("v", mt2)
            nrm_free[pp2] = run("v", vE, "reduce_max", stats[0:1, idx:idx + 1],
                                (psA, psB)[pp2][0:1, 0:TB], axis=AX.X)
    def emit_qkv(b):
        nonlocal last_q_ev, last_v_ev
        bi = b
        par = (NOB + b) % 2
        st2c = [NTB]

        def st2():
            if st2c[0] > 0 and st2_q:
                st2c[0] -= 1
                stage2_tile(st2_q.pop(0))
        flush_norms()
        wait("t", blk_ready[b])
        if bi < 2:
            for t in qkv_toks:
                wait("t", t)
            wait("t", g_const)
            for t in b_toks:
                wait("t", t)
        xv = xn_blk[par].rearrange("p (c t) -> p c t", c=MC)
        for which, (wsb, bcol, dst) in enumerate(((qw_sb, bq_sb, Q_sb),
                                                  (kw_sb, bk_sb, K_sb))):
            wv = wsb.rearrange("p (c o) -> p c o", c=MC)
            for h in range(HPC):
                pp = (bi * 2 * HPC + which * HPC + h) % 2
                ps = psm[pp]
                wait("t", qk_ev[pp])
                mt = 0
                for m in range(MC):
                    if m < MC - 1:
                        mm(ps[:, 0:TB], wv[:, m, 128 * h:128 * (h + 1)], xv[:, m, :],
                           start=(m == 0), stop=False)
                    else:
                        mt = mm_i(ps[:, 0:TB], wv[:, m, 128 * h:128 * (h + 1)],
                                  xv[:, m, :], start=(m == 0), stop=True)
                wait("s", mt)
                qk_ev[pp] = run("s", scE, "activation",
                                dst[:, S * h + TB * b:S * h + TB * (b + 1)],
                                ps[:, 0:TB], AF.Identity, bias=bcol[:, h:h + 1])
                last_q_ev = qk_ev[pp]
                sqp = (nsq[0] + len(pending_norm)) % 4
                wait("s", qk_ev[pp])
                wait("s", sq_free[sqp])
                sqt = run("s", scE, "activation", p_buf[sqp][:, 0:TB],
                          dst[:, S * h + TB * b:S * h + TB * (b + 1)], AF.Square)
                pending_norm.append((p_buf[sqp][:, 0:TB],
                                     (2 * h + which) * NB + b, sqt))
                st2()
        wv = vw_sb.rearrange("p (c o) -> p c o", c=MC)
        npair = (VTPB + 1) // 2
        for pair in range(npair):
            pp = 2 + (bi * npair + pair) % 2
            ps = psm[pp]
            wait("t", v_ev[pp - 2])
            wait("t", bv_split)
            nj = min(2, VTPB - 2 * pair)
            mt = 0
            for j in range(nj):
                t4 = 2 * pair + j
                for m in range(MC):
                    mm(ps[:, OCW * j:OCW * (j + 1)], xv[:, m, 128 * t4:128 * (t4 + 1)],
                       wv[:, m, :], start=(m == 0), stop=False)
                mm(ps[:, OCW * j:OCW * (j + 1)], ones_row, bv_hi, start=False, stop=False)
                mt = mm_i(ps[:, OCW * j:OCW * (j + 1)], ones_row, bv_lo,
                          start=False, stop=True)
            wait("v", mt)
            base = OCW * (VTPB * b + 2 * pair)
            v_ev[pp - 2] = run("v", vE, "tensor_copy",
                               V_sb[:, base:base + nj * OCW], ps[:, 0:nj * OCW])
            last_v_ev = v_ev[pp - 2]
            st2()
        while st2c[0] > 0 and st2_q:
            st2()
        xnb_free[par] = (tE, tE.n)

    stage1_block(NOB, x_full, 0, 0)
    stage2_n(NTB)
    for b in range(NB):
        if b + 1 < NB:
            stage1_block(NOB + b + 1, x_full, (b + 1) * TB, b + 1)
        emit_qkv(b)
    assert not st2_q and not _pend, (st2_q, _pend)

    t_blob = dma("g", ag_w_in[:, :], wblob[:, :])
    wait("g", t_blob)
    _wag_hole = [None]
    em("g", lambda g: g.wait_ge(*_wag_hole[0]))
    ccs[1].n = 1
    cc_w = (ccs[1], 1)
    em("g", lambda g: g.collective_compute(
        "AllGather", ALU.bypass, replica_groups=[list(range(NCORE))],
        ins=[ag_w_in.ap().opt()], outs=[ag_w_out.ap().opt()]).then_inc(ccs[1].h, 1))

    p2_t_end = (tE, tE.n)
    blobv = ag_w_out.ap().rearrange("r w -> (r w)")

    def blob_slice(off, rows, cols):
        return blobv[off:off + rows * cols].rearrange("(c p o) -> p c o", p=128, o=cols)

    OFF_WO, OFF_GU, OFF_WD = cfg["OFF_WO"], cfg["OFF_GU"], cfg["OFF_WD"]
    wait("g", cc_w)
    wait("g", p2_t_end)
    wo_tok = {0: dma("g", wo_buf[0].rearrange("p (c o) -> p c o", c=MC),
                     blob_slice(OFF_WO, D, D)[:, :, 0:512])}
    if NOC > 1:
        wo_tok[1] = dma("g", wo_buf[1].rearrange("p (c o) -> p c o", c=MC),
                        blob_slice(OFF_WO, D, D)[:, :, 512:1024])

    if debug:
        wait("y", last_q_ev)
        wait("y", last_v_ev)
        dma("y", dbg["q"].ap().rearrange("(h p) t -> p h t", p=128),
            Q_sb.rearrange("p (h t) -> p h t", h=HPC))
        dma("y", dbg["k"].ap().rearrange("(h p) t -> p h t", p=128),
            K_sb.rearrange("p (h t) -> p h t", h=HPC))
        dma("y", dbg["v"].ap().rearrange("(n p) o -> p n o", p=128),
            V_sb.rearrange("p (n o) -> p n o", n=NVT))

    flush_norms()
    wait("v", tmax(nrm_free[0], nrm_free[1]))
    run0("v", "drain")
    biasC_tok = 0
    for h in range(HPC):
        run0("v", "reduce_max", sc1[0:1, 0:1],
             stats[0:1, 2 * h * NB:2 * h * NB + NB], axis=AX.X)
        run0("v", "drain")
        run0("v", "reduce_max", sc1[0:1, 1:2],
             stats[0:1, (2 * h + 1) * NB:(2 * h + 1) * NB + NB], axis=AX.X)
        run0("v", "drain")
        vt = run("v", vE, "tensor_tensor", sc1[0:1, 2:3], sc1[0:1, 0:1],
                 sc1[0:1, 1:2], ALU.mult)
        wait("s", vt)
        st = run("s", scE, "activation", sc1[0:1, 3:4], sc1[0:1, 2:3], AF.Sqrt)
        wait("v", st)
        vt = run("v", vE, "tensor_scalar_mul", sc1[0:1, 4:5], sc1[0:1, 3:4], -ALPHA)
        wait("t", vt)
        wait("t", last_v_ev)
        mt = mm_i(psm[2][:, 0:1], ones_f32, sc1[0:1, 4:5], start=True, stop=True)
        wait("v", mt)
        biasC_tok = run("v", vE, "tensor_copy", biasC[:, h:h + 1], psm[2][:, 0:1])
        run0("v", "drain")
    p2norm_t = (tE, tE.n)
    if debug:
        wait("y", biasC_tok)
        dma("y", dbg["bc"].ap(), biasC)

    wait("y", p2_t_end)
    bo_tok0 = dma("y", bo_f[:, :], bo_c[:, :])
    wait("v", bo_tok0)
    wait("v", pr_nrm[max(pr_nrm)])
    wait("v", (vE, vE.n))
    run0("v", "tensor_copy", bo_hi, bo_f)
    run0("v", "drain")
    for q in range(D // 512):
        run0("v", "tensor_copy", bo_t1, bo_hi[0:1, 512 * q:512 * (q + 1)])
        run0("v", "drain")
        run0("v", "tensor_sub", bo_t1, bo_f[0:1, 512 * q:512 * (q + 1)], bo_t1)
        run0("v", "drain")
        run0("v", "tensor_copy", bo_lo[0:1, 512 * q:512 * (q + 1)], bo_t1)
        run0("v", "drain")
    bo_split = run("v", vE, "drain")

    scq = [psA[:, 0:512], psA[:, 512:1024], psB[:, 0:512], psB[:, 512:1024]]
    exp_tok = {q: 0 for q in range(4)}
    pbuf_cons = {q: p2norm_t for q in range(4)}
    sc_cons = {q: nrm_free[q // 2] for q in range(4)}
    sc_cons[1] = tmax(sc_cons[1], gran_ev[0])
    sc_cons[3] = tmax(sc_cons[3], gran_ev[1])
    av_free = {0: last_v_ev, 1: last_v_ev}
    sum_free = {0: biasC_tok, 1: biasC_tok}
    ho_free = {0: [], 1: []}
    wait("s", biasC_tok)
    wait("t", last_q_ev)
    wait("t", last_v_ev)
    a2a_ship = {p: [] for p in range(PH)}
    TOKC = min(TOK // PH, 512) if PH == 2 else min(TOK, 512)
    LA = 3
    TLA = 4

    if PH == 2:
        strip_order = [i for i in range(NST - 2, -1, -2)] + \
                      [i for i in range(NST - 1, 0, -2)]
    else:
        strip_order = list(range(NST))
    jobs = []
    for i in strip_order:
        nt = 4 * (i + 1)
        for h in range(HPC):
            for kt in range(nt):
                jobs.append((i, kt, h))
    njobs = len(jobs)
    avt_h = {}
    tails = {}

    def emit_sc(g):
        i, kt, h = jobs[g]
        q4 = g % 4
        qs = Q_sb[:, S * h + 512 * i:S * h + 512 * (i + 1)]
        wait("t", exp_tok[q4])
        wait("t", sc_cons[q4])
        mt = mm_i(scq[q4], K_sb[:, S * h + 128 * kt:S * h + 128 * (kt + 1)],
                  qs, start=True, stop=True)
        if g % 2 == 0:
            return
        qe, qo = (g - 1) % 4, q4
        pj = bfv(TAIL0 + ((g - 1) % 4) * K_, 1024)
        psrc = (psA, psB)[(g % 4) // 2][:, 0:1024]
        wait("s", mt)
        wait("s", pbuf_cons[qe])
        wait("s", pbuf_cons[qo])
        et = run("s", scE, "activation", pj, psrc, AF.Exp,
                 bias=biasC[:, h:h + 1], scale=ALPHA)
        exp_tok[qe] = exp_tok[qo] = et
        sc_cons[qe] = sc_cons[qo] = 0
        k0 = jobs[g - 1][1]
        if kt >= 4 * i or k0 >= 4 * i:
            wait("v", et)
            for (qq, kk) in ((qe, k0), (qo, kt)):
                if kk >= 4 * i:
                    run0("v", "tensor_tensor", p_buf[qq], p_buf[qq],
                         masks[kk - 4 * i], ALU.mult)
            gt = run("v", vE, "drain")
            exp_tok[qe] = exp_tok[qo] = gt

    def emit_av(g):
        i, kt, h = jobs[g]
        q4 = g % 4
        nt = 4 * (i + 1)
        wait("t", exp_tok[q4])
        if kt == 0:
            wait("t", av_free[h])
            wait("t", sum_free[h])
        mm(psm[h][:, 0:512],
           V_sb[:, OCW * kt + 128 * h:OCW * kt + 128 * (h + 1)],
           p_buf[q4], start=(kt == 0), stop=(kt == nt - 1))
        avt = mm_i(psm[2 + h][:, 0:512], ones_mat, p_buf[q4],
                   start=(kt == 0), stop=(kt == nt - 1))
        pbuf_cons[q4] = avt
        if kt == nt - 1:
            avt_h[(i, h)] = avt
            tails[g + TLA] = (i, h)

    def emit_tail(i, h):
        hb = (i * HPC + h) % 2
        wait("v", avt_h[(i, h)])
        for t in ho_free[hb]:
            wait("v", t)
        vtr = run("v", vE, "reciprocal", r_sb2[hb], psm[2 + h][:, 0:512])
        sum_free[h] = vtr
        run0("v", "drain")
        vt = run("v", vE, "tensor_tensor", ho_buf[hb], psm[h][:, 0:512],
                 r_sb2[hb], ALU.mult)
        av_free[h] = vt
        wait("y", vt)
        ho_free[hb] = []
        for u in range(512 // TOKC):
            g_t = 512 * i + u * TOKC
            rows0 = OCW * (g_t // TOK) + 128 * h
            hp = (g_t % TOK) // (TOK // PH)
            col0 = g_t % (TOK // PH)
            tok = dma("y", a2a_in_p[hp].ap()[rows0:rows0 + 128, col0:col0 + TOKC],
                      ho_buf[hb][:, u * TOKC:(u + 1) * TOKC])
            ho_free[hb].append(tok)
            a2a_ship[hp].append(tok)

    wag_gate = [None]
    xfree_gate = [None]
    XSAFE = 4
    _remaining = set(strip_order)
    _done_ct = {}
    for i in strip_order:
        _done_ct[i] = 4 * (i + 1) * HPC
    _per_strip_left = dict(_done_ct)
    for g in range(njobs + TLA + 1):
        if g in tails:
            i_, h_ = tails.pop(g)
            emit_tail(i_, h_)
        if g < njobs:
            emit_sc(g)
        if g - LA >= 0 and g - LA < njobs:
            emit_av(g - LA)
            i_av = jobs[g - LA][0]
            _per_strip_left[i_av] -= 1
            if _per_strip_left[i_av] == 0:
                _remaining.discard(i_av)
                if xfree_gate[0] is None and all(i2 < XSAFE for i2 in _remaining):
                    xfree_gate[0] = (tE, tE.n)
        if g == njobs // 3:
            wag_gate[0] = (tE, tE.n)
    p3_t_end = (tE, tE.n)
    if xfree_gate[0] is None:
        xfree_gate[0] = p3_t_end
    _wag_hole[0] = (wag_gate[0][0].h, wag_gate[0][1])

    cc_a2a_p = []
    for p in range(PH):
        for t in a2a_ship[p]:
            wait("g", t)
        ccs[2].n += 1
        cc_a2a_p.append((ccs[2], ccs[2].n))

        def _emit_a2a(g, _p=p):
            with allow_ici_all_to_all():
                g.collective_compute(
                    "AllToAll", ALU.bypass, replica_groups=[list(range(NCORE))],
                    ins=[a2a_in_p[_p].ap().opt()],
                    outs=[a2a_out_p[_p].ap().opt()]).then_inc(ccs[2].h, 1)
        em("g", _emit_a2a)
    cc_a2a = cc_a2a_p[-1]

    wait("g", p3_t_end)
    gu_tok = {0: dma("g", gu_buf[1].rearrange("p (c f) -> p c f", c=MC),
                     blob_slice(OFF_GU, D, 2 * SEG))}
    gu_home = {0: 1}
    wait("y", p3_t_end)
    ho_own_toks = []
    for p in range(PH):
        wait("y", cc_a2a_p[p])
        ho_own_toks.append(
            dma("y", ho_own.rearrange("p (c t) -> p c t", c=MC)[
                :, :, (TOK // PH) * p:(TOK // PH) * (p + 1)],
                a2a_out_p[p].ap().rearrange("(c p) t -> p c t", p=128)))

    x4_done = {}
    acc_done = {}
    for tt in range(NTT):
        par = tt % 2
        wait("y", x4_done.get(tt - 2, 0))
        wait("y", xfree_gate[0])
        x4_tok = dma("y", x_t4[par], x_own.ap()[128 * tt:128 * (tt + 1), :])
        wait("v", x4_tok)
        wait("v", p3_t_end)
        if tt == 0:
            wait("v", bo_split)
        acc_done[tt] = run("v", vE, "tensor_copy",
                           accum[:, D * tt:D * (tt + 1)], x_t4[par])
        x4_done[tt] = acc_done[tt]
    run0("v", "drain")
    acc_init = acc_done[NTT - 1]

    hov = ho_own.rearrange("p (c t) -> p c t", c=MC)
    op_ev = {0: acc_init, 1: acc_init}
    NHLF = min(PH, 2)
    tt_half = [[tt for tt in range(NTT) if (tt * 128) // max(1, TOK // NHLF) == hf]
               for hf in range(NHLF)]
    use_seq = [(hf, oc) for hf in range(NHLF) for oc in range(NOC)]
    wo_use_tok = {}
    wo_cons = {}
    for ui, (hf, oc) in enumerate(use_seq):
        if ui < 2:
            wo_use_tok[ui] = wo_tok[min(oc, max(wo_tok))]
    for ui, (hf, oc) in enumerate(use_seq):
        par = ui % 2
        wov = wo_buf[par].rearrange("p (c o) -> p c o", c=MC)
        for tt in tt_half[hf]:
            pp = (ui * NTT + tt) % 2
            ps = psm[pp]
            wait("t", wo_use_tok[ui])
            wait("t", ho_own_toks[min(len(ho_own_toks) - 1, hf)])
            wait("t", op_ev[pp])
            wait("t", bo_split)
            for c in range(MC):
                mm(ps[:, 0:512], hov[:, c, 128 * tt:128 * (tt + 1)], wov[:, c, :],
                   start=(c == 0), stop=False)
            mm(ps[:, 0:512], ones_row, bo_hi[0:1, 512 * oc:512 * (oc + 1)],
               start=False, stop=False)
            mt = mm_i(ps[:, 0:512], ones_row, bo_lo[0:1, 512 * oc:512 * (oc + 1)],
                      start=False, stop=True)
            wait("v", mt)
            wait("v", acc_done[tt])
            op_ev[pp] = run("v", vE, "tensor_tensor",
                            accum[:, D * tt + 512 * oc:D * tt + 512 * (oc + 1)],
                            accum[:, D * tt + 512 * oc:D * tt + 512 * (oc + 1)],
                            ps[:, 0:512], ALU.add)
        wo_cons[par] = (tE, tE.n)
        if ui + 2 < len(use_seq):
            nhf, noc = use_seq[ui + 2]
            wait("g", wo_cons[par])
            wo_use_tok[ui + 2] = dma(
                "g", wo_buf[par].rearrange("p (c o) -> p c o", c=MC),
                blob_slice(OFF_WO, D, D)[:, :, 512 * noc:512 * (noc + 1)])
    oproj_t_end = (tE, tE.n)
    acc_guard = run("v", vE, "drain")

    wait("g", oproj_t_end)
    wait("g", acc_guard)
    xn_own_tok = dma("g", xn_own.rearrange("p (c t) -> p c t", c=MC),
                     xn_own_dram.ap().rearrange("(c p) t -> p c t", p=128))
    wd_tok = {0: dma("g", wd_buf.rearrange("p (c o) -> p c o", c=ICS),
                     blob_slice(OFF_WD, SEG, D))}
    if NSEG > 1:
        gu_tok[1] = dma("g", gu_buf[0].rearrange("p (c f) -> p c f", c=MC),
                        blob_slice(OFF_GU + D * 2 * SEG, D, 2 * SEG))
        gu_home[1] = 0

    xnv = xn_own.rearrange("p (c t) -> p c t", c=MC)
    g_free = 0
    u_free = acc_guard
    gs_free = 0
    d_ev = {0: acc_guard, 1: acc_guard, 2: acc_guard, 3: acc_guard}
    down_wm = {-1: oproj_t_end}
    acc_fin = {}
    gu_cons = {}
    hmul = 0
    for s_ in range(NSEG):
        home = gu_home[s_]
        guv = gu_buf[home].rearrange("p (c f) -> p c f", c=MC)
        wdv = wd_buf.rearrange("p (c o) -> p c o", c=ICS)
        wait("t", gu_tok[s_])
        wait("t", xn_own_tok)
        for ts in range(NTS):
            for ic in range(ICS):
                wait("t", g_free)
                wait("t", d_ev[2])
                wait("t", d_ev[3])
                mtg = 0
                for m in range(MC):
                    if m < MC - 1:
                        mm(psm[2][:, 0:TS], guv[:, m, 128 * ic:128 * (ic + 1)],
                           xnv[:, m, TS * ts:TS * (ts + 1)], start=(m == 0), stop=False)
                    else:
                        mtg = mm_i(psm[2][:, 0:TS], guv[:, m, 128 * ic:128 * (ic + 1)],
                                   xnv[:, m, TS * ts:TS * (ts + 1)], start=(m == 0),
                                   stop=True)
                wait("t", u_free)
                wait("t", d_ev[2])
                wait("t", d_ev[3])
                mtu = 0
                for m in range(MC):
                    if m < MC - 1:
                        mm(psm[3][:, 0:TS], guv[:, m, SEG + 128 * ic:SEG + 128 * (ic + 1)],
                           xnv[:, m, TS * ts:TS * (ts + 1)], start=(m == 0), stop=False)
                    else:
                        mtu = mm_i(psm[3][:, 0:TS],
                                   guv[:, m, SEG + 128 * ic:SEG + 128 * (ic + 1)],
                                   xnv[:, m, TS * ts:TS * (ts + 1)], start=(m == 0),
                                   stop=True)
                wait("s", mtg)
                wait("s", gs_free)
                g_free = run("s", scE, "activation", gs_sb, psm[2][:, 0:TS], AF.Silu)
                wait("v", g_free)
                wait("v", mtu)
                wait("v", down_wm[s_ - 1])
                hmul = run("v", vE, "tensor_tensor",
                           h_sb[:, TOK * ic + TS * ts:TOK * ic + TS * (ts + 1)],
                           gs_sb, psm[3][:, 0:TS], ALU.mult)
                u_free = hmul
                gs_free = hmul
        gu_cons[s_] = (tE, tE.n)
        hv = h_sb.rearrange("p (c t) -> p c t", c=ICS)
        wait("t", wd_tok[s_])
        wait("t", hmul)
        for tt in range(NTT):
            for oc2 in range(NOC):
                pp = (tt * NOC + oc2) % 4
                ps = psm[pp]
                wait("t", d_ev[pp])
                mt = 0
                for ic in range(ICS):
                    if ic < ICS - 1:
                        mm(ps[:, 0:512], hv[:, ic, 128 * tt:128 * (tt + 1)],
                           wdv[:, ic, 512 * oc2:512 * (oc2 + 1)],
                           start=(ic == 0), stop=False)
                    else:
                        mt = mm_i(ps[:, 0:512], hv[:, ic, 128 * tt:128 * (tt + 1)],
                                  wdv[:, ic, 512 * oc2:512 * (oc2 + 1)],
                                  start=(ic == 0), stop=True)
                wait("v", mt)
                d_ev[pp] = run("v", vE, "tensor_tensor",
                               accum[:, D * tt + 512 * oc2:D * tt + 512 * (oc2 + 1)],
                               accum[:, D * tt + 512 * oc2:D * tt + 512 * (oc2 + 1)],
                               ps[:, 0:512], ALU.add)
                if s_ == NSEG - 1 and oc2 == NOC - 1:
                    acc_fin[tt] = d_ev[pp]
        down_wm[s_] = (tE, tE.n)
        run0("v", "drain")
        if s_ + 2 < NSEG:
            wait("g", gu_cons[s_])
            gu_tok[s_ + 2] = dma("g", gu_buf[home].rearrange("p (c f) -> p c f", c=MC),
                                 blob_slice(OFF_GU + (s_ + 2) * D * 2 * SEG, D, 2 * SEG))
            gu_home[s_ + 2] = home
        if s_ + 1 < NSEG:
            wait("g", down_wm[s_])
            wd_tok[s_ + 1] = dma("g", wd_buf.rearrange("p (c o) -> p c o", c=ICS),
                                 blob_slice(OFF_WD + (s_ + 1) * SEG * D, SEG, D))

    out_toks = []
    for tt in range(NTT):
        wait("y", acc_fin[tt])
        out_toks.append(dma("y", out_own.ap()[128 * tt:128 * (tt + 1), :],
                            accum[:, D * tt:D * (tt + 1)]))
    for t in out_toks:
        wait("y", t)

    with nc.Block() as block:
        @block.tensor
        def _(eng):
            for f in ops["t"]:
                f(eng)

        @block.vector
        def _(eng):
            for f in ops["v"]:
                f(eng)

        @block.scalar
        def _(eng):
            for f in ops["s"]:
                f(eng)

        @block.sync
        def _(eng):
            for f in ops["y"]:
                f(eng)

        @block.gpsimd
        def _(eng):
            for f in ops["g"]:
                f(eng)

    stack.close()
    n_inst = sum(len(v) for v in ops.values())
    return nc, n_inst


_PROG_CACHE = {}


def _get_prog(key_cfg, debug=False):
    key = (tuple(sorted(key_cfg.items())), debug)
    if key not in _PROG_CACHE:
        _PROG_CACHE[key] = build_program(make_cfg(**key_cfg), debug=debug)
    return _PROG_CACHE[key]


def stage_inputs(cfg, x, Wq, bq, Wk, bk, Wv, bv, Wo, bo, w_gate_up, w_down):
    S, D, NH, INTER = cfg["S"], cfg["D"], cfg["NH"], cfg["INTER"]
    OCW, TOK, SEG, NSEG, BW = cfg["OCW"], cfg["TOK"], cfg["SEG"], cfg["NSEG"], cfg["BW"]
    f = np.float32
    x = np.asarray(x, dtype=f).reshape(S, D)
    WqT = np.ascontiguousarray(np.asarray(Wq, dtype=f).T)
    WkT = np.ascontiguousarray(np.asarray(Wk, dtype=f).T)
    WvT = np.ascontiguousarray(np.asarray(Wv, dtype=f).T)
    WoT = np.ascontiguousarray(np.asarray(Wo, dtype=f).T)
    WguT = np.ascontiguousarray(np.asarray(w_gate_up, dtype=f).T)
    WdT = np.ascontiguousarray(np.asarray(w_down, dtype=f).T)
    bq = np.asarray(bq, dtype=f).reshape(D)
    bk = np.asarray(bk, dtype=f).reshape(D)
    bv = np.asarray(bv, dtype=f).reshape(D)
    bo = np.asarray(bo, dtype=f).reshape(1, D)

    parts = [WoT.ravel()]
    for s_ in range(NSEG):
        seg = np.concatenate([WguT[:, SEG * s_:SEG * (s_ + 1)],
                              WguT[:, INTER + SEG * s_:INTER + SEG * (s_ + 1)]], axis=1)
        parts.append(np.ascontiguousarray(seg).ravel())
    for s_ in range(NSEG):
        parts.append(np.ascontiguousarray(WdT[SEG * s_:SEG * (s_ + 1), :]).ravel())
    blob = np.concatenate(parts).reshape(NCORE, BW)
    x_c = np.ascontiguousarray(x)

    in_maps = []
    for c in range(NCORE):
        sl = slice(OCW * c, OCW * (c + 1))
        in_maps.append({
            "x_own": np.ascontiguousarray(x[TOK * c:TOK * (c + 1)]),
            "x_full": x_c,
            "wq_t": np.ascontiguousarray(WqT[:, sl]),
            "wk_t": np.ascontiguousarray(WkT[:, sl]),
            "wv_t": np.ascontiguousarray(WvT[:, sl]),
            "bq_c": np.ascontiguousarray(bq[sl].reshape(OCW, 1)),
            "bk_c": np.ascontiguousarray(bk[sl].reshape(OCW, 1)),
            "bv_c": np.ascontiguousarray(bv[sl].reshape(1, OCW)),
            "bo_c": bo,
            "wblob": np.ascontiguousarray(blob[c:c + 1]),
        })
    return in_maps


def kernel(L0_attn_hidden_states, L0_attn_Wq, L0_attn_bq, L0_attn_Wk, L0_attn_bk,
           L0_attn_Wv, L0_attn_bv, L0_attn_Wo, L0_attn_bo,
           L0_ffn_w_gate_up, L0_ffn_w_down):
    key_cfg = dict(S=8192, D=2048, NH=16, INTER=8192)
    cfg = make_cfg(**key_cfg)
    nc, _ = _get_prog(key_cfg)
    in_maps = stage_inputs(cfg, L0_attn_hidden_states, L0_attn_Wq, L0_attn_bq,
                           L0_attn_Wk, L0_attn_bk, L0_attn_Wv, L0_attn_bv,
                           L0_attn_Wo, L0_attn_bo, L0_ffn_w_gate_up, L0_ffn_w_down)
    res = run_bass_kernel_spmd(nc, in_maps, core_ids=list(range(NCORE)))
    out = np.concatenate([res.results[c]["out_own"] for c in range(NCORE)], axis=0)
    return out.reshape(1, cfg["S"], cfg["D"]).astype(np.float32)


# version 22
# speedup vs baseline: 1.0391x; 1.0074x over previous
import numpy as np
import concourse.bass as bass
import concourse.mybir as mybir
from concourse.bass_utils import run_bass_kernel_spmd
from concourse.aot_env import allow_ici_all_to_all

F32 = mybir.dt.float32
BF16 = mybir.dt.bfloat16
AF = mybir.ActivationFunctionType
ALU = mybir.AluOpType
AX = mybir.AxisListType

NCORE = 8
HD = 128
EPS = 1e-5
ALPHA = 1.0 / float(np.sqrt(HD))


def make_cfg(S=8192, D=2048, NH=16, INTER=8192):
    c = dict(S=S, D=D, NH=NH, INTER=INTER)
    c["HPC"] = NH // NCORE
    c["OCW"] = c["HPC"] * HD
    c["TOK"] = S // NCORE
    c["NTT"] = c["TOK"] // 128
    c["TB"] = min(512 if D <= 1024 else 256, c["TOK"])
    c["NB"] = S // c["TB"]
    c["NST"] = S // 512
    c["MC"] = D // 128
    c["NOC"] = D // 512
    c["SEG"] = 512
    c["NSEG"] = INTER // c["SEG"]
    c["TS"] = min(512, c["TOK"])
    c["NTS"] = c["TOK"] // c["TS"]
    c["OFF_WO"] = 0
    c["OFF_GU"] = D * D
    c["OFF_WD"] = D * D + 2 * INTER * D
    c["BLOB"] = D * D + 2 * INTER * D + INTER * D
    assert c["BLOB"] % NCORE == 0
    c["BW"] = c["BLOB"] // NCORE
    assert D == NH * HD
    return c


class _Sem:
    __slots__ = ("h", "n")

    def __init__(self, h):
        self.h = h
        self.n = 0


def build_program(cfg, debug=False):
    S, D, NH, INTER = cfg["S"], cfg["D"], cfg["NH"], cfg["INTER"]
    HPC, OCW, TOK, NTT = cfg["HPC"], cfg["OCW"], cfg["TOK"], cfg["NTT"]
    TB, NB, NST, MC, NOC = cfg["TB"], cfg["NB"], cfg["NST"], cfg["MC"], cfg["NOC"]
    SEG, NSEG, TS, NTS = cfg["SEG"], cfg["NSEG"], cfg["TS"], cfg["NTS"]
    BW = cfg["BW"]
    NVT = S // 128
    VTPB = TB // 128
    ICS = SEG // 128

    nc = bass.Bass("TRN2", target_bir_lowering=False, debug=False)

    x_own = nc.dram_tensor("x_own", [TOK, D], F32, kind="ExternalInput")
    x_full = nc.dram_tensor("x_full", [S, D], F32, kind="ExternalInput")
    wq_t = nc.dram_tensor("wq_t", [D, OCW], F32, kind="ExternalInput")
    wk_t = nc.dram_tensor("wk_t", [D, OCW], F32, kind="ExternalInput")
    wv_t = nc.dram_tensor("wv_t", [D, OCW], F32, kind="ExternalInput")
    bq_c = nc.dram_tensor("bq_c", [OCW, 1], F32, kind="ExternalInput")
    bk_c = nc.dram_tensor("bk_c", [OCW, 1], F32, kind="ExternalInput")
    bv_c = nc.dram_tensor("bv_c", [1, OCW], F32, kind="ExternalInput")
    bo_c = nc.dram_tensor("bo_c", [1, D], F32, kind="ExternalInput")
    wblob = nc.dram_tensor("wblob", [1, BW], F32, kind="ExternalInput")
    out_own = nc.dram_tensor("out_own", [TOK, D], F32, kind="ExternalOutput")

    xn_own_dram = nc.dram_tensor("xn_own_dram", [D, TOK], BF16)
    ag_w_in = nc.dram_tensor("ag_w_in", [1, BW], BF16)
    ag_w_out = nc.dram_tensor("ag_w_out", [NCORE, BW], BF16, addr_space="Shared")
    PH = 2 if TOK >= 1024 else 1
    a2a_in_p = [nc.dram_tensor("a2a_in%d" % p, [NCORE * OCW, TOK // PH], BF16)
                for p in range(PH)]
    a2a_out_p = [nc.dram_tensor("a2a_out%d" % p, [NCORE * OCW, TOK // PH], BF16)
                 for p in range(PH)]

    dbg = {}
    if debug:
        dbg["xn"] = nc.dram_tensor("dbg_xn", [D, TOK], BF16, kind="ExternalOutput")
        dbg["q"] = nc.dram_tensor("dbg_q", [OCW, S], BF16, kind="ExternalOutput")
        dbg["k"] = nc.dram_tensor("dbg_k", [OCW, S], BF16, kind="ExternalOutput")
        dbg["v"] = nc.dram_tensor("dbg_v", [S, OCW], BF16, kind="ExternalOutput")
        dbg["bc"] = nc.dram_tensor("dbg_bc", [128, HPC], F32, kind="ExternalOutput")

    from contextlib import ExitStack
    stack = ExitStack()
    en = stack.enter_context

    K_ = 1024
    AR_BYTES = 192 * K_
    arena = en(nc.sbuf_tensor("arena", [128, AR_BYTES // 4], F32))
    arf = arena[:, :]
    arb = arf.bitcast(BF16)

    def bfv(off, cols, parts=128):
        assert off % 32 == 0 and off + 2 * cols <= AR_BYTES, (off, cols)
        return arb[0:parts, off // 2: off // 2 + cols]

    def f32v(off, cols, parts=128):
        assert off % 32 == 0 and off + 4 * cols <= AR_BYTES, (off, cols)
        return arf[0:parts, off // 4: off // 4 + cols]

    o = 0
    qw_sb = bfv(o, MC * OCW); o += 2 * MC * OCW
    kw_sb = bfv(o, MC * OCW); o += 2 * MC * OCW
    vw_sb = bfv(o, MC * OCW); o += 2 * MC * OCW
    _qkv_end = o
    Q_sb = bfv(o, HPC * S); o += 2 * HPC * S
    K_sb = bfv(o, HPC * S); o += 2 * HPC * S
    V_sb = bfv(o, NVT * OCW); o += 2 * NVT * OCW
    _xns = o
    xn_blk = [bfv(_xns + i * 2 * MC * TB, MC * TB) for i in range(2)]
    o += 2 * 2 * MC * TB
    _attn_hi = o
    TAIL0 = AR_BYTES - 24 * K_
    assert _attn_hi <= TAIL0
    p_buf = [bfv(TAIL0 + i * K_, 512) for i in range(4)]
    masks = [bfv(TAIL0 + 4 * K_ + i * K_, 512) for i in range(4)]
    ho_buf = [bfv(TAIL0 + 8 * K_ + i * K_, 512) for i in range(2)]
    r_sb2 = [f32v(TAIL0 + 10 * K_ + i * 2 * K_, 512) for i in range(2)]

    NXT = 2
    NXB = 2 * (TB // 128)
    xt = [f32v(_attn_hi + i * 4 * D, D) for i in range(NXT)]
    xnb = [bfv(TAIL0 + 4 * K_ + i * 2 * D, D) for i in range(NXB)]
    assert _attn_hi + NXT * 4 * D <= TAIL0
    assert 4 * K_ + NXB * 2 * D <= 24 * K_

    accum = f32v(0, NTT * D)
    _A = 4 * NTT * D
    ho_own = bfv(_A, MC * TOK)
    xn_own = bfv(_A, MC * TOK)
    _B = _A + 2 * MC * TOK
    gu_buf = [bfv(_B, MC * 2 * SEG),
              bfv(AR_BYTES - 4 * MC * SEG, MC * 2 * SEG)]
    _C = _B + 2 * MC * K_
    wo_buf = [bfv(_xns + i * MC * K_, MC * 512) for i in range(2)]
    _wo_end = _xns + 2 * MC * K_
    _x_off = _B if _B + 8 * D <= _xns else _wo_end
    x_t4 = [f32v(_x_off + i * 4 * D, D) for i in range(2)]
    bo_f = f32v(0, D, parts=1)
    wd_buf = bfv(_C, ICS * D)
    _E = _C + 8 * D
    h_sb = bfv(_E, ICS * TOK)
    _F = max(_E + 2 * ICS * TOK, _wo_end,
             _x_off + 8 * D if _x_off >= _wo_end else 0)
    bo_hi = bfv(_F, D, parts=1)
    bo_lo = bfv(_F + 2 * D, D, parts=1)
    assert _F + 4 * D <= AR_BYTES - 4 * MC * SEG, ("P4 map overflow", _F, AR_BYTES)

    gs_sb = en(nc.sbuf_tensor("gs_sb", [128, TS], BF16))[:, :]
    consts = en(nc.sbuf_tensor("consts", [128, 1280], BF16))
    ident = consts[:, 0:128]
    ones_col = consts[:, 128:129]
    ones_row = consts[0:1, 160:288]
    ones_mat = consts[:, 1024:1152]
    bv_hi = consts[0:1, 288:288 + OCW]
    bv_lo = consts[0:1, 544:544 + OCW]
    fsm = en(nc.sbuf_tensor("fsm", [128, 3072], F32))
    ones_f32 = fsm[0:1, 0:128]
    stats = fsm[0:1, 1536:1664]
    sc1 = fsm[0:1, 1728:1736]
    bq_sb = fsm[:, 1744:1744 + HPC]
    bk_sb = fsm[:, 1752:1752 + HPC]
    biasC = fsm[:, 1760:1760 + HPC]
    mst = fsm[:, 1776:1792]
    bv_f = fsm[0:1, 1792:1792 + OCW]
    bv_t1 = fsm[0:1, 384:384 + OCW]
    bnst = [fsm[:, 640 + 24 * s_:640 + 24 * s_ + 24] for s_ in range(4)]
    mvs = fsm[:, 736:752]
    veps = [fsm[:, 752 + 2 * q_:754 + 2 * q_] for q_ in range(4)]
    lnR2 = [fsm[:, 760 + 2 * q_:762 + 2 * q_] for q_ in range(4)]
    lnM2 = [fsm[:, 768 + 2 * q_:770 + 2 * q_] for q_ in range(4)]
    epsc = fsm[:, 776:777]
    identF = fsm[:, 2304:2432]
    bo_t1 = fsm[0:1, 2432:2432 + 512]

    psA = en(nc.psum_tensor("psA", [128, 1024], F32))
    psB = en(nc.psum_tensor("psB", [128, 1024], F32))
    psm = [en(nc.psum_tensor("psm%d" % i, [128, 512], F32)) for i in range(4)]

    NCH = 6
    dch = {"y": [_Sem(en(nc.semaphore("sY%d" % i))) for i in range(NCH)],
           "g": [_Sem(en(nc.semaphore("sG%d" % i))) for i in range(NCH)]}
    _rr = {"y": 0, "g": 0}
    tE = _Sem(en(nc.semaphore("tE")))
    vE = _Sem(en(nc.semaphore("vE")))
    scE = _Sem(en(nc.semaphore("scE")))
    gE = _Sem(en(nc.semaphore("gE")))
    ccs = [_Sem(en(nc.semaphore("cc%d" % i))) for i in range(3)]

    ops = {k: [] for k in "tvsyg"}
    _wmax = {}

    def em(e, f):
        ops[e].append(f)

    def wait(e, tok):
        if not tok:
            return
        sem, val = tok
        if val <= 0:
            return
        key = (e, id(sem))
        if _wmax.get(key, 0) >= val:
            return
        _wmax[key] = val
        em(e, lambda eng, _s=sem.h, _t=val: eng.wait_ge(_s, _t))

    def tmax(*toks):
        best = 0
        for t in toks:
            if t and (not best or t[1] > best[1]):
                if best:
                    assert t[0] is best[0]
                best = t
        return best

    def dma(e, out, in_, **kw):
        ch = _rr[e] % NCH
        _rr[e] += 1
        sem = dch[e][ch]
        wait(e, (sem, sem.n))
        sem.n += 16
        em(e, lambda eng, _o=out, _i=in_, _s=sem.h, _k=dict(kw):
           eng.dma_start(out=_o, in_=_i, **_k).then_inc(_s, 16))
        return (sem, sem.n)

    def run(e, sem, meth, *a, **kw):
        sem.n += 1
        em(e, lambda eng, _m=meth, _a=a, _k=dict(kw), _s=sem.h:
           getattr(eng, _m)(*_a, **_k).then_inc(_s, 1))
        return (sem, sem.n)

    def run0(e, meth, *a, **kw):
        em(e, lambda eng, _m=meth, _a=a, _k=dict(kw): getattr(eng, _m)(*_a, **_k))

    def mm(out, lhsT, rhs, start, stop):
        run0("t", "matmul", out, lhsT, rhs, start=start, stop=stop,
             skip_group_check=True)

    def mm_i(out, lhsT, rhs, start, stop):
        return run("t", tE, "matmul", out, lhsT, rhs, start=start, stop=stop,
                   skip_group_check=True)


    def fence(wait_engine, q):
        for sem in dch[q]:
            wait(wait_engine, (sem, sem.n))

    qkv_toks = []
    for (w_in, w_sb) in ((wq_t, qw_sb), (wk_t, kw_sb), (wv_t, vw_sb)):
        qkv_toks.append(dma("g", w_sb.rearrange("p (c o) -> p c o", c=MC),
                            w_in.ap().rearrange("(c p) o -> p c o", p=128)))
    b_toks = [dma("y", bq_sb[:, :], bq_c.ap().rearrange("(c p) one -> p (c one)", p=128),
                  allow_slow_non_contiguous=True),
              dma("y", bk_sb[:, :], bk_c.ap().rearrange("(c p) one -> p (c one)", p=128),
                  allow_slow_non_contiguous=True),
              dma("y", bv_f[:, :], bv_c[:, :])]

    run("g", gE, "memset", ident, 1.0)
    run("g", gE, "memset", identF, 1.0)
    run("g", gE, "memset", consts[:, 128:160], 1.0)
    run("g", gE, "memset", consts[0:1, 160:288], 1.0)
    run("g", gE, "memset", ones_mat, 1.0)
    run("g", gE, "memset", fsm[0:1, 0:128], 1.0)
    run("g", gE, "memset", epsc, float(EPS))
    wait("g", (gE, gE.n))
    run("g", gE, "affine_select", out=ident, in_=ident, compare_op=ALU.is_equal,
        fill=0.0, base=0, pattern=[[-1, 128]], channel_multiplier=1)
    run("g", gE, "affine_select", out=identF, in_=identF, compare_op=ALU.is_equal,
        fill=0.0, base=0, pattern=[[-1, 128]], channel_multiplier=1)
    g_const = (gE, gE.n)

    for t in b_toks:
        wait("v", t)
    run0("v", "tensor_copy", bv_hi, bv_f)
    run0("v", "drain")
    run0("v", "tensor_copy", bv_t1, bv_hi)
    run0("v", "drain")
    run0("v", "tensor_sub", bv_t1, bv_f, bv_t1)
    run0("v", "drain")
    run0("v", "tensor_copy", bv_lo, bv_t1)
    bv_split = run("v", vE, "drain")

    NTB = TB // 128
    NG = D // 512
    NGR = (MC + 7) // 8
    assert NG <= 4 and MC % 8 == 0
    psAb = psA[:, :].bitcast(BF16)
    psBb = psB[:, :].bitcast(BF16)
    grans = [psAb[:, 1024:2048], psBb[:, 1024:2048]]
    gran_ev = [0, 0]
    pr_nrm = {}
    pr_agg = {}
    pr_pair = {}
    pr_phB_rd = {0: 0, 1: 0, 2: 0, 3: 0}
    ship_war = {0: [], 1: []}
    xnb_free = {0: 0, 1: 0}
    xnb_war = {}
    blk_ready = {}
    _gt = [0]
    _pq = [0]
    _gu = [0]
    _pend = []
    _xd_tok = {}

    _tinfo = {}
    _sd_rd = {0: 0, 1: 0, 2: 0, 3: 0}

    def _flush_pair():
        if not _pend:
            return
        pend = list(_pend)
        _pend.clear()
        w = len(pend)
        q = _pq[0] % 4
        _pq[0] += 1
        s0 = pend[0] % 4
        assert w == 1 or s0 != 3
        mv2 = mvs[:, 2 * s0:2 * s0 + 2 * w].rearrange("p (s two) -> p two s", two=2)
        wait("s", pr_agg[pend[-1]])
        wait("s", _sd_rd[q])
        wait("s", g_const)
        st = run("s", scE, "activation", veps[q][:, 0:w], mv2[:, 1, :],
                 AF.Sqrt, bias=epsc)
        wait("v", st)
        wait("v", pr_phB_rd[q])
        rt = run("v", vE, "reciprocal", lnR2[q][:, 0:w], veps[q][:, 0:w])
        _sd_rd[q] = rt
        wait("v", rt)
        ptok = run("v", vE, "scalar_tensor_tensor", lnM2[q][:, 0:w],
                   mv2[:, 0, :], -1.0, lnR2[q][:, 0:w], ALU.mult, ALU.mult)
        pr_pair[pend[0] // 2] = ptok
        nrm_last = 0
        for j, t in enumerate(pend):
            slot = t % NXT
            xb = t % NXB
            wait("s", ptok)
            wait("s", _xd_tok.pop(t))
            wait("s", xnb_war.get(xb, 0))
            nrm_last = run("s", scE, "activation", xnb[xb], xt[slot], AF.Identity,
                           scale=lnR2[q][:, j:j + 1], bias=lnM2[q][:, j:j + 1])
            pr_nrm[t] = nrm_last
        pr_phB_rd[q] = nrm_last

    def stage1_tile(dq, src_rows, par, toff, bkey):
        t = _gt[0]
        _gt[0] += 1
        slot, s4 = t % NXT, t % 4
        wait(dq, pr_nrm.get(t - NXT, 0))
        wait(dq, pr_agg.get(t - NXT, 0))
        xd = dma(dq, xt[slot], src_rows)
        _xd_tok[t] = xd
        _tinfo[t] = (par, toff, bkey)
        wait("v", xd)
        wait("v", pr_agg.get(t - 4, 0))
        wait("v", pr_pair.get((t - 4) // 2, 0))
        for g in range(NG):
            run("v", vE, "bn_stats", bnst[s4][:, 6 * g:6 * g + 6],
                xt[slot][:, 512 * g:512 * (g + 1)])
        wait("v", (vE, vE.n))
        pr_agg[t] = run("v", vE, "bn_aggr", mvs[:, 2 * s4:2 * s4 + 2],
                        bnst[s4][:, 0:6 * NG])
        _pend.append(t)
        if len(_pend) == 2:
            _flush_pair()

    def stage2_tile(t):
        if t not in pr_nrm:
            _flush_pair()
        par, toff, bkey = _tinfo.pop(t)
        xb = t % NXB
        wait("t", g_const)
        wait("t", pr_nrm[t])
        for gr in range(NGR):
            cw = min(8, MC - 8 * gr)
            g2 = _gu[0] % 2
            _gu[0] += 1
            wait("t", gran_ev[g2])
            tt2 = 0
            for c8 in range(cw):
                c = 8 * gr + c8
                tt2 = run("t", tE, "transpose",
                          grans[g2][:, 128 * c8:128 * (c8 + 1)],
                          xnb[xb][:, 128 * c:128 * (c + 1)], ident)
            xnb_war[xb] = tt2
            wait("v", tt2)
            wait("v", xnb_free[par])
            for stok in ship_war[par]:
                wait("v", stok)
            dstv = xn_blk[par].rearrange("p (c t) -> p c t", c=MC)[
                :, 8 * gr:8 * gr + cw, toff:toff + 128]
            srcv = grans[g2].rearrange("p (c t) -> p c t", c=8)[:, 0:cw, :]
            gran_ev[g2] = run("v", vE, "tensor_copy", dstv, srcv)
        blk_ready[bkey] = gran_ev[(_gu[0] - 1) % 2]

    st2_q = []

    def stage1_block(gb, src_t, row0, bkey):
        par = gb % 2
        for j in range(NTB):
            st2_q.append(_gt[0])
            stage1_tile(("y", "g")[_gt[0] % 2],
                        src_t.ap()[row0 + 128 * j:row0 + 128 * (j + 1), :],
                        par, 128 * j, bkey)
        if NTB % 2 == 1:
            _flush_pair()
        return par

    def stage2_n(n):
        for _ in range(n):
            if st2_q:
                stage2_tile(st2_q.pop(0))

    NOB = TOK // TB
    for ob in range(NOB):
        par = stage1_block(ob, x_own, ob * TB, ("own", ob))
        _flush_pair()
        stage2_n(NTB)
        wait("y", blk_ready[("own", ob)])
        shv = xn_blk[par].rearrange("p (c t) -> p c t", c=MC)
        toks = [dma("y", xn_own_dram.ap()[:, ob * TB:(ob + 1) * TB]
                    .rearrange("(c p) t -> p c t", p=128), shv)]
        if debug:
            toks.append(dma("y", dbg["xn"].ap()[:, ob * TB:(ob + 1) * TB]
                            .rearrange("(c p) t -> p c t", p=128), shv))
        ship_war[par] = toks
    qk_ev = {0: 0, 1: 0}
    v_ev = {0: 0, 1: 0}
    last_q_ev = 0
    last_v_ev = 0
    sq_free = {0: 0, 1: 0, 2: 0, 3: 0}
    nrm_free = {0: 0, 1: 0}
    nsq = [0]
    npp = [0]
    pending_norm = []

    def flush_norms():
        while pending_norm:
            sqv, idx, sqt = pending_norm.pop(0)
            pp2 = npp[0] % 2
            npp[0] += 1
            wait("t", sqt)
            wait("t", nrm_free[pp2])
            psN = (psA, psB)[pp2][0:1, 0:512]
            mt2 = mm_i(psN[:, 0:TB] if TB < 512 else psN, ones_col, sqv,
                       start=True, stop=True)
            sq_free[nsq[0] % 4] = mt2
            nsq[0] += 1
            wait("v", mt2)
            nrm_free[pp2] = run("v", vE, "reduce_max", stats[0:1, idx:idx + 1],
                                (psA, psB)[pp2][0:1, 0:TB], axis=AX.X)
    def emit_qkv(b):
        nonlocal last_q_ev, last_v_ev
        bi = b
        par = (NOB + b) % 2
        st2c = [NTB]

        def st2():
            if st2c[0] > 0 and st2_q:
                st2c[0] -= 1
                stage2_tile(st2_q.pop(0))
        flush_norms()
        wait("t", blk_ready[b])
        if bi < 2:
            for t in qkv_toks:
                wait("t", t)
            wait("t", g_const)
            for t in b_toks:
                wait("t", t)
        xv = xn_blk[par].rearrange("p (c t) -> p c t", c=MC)
        for which, (wsb, bcol, dst) in enumerate(((qw_sb, bq_sb, Q_sb),
                                                  (kw_sb, bk_sb, K_sb))):
            wv = wsb.rearrange("p (c o) -> p c o", c=MC)
            for h in range(HPC):
                pp = (bi * 2 * HPC + which * HPC + h) % 2
                ps = psm[pp]
                wait("t", qk_ev[pp])
                mt = 0
                for m in range(MC):
                    if m < MC - 1:
                        mm(ps[:, 0:TB], wv[:, m, 128 * h:128 * (h + 1)], xv[:, m, :],
                           start=(m == 0), stop=False)
                    else:
                        mt = mm_i(ps[:, 0:TB], wv[:, m, 128 * h:128 * (h + 1)],
                                  xv[:, m, :], start=(m == 0), stop=True)
                wait("s", mt)
                qk_ev[pp] = run("s", scE, "activation",
                                dst[:, S * h + TB * b:S * h + TB * (b + 1)],
                                ps[:, 0:TB], AF.Identity, bias=bcol[:, h:h + 1])
                last_q_ev = qk_ev[pp]
                sqp = (nsq[0] + len(pending_norm)) % 4
                wait("s", qk_ev[pp])
                wait("s", sq_free[sqp])
                sqt = run("s", scE, "activation", p_buf[sqp][:, 0:TB],
                          dst[:, S * h + TB * b:S * h + TB * (b + 1)], AF.Square)
                pending_norm.append((p_buf[sqp][:, 0:TB],
                                     (2 * h + which) * NB + b, sqt))
                st2()
        wv = vw_sb.rearrange("p (c o) -> p c o", c=MC)
        npair = (VTPB + 1) // 2
        for pair in range(npair):
            pp = 2 + (bi * npair + pair) % 2
            ps = psm[pp]
            wait("t", v_ev[pp - 2])
            wait("t", bv_split)
            nj = min(2, VTPB - 2 * pair)
            mt = 0
            for j in range(nj):
                t4 = 2 * pair + j
                for m in range(MC):
                    mm(ps[:, OCW * j:OCW * (j + 1)], xv[:, m, 128 * t4:128 * (t4 + 1)],
                       wv[:, m, :], start=(m == 0), stop=False)
                mm(ps[:, OCW * j:OCW * (j + 1)], ones_row, bv_hi, start=False, stop=False)
                mt = mm_i(ps[:, OCW * j:OCW * (j + 1)], ones_row, bv_lo,
                          start=False, stop=True)
            wait("v", mt)
            base = OCW * (VTPB * b + 2 * pair)
            v_ev[pp - 2] = run("v", vE, "tensor_copy",
                               V_sb[:, base:base + nj * OCW], ps[:, 0:nj * OCW])
            last_v_ev = v_ev[pp - 2]
            st2()
        while st2c[0] > 0 and st2_q:
            st2()
        xnb_free[par] = (tE, tE.n)

    stage1_block(NOB, x_full, 0, 0)
    if NB > 1:
        stage1_block(NOB + 1, x_full, TB, 1)
    stage2_n(NTB)
    for b in range(NB):
        if b + 2 < NB:
            stage1_block(NOB + b + 2, x_full, (b + 2) * TB, b + 2)
        emit_qkv(b)
    assert not st2_q and not _pend, (st2_q, _pend)

    t_blob = dma("g", ag_w_in[:, :], wblob[:, :])
    wait("g", (tE, tE.n))
    for j in range(4):
        run("g", gE, "memset", masks[j], 1.0)
    wait("g", (gE, gE.n))
    for j in range(4):
        run("g", gE, "affine_select", out=masks[j], in_=masks[j],
            compare_op=ALU.is_ge, fill=0.0, base=-128 * j,
            pattern=[[1, 512]], channel_multiplier=-1)
    g_masks = (gE, gE.n)
    wait("g", t_blob)
    _wag_hole = [None]
    em("g", lambda g: g.wait_ge(*_wag_hole[0]))
    ccs[1].n = 1
    cc_w = (ccs[1], 1)
    em("g", lambda g: g.collective_compute(
        "AllGather", ALU.bypass, replica_groups=[list(range(NCORE))],
        ins=[ag_w_in.ap().opt()], outs=[ag_w_out.ap().opt()]).then_inc(ccs[1].h, 1))

    p2_t_end = (tE, tE.n)
    blobv = ag_w_out.ap().rearrange("r w -> (r w)")

    def blob_slice(off, rows, cols):
        return blobv[off:off + rows * cols].rearrange("(c p o) -> p c o", p=128, o=cols)

    OFF_WO, OFF_GU, OFF_WD = cfg["OFF_WO"], cfg["OFF_GU"], cfg["OFF_WD"]
    wait("g", cc_w)
    wait("g", p2_t_end)
    wo_tok = {0: dma("g", wo_buf[0].rearrange("p (c o) -> p c o", c=MC),
                     blob_slice(OFF_WO, D, D)[:, :, 0:512])}
    if NOC > 1:
        wo_tok[1] = dma("g", wo_buf[1].rearrange("p (c o) -> p c o", c=MC),
                        blob_slice(OFF_WO, D, D)[:, :, 512:1024])

    if debug:
        wait("y", last_q_ev)
        wait("y", last_v_ev)
        dma("y", dbg["q"].ap().rearrange("(h p) t -> p h t", p=128),
            Q_sb.rearrange("p (h t) -> p h t", h=HPC))
        dma("y", dbg["k"].ap().rearrange("(h p) t -> p h t", p=128),
            K_sb.rearrange("p (h t) -> p h t", h=HPC))
        dma("y", dbg["v"].ap().rearrange("(n p) o -> p n o", p=128),
            V_sb.rearrange("p (n o) -> p n o", n=NVT))

    flush_norms()
    wait("v", tmax(nrm_free[0], nrm_free[1]))
    run0("v", "drain")
    biasC_tok = 0
    for h in range(HPC):
        run0("v", "reduce_max", sc1[0:1, 0:1],
             stats[0:1, 2 * h * NB:2 * h * NB + NB], axis=AX.X)
        run0("v", "drain")
        run0("v", "reduce_max", sc1[0:1, 1:2],
             stats[0:1, (2 * h + 1) * NB:(2 * h + 1) * NB + NB], axis=AX.X)
        run0("v", "drain")
        vt = run("v", vE, "tensor_tensor", sc1[0:1, 2:3], sc1[0:1, 0:1],
                 sc1[0:1, 1:2], ALU.mult)
        wait("s", vt)
        st = run("s", scE, "activation", sc1[0:1, 3:4], sc1[0:1, 2:3], AF.Sqrt)
        wait("v", st)
        vt = run("v", vE, "tensor_scalar_mul", sc1[0:1, 4:5], sc1[0:1, 3:4], -ALPHA)
        wait("t", vt)
        wait("t", last_v_ev)
        mt = mm_i(psm[2][:, 0:1], ones_f32, sc1[0:1, 4:5], start=True, stop=True)
        wait("v", mt)
        biasC_tok = run("v", vE, "tensor_copy", biasC[:, h:h + 1], psm[2][:, 0:1])
        run0("v", "drain")
    p2norm_t = (tE, tE.n)
    if debug:
        wait("y", biasC_tok)
        dma("y", dbg["bc"].ap(), biasC)

    wait("y", p2_t_end)
    bo_tok0 = dma("y", bo_f[:, :], bo_c[:, :])
    wait("v", bo_tok0)
    wait("v", pr_nrm[max(pr_nrm)])
    wait("v", (vE, vE.n))
    run0("v", "tensor_copy", bo_hi, bo_f)
    run0("v", "drain")
    for q in range(D // 512):
        run0("v", "tensor_copy", bo_t1, bo_hi[0:1, 512 * q:512 * (q + 1)])
        run0("v", "drain")
        run0("v", "tensor_sub", bo_t1, bo_f[0:1, 512 * q:512 * (q + 1)], bo_t1)
        run0("v", "drain")
        run0("v", "tensor_copy", bo_lo[0:1, 512 * q:512 * (q + 1)], bo_t1)
        run0("v", "drain")
    bo_split = run("v", vE, "drain")

    scq = [psA[:, 0:512], psA[:, 512:1024], psB[:, 0:512], psB[:, 512:1024]]
    exp_tok = {q: 0 for q in range(4)}
    pbuf_cons = {q: p2norm_t for q in range(4)}
    sc_cons = {q: nrm_free[q // 2] for q in range(4)}
    sc_cons[1] = tmax(sc_cons[1], gran_ev[0])
    sc_cons[3] = tmax(sc_cons[3], gran_ev[1])
    av_free = {0: last_v_ev, 1: last_v_ev}
    sum_free = {0: biasC_tok, 1: biasC_tok}
    ho_free = {0: [], 1: []}
    wait("s", biasC_tok)
    wait("t", last_q_ev)
    wait("t", last_v_ev)
    a2a_ship = {p: [] for p in range(PH)}
    TOKC = min(TOK // PH, 512) if PH == 2 else min(TOK, 512)
    LA = 3
    TLA = 4

    if PH == 2:
        strip_order = [i for i in range(NST - 2, -1, -2)] + \
                      [i for i in range(NST - 1, 0, -2)]
    else:
        strip_order = list(range(NST))
    jobs = []
    for i in strip_order:
        nt = 4 * (i + 1)
        for h in range(HPC):
            for kt in range(nt):
                jobs.append((i, kt, h))
    njobs = len(jobs)
    avt_h = {}
    tails = {}

    def emit_sc(g):
        i, kt, h = jobs[g]
        q4 = g % 4
        qs = Q_sb[:, S * h + 512 * i:S * h + 512 * (i + 1)]
        wait("t", exp_tok[q4])
        wait("t", sc_cons[q4])
        mt = mm_i(scq[q4], K_sb[:, S * h + 128 * kt:S * h + 128 * (kt + 1)],
                  qs, start=True, stop=True)
        if g % 2 == 0:
            return
        qe, qo = (g - 1) % 4, q4
        pj = bfv(TAIL0 + ((g - 1) % 4) * K_, 1024)
        psrc = (psA, psB)[(g % 4) // 2][:, 0:1024]
        wait("s", mt)
        wait("s", pbuf_cons[qe])
        wait("s", pbuf_cons[qo])
        et = run("s", scE, "activation", pj, psrc, AF.Exp,
                 bias=biasC[:, h:h + 1], scale=ALPHA)
        exp_tok[qe] = exp_tok[qo] = et
        sc_cons[qe] = sc_cons[qo] = 0
        k0 = jobs[g - 1][1]
        if kt >= 4 * i or k0 >= 4 * i:
            wait("v", g_masks)
            wait("v", et)
            for (qq, kk) in ((qe, k0), (qo, kt)):
                if kk >= 4 * i:
                    run0("v", "tensor_tensor", p_buf[qq], p_buf[qq],
                         masks[kk - 4 * i], ALU.mult)
            gt = run("v", vE, "drain")
            exp_tok[qe] = exp_tok[qo] = gt

    def emit_av(g):
        i, kt, h = jobs[g]
        q4 = g % 4
        nt = 4 * (i + 1)
        wait("t", exp_tok[q4])
        if kt == 0:
            wait("t", av_free[h])
            wait("t", sum_free[h])
        mm(psm[h][:, 0:512],
           V_sb[:, OCW * kt + 128 * h:OCW * kt + 128 * (h + 1)],
           p_buf[q4], start=(kt == 0), stop=(kt == nt - 1))
        avt = mm_i(psm[2 + h][:, 0:512], ones_mat, p_buf[q4],
                   start=(kt == 0), stop=(kt == nt - 1))
        pbuf_cons[q4] = avt
        if kt == nt - 1:
            avt_h[(i, h)] = avt
            tails[g + TLA] = (i, h)

    def emit_tail(i, h):
        hb = (i * HPC + h) % 2
        wait("v", avt_h[(i, h)])
        for t in ho_free[hb]:
            wait("v", t)
        vtr = run("v", vE, "reciprocal", r_sb2[hb], psm[2 + h][:, 0:512])
        sum_free[h] = vtr
        run0("v", "drain")
        vt = run("v", vE, "tensor_tensor", ho_buf[hb], psm[h][:, 0:512],
                 r_sb2[hb], ALU.mult)
        av_free[h] = vt
        wait("y", vt)
        ho_free[hb] = []
        for u in range(512 // TOKC):
            g_t = 512 * i + u * TOKC
            rows0 = OCW * (g_t // TOK) + 128 * h
            hp = (g_t % TOK) // (TOK // PH)
            col0 = g_t % (TOK // PH)
            tok = dma("y", a2a_in_p[hp].ap()[rows0:rows0 + 128, col0:col0 + TOKC],
                      ho_buf[hb][:, u * TOKC:(u + 1) * TOKC])
            ho_free[hb].append(tok)
            a2a_ship[hp].append(tok)

    wag_gate = [None]
    xfree_gate = [None]
    XSAFE = 4
    _remaining = set(strip_order)
    _done_ct = {}
    for i in strip_order:
        _done_ct[i] = 4 * (i + 1) * HPC
    _per_strip_left = dict(_done_ct)
    for g in range(njobs + TLA + 1):
        if g in tails:
            i_, h_ = tails.pop(g)
            emit_tail(i_, h_)
        if g < njobs:
            emit_sc(g)
        if g - LA >= 0 and g - LA < njobs:
            emit_av(g - LA)
            i_av = jobs[g - LA][0]
            _per_strip_left[i_av] -= 1
            if _per_strip_left[i_av] == 0:
                _remaining.discard(i_av)
                if xfree_gate[0] is None and all(i2 < XSAFE for i2 in _remaining):
                    xfree_gate[0] = (tE, tE.n)
        if g == njobs // 3:
            wag_gate[0] = (tE, tE.n)
    p3_t_end = (tE, tE.n)
    if xfree_gate[0] is None:
        xfree_gate[0] = p3_t_end
    _wag_hole[0] = (wag_gate[0][0].h, wag_gate[0][1])

    cc_a2a_p = []
    for p in range(PH):
        for t in a2a_ship[p]:
            wait("g", t)
        ccs[2].n += 1
        cc_a2a_p.append((ccs[2], ccs[2].n))

        def _emit_a2a(g, _p=p):
            with allow_ici_all_to_all():
                g.collective_compute(
                    "AllToAll", ALU.bypass, replica_groups=[list(range(NCORE))],
                    ins=[a2a_in_p[_p].ap().opt()],
                    outs=[a2a_out_p[_p].ap().opt()]).then_inc(ccs[2].h, 1)
        em("g", _emit_a2a)
    cc_a2a = cc_a2a_p[-1]

    wait("g", p3_t_end)
    gu_tok = {0: dma("g", gu_buf[1].rearrange("p (c f) -> p c f", c=MC),
                     blob_slice(OFF_GU, D, 2 * SEG))}
    gu_home = {0: 1}
    wait("y", p3_t_end)
    ho_own_toks = []
    for p in range(PH):
        wait("y", cc_a2a_p[p])
        ho_own_toks.append(
            dma("y", ho_own.rearrange("p (c t) -> p c t", c=MC)[
                :, :, (TOK // PH) * p:(TOK // PH) * (p + 1)],
                a2a_out_p[p].ap().rearrange("(c p) t -> p c t", p=128)))

    x4_done = {}
    acc_done = {}
    for tt in range(NTT):
        par = tt % 2
        wait("y", x4_done.get(tt - 2, 0))
        wait("y", xfree_gate[0])
        x4_tok = dma("y", x_t4[par], x_own.ap()[128 * tt:128 * (tt + 1), :])
        wait("v", x4_tok)
        wait("v", p3_t_end)
        if tt == 0:
            wait("v", bo_split)
        acc_done[tt] = run("v", vE, "tensor_copy",
                           accum[:, D * tt:D * (tt + 1)], x_t4[par])
        x4_done[tt] = acc_done[tt]
    run0("v", "drain")
    acc_init = acc_done[NTT - 1]

    hov = ho_own.rearrange("p (c t) -> p c t", c=MC)
    op_ev = {0: acc_init, 1: acc_init}
    NHLF = min(PH, 2)
    tt_half = [[tt for tt in range(NTT) if (tt * 128) // max(1, TOK // NHLF) == hf]
               for hf in range(NHLF)]
    use_seq = [(hf, oc) for hf in range(NHLF) for oc in range(NOC)]
    wo_use_tok = {}
    wo_cons = {}
    for ui, (hf, oc) in enumerate(use_seq):
        if ui < 2:
            wo_use_tok[ui] = wo_tok[min(oc, max(wo_tok))]
    for ui, (hf, oc) in enumerate(use_seq):
        par = ui % 2
        wov = wo_buf[par].rearrange("p (c o) -> p c o", c=MC)
        for tt in tt_half[hf]:
            pp = (ui * NTT + tt) % 2
            ps = psm[pp]
            wait("t", wo_use_tok[ui])
            wait("t", ho_own_toks[min(len(ho_own_toks) - 1, hf)])
            wait("t", op_ev[pp])
            wait("t", bo_split)
            for c in range(MC):
                mm(ps[:, 0:512], hov[:, c, 128 * tt:128 * (tt + 1)], wov[:, c, :],
                   start=(c == 0), stop=False)
            mm(ps[:, 0:512], ones_row, bo_hi[0:1, 512 * oc:512 * (oc + 1)],
               start=False, stop=False)
            mt = mm_i(ps[:, 0:512], ones_row, bo_lo[0:1, 512 * oc:512 * (oc + 1)],
                      start=False, stop=True)
            wait("v", mt)
            wait("v", acc_done[tt])
            op_ev[pp] = run("v", vE, "tensor_tensor",
                            accum[:, D * tt + 512 * oc:D * tt + 512 * (oc + 1)],
                            accum[:, D * tt + 512 * oc:D * tt + 512 * (oc + 1)],
                            ps[:, 0:512], ALU.add)
        wo_cons[par] = (tE, tE.n)
        if ui + 2 < len(use_seq):
            nhf, noc = use_seq[ui + 2]
            wait("g", wo_cons[par])
            wo_use_tok[ui + 2] = dma(
                "g", wo_buf[par].rearrange("p (c o) -> p c o", c=MC),
                blob_slice(OFF_WO, D, D)[:, :, 512 * noc:512 * (noc + 1)])
    oproj_t_end = (tE, tE.n)
    acc_guard = run("v", vE, "drain")

    wait("g", oproj_t_end)
    wait("g", acc_guard)
    xn_own_tok = dma("g", xn_own.rearrange("p (c t) -> p c t", c=MC),
                     xn_own_dram.ap().rearrange("(c p) t -> p c t", p=128))
    wd_tok = {0: dma("g", wd_buf.rearrange("p (c o) -> p c o", c=ICS),
                     blob_slice(OFF_WD, SEG, D))}
    if NSEG > 1:
        gu_tok[1] = dma("g", gu_buf[0].rearrange("p (c f) -> p c f", c=MC),
                        blob_slice(OFF_GU + D * 2 * SEG, D, 2 * SEG))
        gu_home[1] = 0

    xnv = xn_own.rearrange("p (c t) -> p c t", c=MC)
    g_free = 0
    u_free = acc_guard
    gs_free = 0
    d_ev = {0: acc_guard, 1: acc_guard, 2: acc_guard, 3: acc_guard}
    down_wm = {-1: oproj_t_end}
    acc_fin = {}
    gu_cons = {}
    hmul = 0
    for s_ in range(NSEG):
        home = gu_home[s_]
        guv = gu_buf[home].rearrange("p (c f) -> p c f", c=MC)
        wdv = wd_buf.rearrange("p (c o) -> p c o", c=ICS)
        wait("t", gu_tok[s_])
        wait("t", xn_own_tok)
        for ts in range(NTS):
            for ic in range(ICS):
                wait("t", g_free)
                wait("t", d_ev[2])
                wait("t", d_ev[3])
                mtg = 0
                for m in range(MC):
                    if m < MC - 1:
                        mm(psm[2][:, 0:TS], guv[:, m, 128 * ic:128 * (ic + 1)],
                           xnv[:, m, TS * ts:TS * (ts + 1)], start=(m == 0), stop=False)
                    else:
                        mtg = mm_i(psm[2][:, 0:TS], guv[:, m, 128 * ic:128 * (ic + 1)],
                                   xnv[:, m, TS * ts:TS * (ts + 1)], start=(m == 0),
                                   stop=True)
                wait("t", u_free)
                wait("t", d_ev[2])
                wait("t", d_ev[3])
                mtu = 0
                for m in range(MC):
                    if m < MC - 1:
                        mm(psm[3][:, 0:TS], guv[:, m, SEG + 128 * ic:SEG + 128 * (ic + 1)],
                           xnv[:, m, TS * ts:TS * (ts + 1)], start=(m == 0), stop=False)
                    else:
                        mtu = mm_i(psm[3][:, 0:TS],
                                   guv[:, m, SEG + 128 * ic:SEG + 128 * (ic + 1)],
                                   xnv[:, m, TS * ts:TS * (ts + 1)], start=(m == 0),
                                   stop=True)
                wait("s", mtg)
                wait("s", gs_free)
                g_free = run("s", scE, "activation", gs_sb, psm[2][:, 0:TS], AF.Silu)
                wait("v", g_free)
                wait("v", mtu)
                wait("v", down_wm[s_ - 1])
                hmul = run("v", vE, "tensor_tensor",
                           h_sb[:, TOK * ic + TS * ts:TOK * ic + TS * (ts + 1)],
                           gs_sb, psm[3][:, 0:TS], ALU.mult)
                u_free = hmul
                gs_free = hmul
        gu_cons[s_] = (tE, tE.n)
        hv = h_sb.rearrange("p (c t) -> p c t", c=ICS)
        wait("t", wd_tok[s_])
        wait("t", hmul)
        for tt in range(NTT):
            for oc2 in range(NOC):
                pp = (tt * NOC + oc2) % 4
                ps = psm[pp]
                wait("t", d_ev[pp])
                mt = 0
                for ic in range(ICS):
                    if ic < ICS - 1:
                        mm(ps[:, 0:512], hv[:, ic, 128 * tt:128 * (tt + 1)],
                           wdv[:, ic, 512 * oc2:512 * (oc2 + 1)],
                           start=(ic == 0), stop=False)
                    else:
                        mt = mm_i(ps[:, 0:512], hv[:, ic, 128 * tt:128 * (tt + 1)],
                                  wdv[:, ic, 512 * oc2:512 * (oc2 + 1)],
                                  start=(ic == 0), stop=True)
                wait("v", mt)
                d_ev[pp] = run("v", vE, "tensor_tensor",
                               accum[:, D * tt + 512 * oc2:D * tt + 512 * (oc2 + 1)],
                               accum[:, D * tt + 512 * oc2:D * tt + 512 * (oc2 + 1)],
                               ps[:, 0:512], ALU.add)
                if s_ == NSEG - 1 and oc2 == NOC - 1:
                    acc_fin[tt] = d_ev[pp]
        down_wm[s_] = (tE, tE.n)
        run0("v", "drain")
        if s_ + 2 < NSEG:
            wait("g", gu_cons[s_])
            gu_tok[s_ + 2] = dma("g", gu_buf[home].rearrange("p (c f) -> p c f", c=MC),
                                 blob_slice(OFF_GU + (s_ + 2) * D * 2 * SEG, D, 2 * SEG))
            gu_home[s_ + 2] = home
        if s_ + 1 < NSEG:
            wait("g", down_wm[s_])
            wd_tok[s_ + 1] = dma("g", wd_buf.rearrange("p (c o) -> p c o", c=ICS),
                                 blob_slice(OFF_WD + (s_ + 1) * SEG * D, SEG, D))

    out_toks = []
    for tt in range(NTT):
        wait("y", acc_fin[tt])
        out_toks.append(dma("y", out_own.ap()[128 * tt:128 * (tt + 1), :],
                            accum[:, D * tt:D * (tt + 1)]))
    for t in out_toks:
        wait("y", t)

    with nc.Block() as block:
        @block.tensor
        def _(eng):
            for f in ops["t"]:
                f(eng)

        @block.vector
        def _(eng):
            for f in ops["v"]:
                f(eng)

        @block.scalar
        def _(eng):
            for f in ops["s"]:
                f(eng)

        @block.sync
        def _(eng):
            for f in ops["y"]:
                f(eng)

        @block.gpsimd
        def _(eng):
            for f in ops["g"]:
                f(eng)

    stack.close()
    n_inst = sum(len(v) for v in ops.values())
    return nc, n_inst


_PROG_CACHE = {}


def _get_prog(key_cfg, debug=False):
    key = (tuple(sorted(key_cfg.items())), debug)
    if key not in _PROG_CACHE:
        _PROG_CACHE[key] = build_program(make_cfg(**key_cfg), debug=debug)
    return _PROG_CACHE[key]


def stage_inputs(cfg, x, Wq, bq, Wk, bk, Wv, bv, Wo, bo, w_gate_up, w_down):
    S, D, NH, INTER = cfg["S"], cfg["D"], cfg["NH"], cfg["INTER"]
    OCW, TOK, SEG, NSEG, BW = cfg["OCW"], cfg["TOK"], cfg["SEG"], cfg["NSEG"], cfg["BW"]
    f = np.float32
    x = np.asarray(x, dtype=f).reshape(S, D)
    WqT = np.ascontiguousarray(np.asarray(Wq, dtype=f).T)
    WkT = np.ascontiguousarray(np.asarray(Wk, dtype=f).T)
    WvT = np.ascontiguousarray(np.asarray(Wv, dtype=f).T)
    WoT = np.ascontiguousarray(np.asarray(Wo, dtype=f).T)
    WguT = np.ascontiguousarray(np.asarray(w_gate_up, dtype=f).T)
    WdT = np.ascontiguousarray(np.asarray(w_down, dtype=f).T)
    bq = np.asarray(bq, dtype=f).reshape(D)
    bk = np.asarray(bk, dtype=f).reshape(D)
    bv = np.asarray(bv, dtype=f).reshape(D)
    bo = np.asarray(bo, dtype=f).reshape(1, D)

    parts = [WoT.ravel()]
    for s_ in range(NSEG):
        seg = np.concatenate([WguT[:, SEG * s_:SEG * (s_ + 1)],
                              WguT[:, INTER + SEG * s_:INTER + SEG * (s_ + 1)]], axis=1)
        parts.append(np.ascontiguousarray(seg).ravel())
    for s_ in range(NSEG):
        parts.append(np.ascontiguousarray(WdT[SEG * s_:SEG * (s_ + 1), :]).ravel())
    blob = np.concatenate(parts).reshape(NCORE, BW)
    x_c = np.ascontiguousarray(x)

    in_maps = []
    for c in range(NCORE):
        sl = slice(OCW * c, OCW * (c + 1))
        in_maps.append({
            "x_own": np.ascontiguousarray(x[TOK * c:TOK * (c + 1)]),
            "x_full": x_c,
            "wq_t": np.ascontiguousarray(WqT[:, sl]),
            "wk_t": np.ascontiguousarray(WkT[:, sl]),
            "wv_t": np.ascontiguousarray(WvT[:, sl]),
            "bq_c": np.ascontiguousarray(bq[sl].reshape(OCW, 1)),
            "bk_c": np.ascontiguousarray(bk[sl].reshape(OCW, 1)),
            "bv_c": np.ascontiguousarray(bv[sl].reshape(1, OCW)),
            "bo_c": bo,
            "wblob": np.ascontiguousarray(blob[c:c + 1]),
        })
    return in_maps


def kernel(L0_attn_hidden_states, L0_attn_Wq, L0_attn_bq, L0_attn_Wk, L0_attn_bk,
           L0_attn_Wv, L0_attn_bv, L0_attn_Wo, L0_attn_bo,
           L0_ffn_w_gate_up, L0_ffn_w_down):
    key_cfg = dict(S=8192, D=2048, NH=16, INTER=8192)
    cfg = make_cfg(**key_cfg)
    nc, _ = _get_prog(key_cfg)
    in_maps = stage_inputs(cfg, L0_attn_hidden_states, L0_attn_Wq, L0_attn_bq,
                           L0_attn_Wk, L0_attn_bk, L0_attn_Wv, L0_attn_bv,
                           L0_attn_Wo, L0_attn_bo, L0_ffn_w_gate_up, L0_ffn_w_down)
    res = run_bass_kernel_spmd(nc, in_maps, core_ids=list(range(NCORE)))
    out = np.concatenate([res.results[c]["out_own"] for c in range(NCORE)], axis=0)
    return out.reshape(1, cfg["S"], cfg["D"]).astype(np.float32)


# version 23
# speedup vs baseline: 1.0576x; 1.0178x over previous
import numpy as np
import concourse.bass as bass
import concourse.mybir as mybir
from concourse.bass_utils import run_bass_kernel_spmd
from concourse.aot_env import allow_ici_all_to_all

F32 = mybir.dt.float32
BF16 = mybir.dt.bfloat16
AF = mybir.ActivationFunctionType
ALU = mybir.AluOpType
AX = mybir.AxisListType

NCORE = 8
HD = 128
EPS = 1e-5
ALPHA = 1.0 / float(np.sqrt(HD))


def make_cfg(S=8192, D=2048, NH=16, INTER=8192):
    c = dict(S=S, D=D, NH=NH, INTER=INTER)
    c["HPC"] = NH // NCORE
    c["OCW"] = c["HPC"] * HD
    c["TOK"] = S // NCORE
    c["NTT"] = c["TOK"] // 128
    c["TB"] = min(512 if D <= 1024 else 256, c["TOK"])
    c["NB"] = S // c["TB"]
    c["NST"] = S // 512
    c["MC"] = D // 128
    c["NOC"] = D // 512
    c["SEG"] = 512
    c["NSEG"] = INTER // c["SEG"]
    c["TS"] = min(512, c["TOK"])
    c["NTS"] = c["TOK"] // c["TS"]
    c["OFF_WO"] = 0
    c["OFF_GU"] = D * D
    c["OFF_WD"] = D * D + 2 * INTER * D
    c["BLOB"] = D * D + 2 * INTER * D + INTER * D
    assert c["BLOB"] % NCORE == 0
    c["BW"] = c["BLOB"] // NCORE
    assert D == NH * HD
    return c


class _Sem:
    __slots__ = ("h", "n")

    def __init__(self, h):
        self.h = h
        self.n = 0


def build_program(cfg, debug=False):
    S, D, NH, INTER = cfg["S"], cfg["D"], cfg["NH"], cfg["INTER"]
    HPC, OCW, TOK, NTT = cfg["HPC"], cfg["OCW"], cfg["TOK"], cfg["NTT"]
    TB, NB, NST, MC, NOC = cfg["TB"], cfg["NB"], cfg["NST"], cfg["MC"], cfg["NOC"]
    SEG, NSEG, TS, NTS = cfg["SEG"], cfg["NSEG"], cfg["TS"], cfg["NTS"]
    BW = cfg["BW"]
    NVT = S // 128
    VTPB = TB // 128
    ICS = SEG // 128

    nc = bass.Bass("TRN2", target_bir_lowering=False, debug=False)

    x_own = nc.dram_tensor("x_own", [TOK, D], F32, kind="ExternalInput")
    x_full = nc.dram_tensor("x_full", [S, D], F32, kind="ExternalInput")
    wq_t = nc.dram_tensor("wq_t", [D, OCW], F32, kind="ExternalInput")
    wk_t = nc.dram_tensor("wk_t", [D, OCW], F32, kind="ExternalInput")
    wv_t = nc.dram_tensor("wv_t", [D, OCW], F32, kind="ExternalInput")
    bq_c = nc.dram_tensor("bq_c", [OCW, 1], F32, kind="ExternalInput")
    bk_c = nc.dram_tensor("bk_c", [OCW, 1], F32, kind="ExternalInput")
    bv_c = nc.dram_tensor("bv_c", [1, OCW], F32, kind="ExternalInput")
    bo_c = nc.dram_tensor("bo_c", [1, D], F32, kind="ExternalInput")
    wblob = nc.dram_tensor("wblob", [1, BW], F32, kind="ExternalInput")
    out_own = nc.dram_tensor("out_own", [TOK, D], F32, kind="ExternalOutput")

    xn_own_dram = nc.dram_tensor("xn_own_dram", [D, TOK], BF16)
    ag_w_in = nc.dram_tensor("ag_w_in", [1, BW], BF16)
    ag_w_out = nc.dram_tensor("ag_w_out", [NCORE, BW], BF16, addr_space="Shared")
    PH = 2 if TOK >= 1024 else 1
    a2a_in_p = [nc.dram_tensor("a2a_in%d" % p, [NCORE * OCW, TOK // PH], BF16)
                for p in range(PH)]
    a2a_out_p = [nc.dram_tensor("a2a_out%d" % p, [NCORE * OCW, TOK // PH], BF16)
                 for p in range(PH)]

    dbg = {}
    if debug:
        dbg["xn"] = nc.dram_tensor("dbg_xn", [D, TOK], BF16, kind="ExternalOutput")
        dbg["q"] = nc.dram_tensor("dbg_q", [OCW, S], BF16, kind="ExternalOutput")
        dbg["k"] = nc.dram_tensor("dbg_k", [OCW, S], BF16, kind="ExternalOutput")
        dbg["v"] = nc.dram_tensor("dbg_v", [S, OCW], BF16, kind="ExternalOutput")
        dbg["bc"] = nc.dram_tensor("dbg_bc", [128, HPC], F32, kind="ExternalOutput")

    from contextlib import ExitStack
    stack = ExitStack()
    en = stack.enter_context

    K_ = 1024
    AR_BYTES = 192 * K_
    arena = en(nc.sbuf_tensor("arena", [128, AR_BYTES // 4], F32))
    arf = arena[:, :]
    arb = arf.bitcast(BF16)

    def bfv(off, cols, parts=128):
        assert off % 32 == 0 and off + 2 * cols <= AR_BYTES, (off, cols)
        return arb[0:parts, off // 2: off // 2 + cols]

    def f32v(off, cols, parts=128):
        assert off % 32 == 0 and off + 4 * cols <= AR_BYTES, (off, cols)
        return arf[0:parts, off // 4: off // 4 + cols]

    o = 0
    qw_sb = bfv(o, MC * OCW); o += 2 * MC * OCW
    kw_sb = bfv(o, MC * OCW); o += 2 * MC * OCW
    vw_sb = bfv(o, MC * OCW); o += 2 * MC * OCW
    _qkv_end = o
    Q_sb = bfv(o, HPC * S); o += 2 * HPC * S
    K_sb = bfv(o, HPC * S); o += 2 * HPC * S
    V_sb = bfv(o, NVT * OCW); o += 2 * NVT * OCW
    _xns = o
    xn_blk = [bfv(_xns + i * 2 * MC * TB, MC * TB) for i in range(2)]
    o += 2 * 2 * MC * TB
    _attn_hi = o
    TAIL0 = AR_BYTES - 24 * K_
    assert _attn_hi <= TAIL0
    p_buf = [bfv(TAIL0 + i * K_, 512) for i in range(4)]
    masks = [bfv(TAIL0 + 4 * K_ + i * K_, 512) for i in range(4)]
    ho_buf = [bfv(TAIL0 + 8 * K_ + i * K_, 512) for i in range(2)]
    r_sb2 = [f32v(TAIL0 + 10 * K_ + i * 2 * K_, 512) for i in range(2)]

    NXT = 2
    NXB = 2 * (TB // 128)
    xt = [f32v(_attn_hi + i * 4 * D, D) for i in range(NXT)]
    xnb = [bfv(TAIL0 + 4 * K_ + i * 2 * D, D) for i in range(NXB)]
    assert _attn_hi + NXT * 4 * D <= TAIL0
    assert 4 * K_ + NXB * 2 * D <= 24 * K_

    accum = f32v(0, NTT * D)
    _A = 4 * NTT * D
    ho_own = bfv(_A, MC * TOK)
    xn_own = bfv(_A, MC * TOK)
    _B = _A + 2 * MC * TOK
    gu_buf = [bfv(_B, MC * 2 * SEG),
              bfv(AR_BYTES - 4 * MC * SEG, MC * 2 * SEG)]
    _C = _B + 2 * MC * K_
    wo_buf = [bfv(_xns + i * MC * K_, MC * 512) for i in range(2)]
    _wo_end = _xns + 2 * MC * K_
    _x_off = _B if _B + 8 * D <= _xns else _wo_end
    x_t4 = [f32v(_x_off + i * 4 * D, D) for i in range(2)]
    bo_f = f32v(0, D, parts=1)
    wd_buf = bfv(_C, ICS * D)
    _E = _C + 8 * D
    h_sb = bfv(_E, ICS * TOK)
    _F = max(_E + 2 * ICS * TOK, _wo_end,
             _x_off + 8 * D if _x_off >= _wo_end else 0)
    bo_hi = bfv(_F, D, parts=1)
    bo_lo = bfv(_F + 2 * D, D, parts=1)
    assert _F + 4 * D <= AR_BYTES - 4 * MC * SEG, ("P4 map overflow", _F, AR_BYTES)

    gs_sb = en(nc.sbuf_tensor("gs_sb", [128, TS], BF16))[:, :]
    consts = en(nc.sbuf_tensor("consts", [128, 1280], BF16))
    ident = consts[:, 0:128]
    ones_col = consts[:, 128:129]
    ones_row = consts[0:1, 160:288]
    ones_mat = consts[:, 1024:1152]
    bv_hi = consts[0:1, 288:288 + OCW]
    bv_lo = consts[0:1, 544:544 + OCW]
    fsm = en(nc.sbuf_tensor("fsm", [128, 3072], F32))
    ones_f32 = fsm[0:1, 0:128]
    stats = fsm[0:1, 1536:1664]
    sc1 = fsm[0:1, 1728:1736]
    bq_sb = fsm[:, 1744:1744 + HPC]
    bk_sb = fsm[:, 1752:1752 + HPC]
    biasC = fsm[:, 1760:1760 + HPC]
    mst = fsm[:, 1776:1792]
    bv_f = fsm[0:1, 1792:1792 + OCW]
    bv_t1 = fsm[0:1, 384:384 + OCW]
    bnst = [fsm[:, 640 + 24 * s_:640 + 24 * s_ + 24] for s_ in range(4)]
    mvs = fsm[:, 736:752]
    veps = [fsm[:, 752 + 2 * q_:754 + 2 * q_] for q_ in range(4)]
    lnR2 = [fsm[:, 760 + 2 * q_:762 + 2 * q_] for q_ in range(4)]
    lnM2 = [fsm[:, 768 + 2 * q_:770 + 2 * q_] for q_ in range(4)]
    epsc = fsm[:, 776:777]
    identF = fsm[:, 2304:2432]
    bo_t1 = fsm[0:1, 2432:2432 + 512]

    psA = en(nc.psum_tensor("psA", [128, 1024], F32))
    psB = en(nc.psum_tensor("psB", [128, 1024], F32))
    psm = [en(nc.psum_tensor("psm%d" % i, [128, 512], F32)) for i in range(4)]

    NCH = 6
    dch = {"y": [_Sem(en(nc.semaphore("sY%d" % i))) for i in range(NCH)],
           "g": [_Sem(en(nc.semaphore("sG%d" % i))) for i in range(NCH)]}
    _rr = {"y": 0, "g": 0}
    tE = _Sem(en(nc.semaphore("tE")))
    vE = _Sem(en(nc.semaphore("vE")))
    scE = _Sem(en(nc.semaphore("scE")))
    gE = _Sem(en(nc.semaphore("gE")))
    ccs = [_Sem(en(nc.semaphore("cc%d" % i))) for i in range(3)]

    ops = {k: [] for k in "tvsyg"}
    _wmax = {}

    def em(e, f):
        ops[e].append(f)

    def wait(e, tok):
        if not tok:
            return
        sem, val = tok
        if val <= 0:
            return
        key = (e, id(sem))
        if _wmax.get(key, 0) >= val:
            return
        _wmax[key] = val
        em(e, lambda eng, _s=sem.h, _t=val: eng.wait_ge(_s, _t))

    def tmax(*toks):
        best = 0
        for t in toks:
            if t and (not best or t[1] > best[1]):
                if best:
                    assert t[0] is best[0]
                best = t
        return best

    def dma(e, out, in_, **kw):
        ch = _rr[e] % NCH
        _rr[e] += 1
        sem = dch[e][ch]
        wait(e, (sem, sem.n))
        sem.n += 16
        em(e, lambda eng, _o=out, _i=in_, _s=sem.h, _k=dict(kw):
           eng.dma_start(out=_o, in_=_i, **_k).then_inc(_s, 16))
        return (sem, sem.n)

    def run(e, sem, meth, *a, **kw):
        sem.n += 1
        em(e, lambda eng, _m=meth, _a=a, _k=dict(kw), _s=sem.h:
           getattr(eng, _m)(*_a, **_k).then_inc(_s, 1))
        return (sem, sem.n)

    def run0(e, meth, *a, **kw):
        em(e, lambda eng, _m=meth, _a=a, _k=dict(kw): getattr(eng, _m)(*_a, **_k))

    def mm(out, lhsT, rhs, start, stop):
        run0("t", "matmul", out, lhsT, rhs, start=start, stop=stop,
             skip_group_check=True)

    def mm_i(out, lhsT, rhs, start, stop):
        return run("t", tE, "matmul", out, lhsT, rhs, start=start, stop=stop,
                   skip_group_check=True)


    def fence(wait_engine, q):
        for sem in dch[q]:
            wait(wait_engine, (sem, sem.n))

    qkv_toks = []
    for (w_in, w_sb) in ((wq_t, qw_sb), (wk_t, kw_sb), (wv_t, vw_sb)):
        qkv_toks.append(dma("g", w_sb.rearrange("p (c o) -> p c o", c=MC),
                            w_in.ap().rearrange("(c p) o -> p c o", p=128)))
    b_toks = [dma("y", bq_sb[:, :], bq_c.ap().rearrange("(c p) one -> p (c one)", p=128),
                  allow_slow_non_contiguous=True),
              dma("y", bk_sb[:, :], bk_c.ap().rearrange("(c p) one -> p (c one)", p=128),
                  allow_slow_non_contiguous=True),
              dma("y", bv_f[:, :], bv_c[:, :])]

    run("g", gE, "memset", ident, 1.0)
    run("g", gE, "memset", identF, 1.0)
    run("g", gE, "memset", consts[:, 128:160], 1.0)
    run("g", gE, "memset", consts[0:1, 160:288], 1.0)
    run("g", gE, "memset", ones_mat, 1.0)
    run("g", gE, "memset", fsm[0:1, 0:128], 1.0)
    run("g", gE, "memset", epsc, float(EPS))
    wait("g", (gE, gE.n))
    run("g", gE, "affine_select", out=ident, in_=ident, compare_op=ALU.is_equal,
        fill=0.0, base=0, pattern=[[-1, 128]], channel_multiplier=1)
    run("g", gE, "affine_select", out=identF, in_=identF, compare_op=ALU.is_equal,
        fill=0.0, base=0, pattern=[[-1, 128]], channel_multiplier=1)
    g_const = (gE, gE.n)

    for t in b_toks:
        wait("v", t)
    run0("v", "tensor_copy", bv_hi, bv_f)
    run0("v", "drain")
    run0("v", "tensor_copy", bv_t1, bv_hi)
    run0("v", "drain")
    run0("v", "tensor_sub", bv_t1, bv_f, bv_t1)
    run0("v", "drain")
    run0("v", "tensor_copy", bv_lo, bv_t1)
    bv_split = run("v", vE, "drain")

    NTB = TB // 128
    NG = D // 512
    NGR = (MC + 7) // 8
    assert NG <= 4 and MC % 8 == 0
    psAb = psA[:, :].bitcast(BF16)
    psBb = psB[:, :].bitcast(BF16)
    grans = [psAb[:, 1024:2048], psBb[:, 1024:2048]]
    gran_ev = [0, 0]
    pr_nrm = {}
    pr_agg = {}
    pr_pair = {}
    pr_phB_rd = {0: 0, 1: 0, 2: 0, 3: 0}
    ship_war = {0: [], 1: []}
    xnb_free = {0: 0, 1: 0}
    xnb_war = {}
    blk_ready = {}
    _gt = [0]
    _pq = [0]
    _gu = [0]
    _pend = []
    _xd_tok = {}

    _tinfo = {}
    _sd_rd = {0: 0, 1: 0, 2: 0, 3: 0}

    pair_q = []

    def _flush_pair():
        if not pair_q:
            return
        pend = pair_q.pop(0)
        w = len(pend)
        q = _pq[0] % 4
        _pq[0] += 1
        s0 = pend[0] % 4
        assert w == 1 or s0 != 3
        mv2 = mvs[:, 2 * s0:2 * s0 + 2 * w].rearrange("p (s two) -> p two s", two=2)
        wait("s", pr_agg[pend[-1]])
        wait("s", _sd_rd[q])
        wait("s", g_const)
        st = run("s", scE, "activation", veps[q][:, 0:w], mv2[:, 1, :],
                 AF.Sqrt, bias=epsc)
        wait("v", st)
        wait("v", pr_phB_rd[q])
        rt = run("v", vE, "reciprocal", lnR2[q][:, 0:w], veps[q][:, 0:w])
        _sd_rd[q] = rt
        wait("v", rt)
        ptok = run("v", vE, "scalar_tensor_tensor", lnM2[q][:, 0:w],
                   mv2[:, 0, :], -1.0, lnR2[q][:, 0:w], ALU.mult, ALU.mult)
        pr_pair[pend[0] // 2] = ptok
        nrm_last = 0
        for j, t in enumerate(pend):
            slot = t % NXT
            xb = t % NXB
            assert t - NXB < 0 or (t - NXB) in _st2b_done, t
            wait("s", ptok)
            wait("s", _xd_tok.pop(t))
            wait("s", xnb_war.get(xb, 0))
            nrm_last = run("s", scE, "activation", xnb[xb], xt[slot], AF.Identity,
                           scale=lnR2[q][:, j:j + 1], bias=lnM2[q][:, j:j + 1])
            pr_nrm[t] = nrm_last
        pr_phB_rd[q] = nrm_last

    def stage1_tile(desc):
        src_rows, par, toff, bkey, pair_end = desc
        t = _gt[0]
        _gt[0] += 1
        dq = ("y", "g")[t % 2]
        slot, s4 = t % NXT, t % 4
        assert t - NXT < 0 or (t - NXT) in pr_nrm, t
        wait(dq, pr_nrm.get(t - NXT, 0))
        wait(dq, pr_agg.get(t - NXT, 0))
        xd = dma(dq, xt[slot], src_rows)
        _xd_tok[t] = xd
        _tinfo[t] = (par, toff, bkey)
        wait("v", xd)
        wait("v", pr_agg.get(t - 4, 0))
        wait("v", pr_pair.get((t - 4) // 2, 0))
        for g in range(NG):
            run("v", vE, "bn_stats", bnst[s4][:, 6 * g:6 * g + 6],
                xt[slot][:, 512 * g:512 * (g + 1)])
        wait("v", (vE, vE.n))
        pr_agg[t] = run("v", vE, "bn_aggr", mvs[:, 2 * s4:2 * s4 + 2],
                        bnst[s4][:, 0:6 * NG])
        _pend.append(t)
        if pair_end:
            pair_q.append(list(_pend))
            _pend.clear()

    _st2b_done = set()

    def stage2b_tile(t):
        _st2b_done.add(t)
        par, toff, bkey = _tinfo.pop(t)
        xb = t % NXB
        wait("t", g_const)
        wait("t", pr_nrm[t])
        for gr in range(NGR):
            cw = min(8, MC - 8 * gr)
            g2 = _gu[0] % 2
            _gu[0] += 1
            wait("t", gran_ev[g2])
            tt2 = 0
            for c8 in range(cw):
                c = 8 * gr + c8
                tt2 = run("t", tE, "transpose",
                          grans[g2][:, 128 * c8:128 * (c8 + 1)],
                          xnb[xb][:, 128 * c:128 * (c + 1)], ident)
            xnb_war[xb] = tt2
            wait("v", tt2)
            wait("v", xnb_free[par])
            for stok in ship_war[par]:
                wait("v", stok)
            dstv = xn_blk[par].rearrange("p (c t) -> p c t", c=MC)[
                :, 8 * gr:8 * gr + cw, toff:toff + 128]
            srcv = grans[g2].rearrange("p (c t) -> p c t", c=8)[:, 0:cw, :]
            gran_ev[g2] = run("v", vE, "tensor_copy", dstv, srcv)
        blk_ready[bkey] = gran_ev[(_gu[0] - 1) % 2]
        if isinstance(bkey, tuple) and toff == (NTB - 1) * 128:
            ob = bkey[1]
            wait("y", blk_ready[bkey])
            shv = xn_blk[par].rearrange("p (c t) -> p c t", c=MC)
            toks = [dma("y", xn_own_dram.ap()[:, ob * TB:(ob + 1) * TB]
                        .rearrange("(c p) t -> p c t", p=128), shv)]
            if debug:
                toks.append(dma("y", dbg["xn"].ap()[:, ob * TB:(ob + 1) * TB]
                                .rearrange("(c p) t -> p c t", p=128), shv))
            ship_war[par] = toks

    q1 = []
    q2a = []
    q2b = []

    def slot():
        if q2b:
            stage2b_tile(q2b.pop(0))
        if q2a:
            t_ = q2a.pop(0)
            if t_ not in pr_nrm:
                _flush_pair()
            q2b.append(t_)
        if q1:
            q2a.append(_gt[0])
            stage1_tile(q1.pop(0))

    def enqueue_block(gb, src_t, row0, bkey):
        par = gb % 2
        for j in range(NTB):
            pair_end = (j % 2 == 1) or (j == NTB - 1)
            q1.append((src_t.ap()[row0 + 128 * j:row0 + 128 * (j + 1), :],
                       par, 128 * j, bkey, pair_end))

    NOB = TOK // TB
    for ob in range(NOB):
        enqueue_block(ob, x_own, ob * TB, ("own", ob))
    for b in range(NB):
        enqueue_block(NOB + b, x_full, b * TB, b)
    for _ in range(NXT):
        if q1:
            q2a.append(_gt[0])
            stage1_tile(q1.pop(0))
    for _ in range(NTB):
        if q2a:
            t_ = q2a.pop(0)
            if t_ not in pr_nrm:
                _flush_pair()
            q2b.append(t_)
        if q1:
            q2a.append(_gt[0])
            stage1_tile(q1.pop(0))
    while len(_st2b_done) < (NOB + 1) * NTB:
        slot()
    qk_ev = {0: 0, 1: 0}
    v_ev = {0: 0, 1: 0}
    last_q_ev = 0
    last_v_ev = 0
    sq_free = {0: 0, 1: 0, 2: 0, 3: 0}
    nrm_free = {0: 0, 1: 0}
    nsq = [0]
    npp = [0]
    pending_norm = []

    def flush_norms():
        while pending_norm:
            sqv, idx, sqt = pending_norm.pop(0)
            pp2 = npp[0] % 2
            npp[0] += 1
            wait("t", sqt)
            wait("t", nrm_free[pp2])
            psN = (psA, psB)[pp2][0:1, 0:512]
            mt2 = mm_i(psN[:, 0:TB] if TB < 512 else psN, ones_col, sqv,
                       start=True, stop=True)
            sq_free[nsq[0] % 4] = mt2
            nsq[0] += 1
            wait("v", mt2)
            nrm_free[pp2] = run("v", vE, "reduce_max", stats[0:1, idx:idx + 1],
                                (psA, psB)[pp2][0:1, 0:TB], axis=AX.X)
    def emit_qkv(b):
        nonlocal last_q_ev, last_v_ev
        bi = b
        par = (NOB + b) % 2
        st2c = [NTB]

        def st2():
            if st2c[0] > 0 and (q2b or q2a or q1):
                st2c[0] -= 1
                slot()
        flush_norms()
        assert len(_st2b_done) >= (NOB + b + 1) * NTB, (b, len(_st2b_done))
        wait("t", blk_ready[b])
        if bi < 2:
            for t in qkv_toks:
                wait("t", t)
            wait("t", g_const)
            for t in b_toks:
                wait("t", t)
        xv = xn_blk[par].rearrange("p (c t) -> p c t", c=MC)
        for which, (wsb, bcol, dst) in enumerate(((qw_sb, bq_sb, Q_sb),
                                                  (kw_sb, bk_sb, K_sb))):
            wv = wsb.rearrange("p (c o) -> p c o", c=MC)
            for h in range(HPC):
                pp = (bi * 2 * HPC + which * HPC + h) % 2
                ps = psm[pp]
                wait("t", qk_ev[pp])
                mt = 0
                for m in range(MC):
                    if m < MC - 1:
                        mm(ps[:, 0:TB], wv[:, m, 128 * h:128 * (h + 1)], xv[:, m, :],
                           start=(m == 0), stop=False)
                    else:
                        mt = mm_i(ps[:, 0:TB], wv[:, m, 128 * h:128 * (h + 1)],
                                  xv[:, m, :], start=(m == 0), stop=True)
                wait("s", mt)
                qk_ev[pp] = run("s", scE, "activation",
                                dst[:, S * h + TB * b:S * h + TB * (b + 1)],
                                ps[:, 0:TB], AF.Identity, bias=bcol[:, h:h + 1])
                last_q_ev = qk_ev[pp]
                sqp = (nsq[0] + len(pending_norm)) % 4
                wait("s", qk_ev[pp])
                wait("s", sq_free[sqp])
                sqt = run("s", scE, "activation", p_buf[sqp][:, 0:TB],
                          dst[:, S * h + TB * b:S * h + TB * (b + 1)], AF.Square)
                pending_norm.append((p_buf[sqp][:, 0:TB],
                                     (2 * h + which) * NB + b, sqt))
                st2()
        wv = vw_sb.rearrange("p (c o) -> p c o", c=MC)
        npair = (VTPB + 1) // 2
        for pair in range(npair):
            pp = 2 + (bi * npair + pair) % 2
            ps = psm[pp]
            wait("t", v_ev[pp - 2])
            wait("t", bv_split)
            nj = min(2, VTPB - 2 * pair)
            mt = 0
            for j in range(nj):
                t4 = 2 * pair + j
                for m in range(MC):
                    mm(ps[:, OCW * j:OCW * (j + 1)], xv[:, m, 128 * t4:128 * (t4 + 1)],
                       wv[:, m, :], start=(m == 0), stop=False)
                mm(ps[:, OCW * j:OCW * (j + 1)], ones_row, bv_hi, start=False, stop=False)
                mt = mm_i(ps[:, OCW * j:OCW * (j + 1)], ones_row, bv_lo,
                          start=False, stop=True)
            wait("v", mt)
            base = OCW * (VTPB * b + 2 * pair)
            v_ev[pp - 2] = run("v", vE, "tensor_copy",
                               V_sb[:, base:base + nj * OCW], ps[:, 0:nj * OCW])
            last_v_ev = v_ev[pp - 2]
            st2()
        while st2c[0] > 0 and (q2b or q2a or q1):
            st2()
        xnb_free[par] = (tE, tE.n)

    for b in range(NB):
        emit_qkv(b)
    assert not q1 and not q2a and not q2b and not _pend and not pair_q

    t_blob = dma("g", ag_w_in[:, :], wblob[:, :])
    wait("g", (tE, tE.n))
    for j in range(4):
        run("g", gE, "memset", masks[j], 1.0)
    wait("g", (gE, gE.n))
    for j in range(4):
        run("g", gE, "affine_select", out=masks[j], in_=masks[j],
            compare_op=ALU.is_ge, fill=0.0, base=-128 * j,
            pattern=[[1, 512]], channel_multiplier=-1)
    g_masks = (gE, gE.n)
    wait("g", t_blob)
    _wag_hole = [None]
    em("g", lambda g: g.wait_ge(*_wag_hole[0]))
    ccs[1].n = 1
    cc_w = (ccs[1], 1)
    em("g", lambda g: g.collective_compute(
        "AllGather", ALU.bypass, replica_groups=[list(range(NCORE))],
        ins=[ag_w_in.ap().opt()], outs=[ag_w_out.ap().opt()]).then_inc(ccs[1].h, 1))

    p2_t_end = (tE, tE.n)
    blobv = ag_w_out.ap().rearrange("r w -> (r w)")

    def blob_slice(off, rows, cols):
        return blobv[off:off + rows * cols].rearrange("(c p o) -> p c o", p=128, o=cols)

    OFF_WO, OFF_GU, OFF_WD = cfg["OFF_WO"], cfg["OFF_GU"], cfg["OFF_WD"]
    wait("g", cc_w)
    wait("g", p2_t_end)
    wo_tok = {0: dma("g", wo_buf[0].rearrange("p (c o) -> p c o", c=MC),
                     blob_slice(OFF_WO, D, D)[:, :, 0:512])}
    if NOC > 1:
        wo_tok[1] = dma("g", wo_buf[1].rearrange("p (c o) -> p c o", c=MC),
                        blob_slice(OFF_WO, D, D)[:, :, 512:1024])

    if debug:
        wait("y", last_q_ev)
        wait("y", last_v_ev)
        dma("y", dbg["q"].ap().rearrange("(h p) t -> p h t", p=128),
            Q_sb.rearrange("p (h t) -> p h t", h=HPC))
        dma("y", dbg["k"].ap().rearrange("(h p) t -> p h t", p=128),
            K_sb.rearrange("p (h t) -> p h t", h=HPC))
        dma("y", dbg["v"].ap().rearrange("(n p) o -> p n o", p=128),
            V_sb.rearrange("p (n o) -> p n o", n=NVT))

    flush_norms()
    wait("v", tmax(nrm_free[0], nrm_free[1]))
    run0("v", "drain")
    biasC_tok = 0
    for h in range(HPC):
        run0("v", "reduce_max", sc1[0:1, 0:1],
             stats[0:1, 2 * h * NB:2 * h * NB + NB], axis=AX.X)
        run0("v", "drain")
        run0("v", "reduce_max", sc1[0:1, 1:2],
             stats[0:1, (2 * h + 1) * NB:(2 * h + 1) * NB + NB], axis=AX.X)
        run0("v", "drain")
        vt = run("v", vE, "tensor_tensor", sc1[0:1, 2:3], sc1[0:1, 0:1],
                 sc1[0:1, 1:2], ALU.mult)
        wait("s", vt)
        st = run("s", scE, "activation", sc1[0:1, 3:4], sc1[0:1, 2:3], AF.Sqrt)
        wait("v", st)
        vt = run("v", vE, "tensor_scalar_mul", sc1[0:1, 4:5], sc1[0:1, 3:4], -ALPHA)
        wait("t", vt)
        wait("t", last_v_ev)
        mt = mm_i(psm[2][:, 0:1], ones_f32, sc1[0:1, 4:5], start=True, stop=True)
        wait("v", mt)
        biasC_tok = run("v", vE, "tensor_copy", biasC[:, h:h + 1], psm[2][:, 0:1])
        run0("v", "drain")
    p2norm_t = (tE, tE.n)
    if debug:
        wait("y", biasC_tok)
        dma("y", dbg["bc"].ap(), biasC)

    wait("y", p2_t_end)
    bo_tok0 = dma("y", bo_f[:, :], bo_c[:, :])
    wait("v", bo_tok0)
    wait("v", pr_nrm[max(pr_nrm)])
    wait("v", (vE, vE.n))
    run0("v", "tensor_copy", bo_hi, bo_f)
    run0("v", "drain")
    for q in range(D // 512):
        run0("v", "tensor_copy", bo_t1, bo_hi[0:1, 512 * q:512 * (q + 1)])
        run0("v", "drain")
        run0("v", "tensor_sub", bo_t1, bo_f[0:1, 512 * q:512 * (q + 1)], bo_t1)
        run0("v", "drain")
        run0("v", "tensor_copy", bo_lo[0:1, 512 * q:512 * (q + 1)], bo_t1)
        run0("v", "drain")
    bo_split = run("v", vE, "drain")

    scq = [psA[:, 0:512], psA[:, 512:1024], psB[:, 0:512], psB[:, 512:1024]]
    exp_tok = {q: 0 for q in range(4)}
    pbuf_cons = {q: p2norm_t for q in range(4)}
    sc_cons = {q: nrm_free[q // 2] for q in range(4)}
    sc_cons[1] = tmax(sc_cons[1], gran_ev[0])
    sc_cons[3] = tmax(sc_cons[3], gran_ev[1])
    av_free = {0: last_v_ev, 1: last_v_ev}
    sum_free = {0: biasC_tok, 1: biasC_tok}
    ho_free = {0: [], 1: []}
    wait("s", biasC_tok)
    wait("t", last_q_ev)
    wait("t", last_v_ev)
    a2a_ship = {p: [] for p in range(PH)}
    TOKC = min(TOK // PH, 512) if PH == 2 else min(TOK, 512)
    LA = 3
    TLA = 4

    if PH == 2:
        strip_order = [i for i in range(NST - 2, -1, -2)] + \
                      [i for i in range(NST - 1, 0, -2)]
    else:
        strip_order = list(range(NST))
    jobs = []
    for i in strip_order:
        nt = 4 * (i + 1)
        for h in range(HPC):
            for kt in range(nt):
                jobs.append((i, kt, h))
    njobs = len(jobs)
    avt_h = {}
    tails = {}

    def emit_sc(g):
        i, kt, h = jobs[g]
        q4 = g % 4
        qs = Q_sb[:, S * h + 512 * i:S * h + 512 * (i + 1)]
        wait("t", exp_tok[q4])
        wait("t", sc_cons[q4])
        mt = mm_i(scq[q4], K_sb[:, S * h + 128 * kt:S * h + 128 * (kt + 1)],
                  qs, start=True, stop=True)
        if g % 2 == 0:
            return
        qe, qo = (g - 1) % 4, q4
        pj = bfv(TAIL0 + ((g - 1) % 4) * K_, 1024)
        psrc = (psA, psB)[(g % 4) // 2][:, 0:1024]
        wait("s", mt)
        wait("s", pbuf_cons[qe])
        wait("s", pbuf_cons[qo])
        et = run("s", scE, "activation", pj, psrc, AF.Exp,
                 bias=biasC[:, h:h + 1], scale=ALPHA)
        exp_tok[qe] = exp_tok[qo] = et
        sc_cons[qe] = sc_cons[qo] = 0
        k0 = jobs[g - 1][1]
        if kt >= 4 * i or k0 >= 4 * i:
            wait("v", g_masks)
            wait("v", et)
            for (qq, kk) in ((qe, k0), (qo, kt)):
                if kk >= 4 * i:
                    run0("v", "tensor_tensor", p_buf[qq], p_buf[qq],
                         masks[kk - 4 * i], ALU.mult)
            gt = run("v", vE, "drain")
            exp_tok[qe] = exp_tok[qo] = gt

    def emit_av(g):
        i, kt, h = jobs[g]
        q4 = g % 4
        nt = 4 * (i + 1)
        wait("t", exp_tok[q4])
        if kt == 0:
            wait("t", av_free[h])
            wait("t", sum_free[h])
        mm(psm[h][:, 0:512],
           V_sb[:, OCW * kt + 128 * h:OCW * kt + 128 * (h + 1)],
           p_buf[q4], start=(kt == 0), stop=(kt == nt - 1))
        avt = mm_i(psm[2 + h][:, 0:512], ones_mat, p_buf[q4],
                   start=(kt == 0), stop=(kt == nt - 1))
        pbuf_cons[q4] = avt
        if kt == nt - 1:
            avt_h[(i, h)] = avt
            tails[g + TLA] = (i, h)

    def emit_tail(i, h):
        hb = (i * HPC + h) % 2
        wait("v", avt_h[(i, h)])
        for t in ho_free[hb]:
            wait("v", t)
        vtr = run("v", vE, "reciprocal", r_sb2[hb], psm[2 + h][:, 0:512])
        sum_free[h] = vtr
        run0("v", "drain")
        vt = run("v", vE, "tensor_tensor", ho_buf[hb], psm[h][:, 0:512],
                 r_sb2[hb], ALU.mult)
        av_free[h] = vt
        wait("y", vt)
        ho_free[hb] = []
        for u in range(512 // TOKC):
            g_t = 512 * i + u * TOKC
            rows0 = OCW * (g_t // TOK) + 128 * h
            hp = (g_t % TOK) // (TOK // PH)
            col0 = g_t % (TOK // PH)
            tok = dma("y", a2a_in_p[hp].ap()[rows0:rows0 + 128, col0:col0 + TOKC],
                      ho_buf[hb][:, u * TOKC:(u + 1) * TOKC])
            ho_free[hb].append(tok)
            a2a_ship[hp].append(tok)

    wag_gate = [None]
    xfree_gate = [None]
    XSAFE = 4
    _remaining = set(strip_order)
    _done_ct = {}
    for i in strip_order:
        _done_ct[i] = 4 * (i + 1) * HPC
    _per_strip_left = dict(_done_ct)
    for g in range(njobs + TLA + 1):
        if g in tails:
            i_, h_ = tails.pop(g)
            emit_tail(i_, h_)
        if g < njobs:
            emit_sc(g)
        if g - LA >= 0 and g - LA < njobs:
            emit_av(g - LA)
            i_av = jobs[g - LA][0]
            _per_strip_left[i_av] -= 1
            if _per_strip_left[i_av] == 0:
                _remaining.discard(i_av)
                if xfree_gate[0] is None and all(i2 < XSAFE for i2 in _remaining):
                    xfree_gate[0] = (tE, tE.n)
        if g == njobs // 3:
            wag_gate[0] = (tE, tE.n)
    p3_t_end = (tE, tE.n)
    if xfree_gate[0] is None:
        xfree_gate[0] = p3_t_end
    _wag_hole[0] = (wag_gate[0][0].h, wag_gate[0][1])

    cc_a2a_p = []
    for p in range(PH):
        for t in a2a_ship[p]:
            wait("g", t)
        ccs[2].n += 1
        cc_a2a_p.append((ccs[2], ccs[2].n))

        def _emit_a2a(g, _p=p):
            with allow_ici_all_to_all():
                g.collective_compute(
                    "AllToAll", ALU.bypass, replica_groups=[list(range(NCORE))],
                    ins=[a2a_in_p[_p].ap().opt()],
                    outs=[a2a_out_p[_p].ap().opt()]).then_inc(ccs[2].h, 1)
        em("g", _emit_a2a)
    cc_a2a = cc_a2a_p[-1]

    wait("g", p3_t_end)
    gu_tok = {0: dma("g", gu_buf[1].rearrange("p (c f) -> p c f", c=MC),
                     blob_slice(OFF_GU, D, 2 * SEG))}
    gu_home = {0: 1}
    wait("y", p3_t_end)
    ho_own_toks = []
    for p in range(PH):
        wait("y", cc_a2a_p[p])
        ho_own_toks.append(
            dma("y", ho_own.rearrange("p (c t) -> p c t", c=MC)[
                :, :, (TOK // PH) * p:(TOK // PH) * (p + 1)],
                a2a_out_p[p].ap().rearrange("(c p) t -> p c t", p=128)))

    x4_done = {}
    acc_done = {}
    for tt in range(NTT):
        par = tt % 2
        wait("y", x4_done.get(tt - 2, 0))
        wait("y", xfree_gate[0])
        x4_tok = dma("y", x_t4[par], x_own.ap()[128 * tt:128 * (tt + 1), :])
        wait("v", x4_tok)
        wait("v", p3_t_end)
        if tt == 0:
            wait("v", bo_split)
        acc_done[tt] = run("v", vE, "tensor_copy",
                           accum[:, D * tt:D * (tt + 1)], x_t4[par])
        x4_done[tt] = acc_done[tt]
    run0("v", "drain")
    acc_init = acc_done[NTT - 1]

    hov = ho_own.rearrange("p (c t) -> p c t", c=MC)
    op_ev = {0: acc_init, 1: acc_init}
    NHLF = min(PH, 2)
    tt_half = [[tt for tt in range(NTT) if (tt * 128) // max(1, TOK // NHLF) == hf]
               for hf in range(NHLF)]
    use_seq = [(hf, oc) for hf in range(NHLF) for oc in range(NOC)]
    wo_use_tok = {}
    wo_cons = {}
    for ui, (hf, oc) in enumerate(use_seq):
        if ui < 2:
            wo_use_tok[ui] = wo_tok[min(oc, max(wo_tok))]
    for ui, (hf, oc) in enumerate(use_seq):
        par = ui % 2
        wov = wo_buf[par].rearrange("p (c o) -> p c o", c=MC)
        for tt in tt_half[hf]:
            pp = (ui * NTT + tt) % 2
            ps = psm[pp]
            wait("t", wo_use_tok[ui])
            wait("t", ho_own_toks[min(len(ho_own_toks) - 1, hf)])
            wait("t", op_ev[pp])
            wait("t", bo_split)
            for c in range(MC):
                mm(ps[:, 0:512], hov[:, c, 128 * tt:128 * (tt + 1)], wov[:, c, :],
                   start=(c == 0), stop=False)
            mm(ps[:, 0:512], ones_row, bo_hi[0:1, 512 * oc:512 * (oc + 1)],
               start=False, stop=False)
            mt = mm_i(ps[:, 0:512], ones_row, bo_lo[0:1, 512 * oc:512 * (oc + 1)],
                      start=False, stop=True)
            wait("v", mt)
            wait("v", acc_done[tt])
            op_ev[pp] = run("v", vE, "tensor_tensor",
                            accum[:, D * tt + 512 * oc:D * tt + 512 * (oc + 1)],
                            accum[:, D * tt + 512 * oc:D * tt + 512 * (oc + 1)],
                            ps[:, 0:512], ALU.add)
        wo_cons[par] = (tE, tE.n)
        if ui + 2 < len(use_seq):
            nhf, noc = use_seq[ui + 2]
            wait("g", wo_cons[par])
            wo_use_tok[ui + 2] = dma(
                "g", wo_buf[par].rearrange("p (c o) -> p c o", c=MC),
                blob_slice(OFF_WO, D, D)[:, :, 512 * noc:512 * (noc + 1)])
    oproj_t_end = (tE, tE.n)
    acc_guard = run("v", vE, "drain")

    wait("g", oproj_t_end)
    wait("g", acc_guard)
    xn_own_tok = dma("g", xn_own.rearrange("p (c t) -> p c t", c=MC),
                     xn_own_dram.ap().rearrange("(c p) t -> p c t", p=128))
    wd_tok = {0: dma("g", wd_buf.rearrange("p (c o) -> p c o", c=ICS),
                     blob_slice(OFF_WD, SEG, D))}
    if NSEG > 1:
        gu_tok[1] = dma("g", gu_buf[0].rearrange("p (c f) -> p c f", c=MC),
                        blob_slice(OFF_GU + D * 2 * SEG, D, 2 * SEG))
        gu_home[1] = 0

    xnv = xn_own.rearrange("p (c t) -> p c t", c=MC)
    g_free = 0
    u_free = acc_guard
    gs_free = 0
    d_ev = {0: acc_guard, 1: acc_guard, 2: acc_guard, 3: acc_guard}
    down_wm = {-1: oproj_t_end}
    acc_fin = {}
    gu_cons = {}
    hmul = 0
    for s_ in range(NSEG):
        home = gu_home[s_]
        guv = gu_buf[home].rearrange("p (c f) -> p c f", c=MC)
        wdv = wd_buf.rearrange("p (c o) -> p c o", c=ICS)
        wait("t", gu_tok[s_])
        wait("t", xn_own_tok)
        for ts in range(NTS):
            for ic in range(ICS):
                wait("t", g_free)
                wait("t", d_ev[2])
                wait("t", d_ev[3])
                mtg = 0
                for m in range(MC):
                    if m < MC - 1:
                        mm(psm[2][:, 0:TS], guv[:, m, 128 * ic:128 * (ic + 1)],
                           xnv[:, m, TS * ts:TS * (ts + 1)], start=(m == 0), stop=False)
                    else:
                        mtg = mm_i(psm[2][:, 0:TS], guv[:, m, 128 * ic:128 * (ic + 1)],
                                   xnv[:, m, TS * ts:TS * (ts + 1)], start=(m == 0),
                                   stop=True)
                wait("t", u_free)
                wait("t", d_ev[2])
                wait("t", d_ev[3])
                mtu = 0
                for m in range(MC):
                    if m < MC - 1:
                        mm(psm[3][:, 0:TS], guv[:, m, SEG + 128 * ic:SEG + 128 * (ic + 1)],
                           xnv[:, m, TS * ts:TS * (ts + 1)], start=(m == 0), stop=False)
                    else:
                        mtu = mm_i(psm[3][:, 0:TS],
                                   guv[:, m, SEG + 128 * ic:SEG + 128 * (ic + 1)],
                                   xnv[:, m, TS * ts:TS * (ts + 1)], start=(m == 0),
                                   stop=True)
                wait("s", mtg)
                wait("s", gs_free)
                g_free = run("s", scE, "activation", gs_sb, psm[2][:, 0:TS], AF.Silu)
                wait("v", g_free)
                wait("v", mtu)
                wait("v", down_wm[s_ - 1])
                hmul = run("v", vE, "tensor_tensor",
                           h_sb[:, TOK * ic + TS * ts:TOK * ic + TS * (ts + 1)],
                           gs_sb, psm[3][:, 0:TS], ALU.mult)
                u_free = hmul
                gs_free = hmul
        gu_cons[s_] = (tE, tE.n)
        hv = h_sb.rearrange("p (c t) -> p c t", c=ICS)
        wait("t", wd_tok[s_])
        wait("t", hmul)
        for tt in range(NTT):
            for oc2 in range(NOC):
                pp = (tt * NOC + oc2) % 4
                ps = psm[pp]
                wait("t", d_ev[pp])
                mt = 0
                for ic in range(ICS):
                    if ic < ICS - 1:
                        mm(ps[:, 0:512], hv[:, ic, 128 * tt:128 * (tt + 1)],
                           wdv[:, ic, 512 * oc2:512 * (oc2 + 1)],
                           start=(ic == 0), stop=False)
                    else:
                        mt = mm_i(ps[:, 0:512], hv[:, ic, 128 * tt:128 * (tt + 1)],
                                  wdv[:, ic, 512 * oc2:512 * (oc2 + 1)],
                                  start=(ic == 0), stop=True)
                wait("v", mt)
                d_ev[pp] = run("v", vE, "tensor_tensor",
                               accum[:, D * tt + 512 * oc2:D * tt + 512 * (oc2 + 1)],
                               accum[:, D * tt + 512 * oc2:D * tt + 512 * (oc2 + 1)],
                               ps[:, 0:512], ALU.add)
                if s_ == NSEG - 1 and oc2 == NOC - 1:
                    acc_fin[tt] = d_ev[pp]
        down_wm[s_] = (tE, tE.n)
        run0("v", "drain")
        if s_ + 2 < NSEG:
            wait("g", gu_cons[s_])
            gu_tok[s_ + 2] = dma("g", gu_buf[home].rearrange("p (c f) -> p c f", c=MC),
                                 blob_slice(OFF_GU + (s_ + 2) * D * 2 * SEG, D, 2 * SEG))
            gu_home[s_ + 2] = home
        if s_ + 1 < NSEG:
            wait("g", down_wm[s_])
            wd_tok[s_ + 1] = dma("g", wd_buf.rearrange("p (c o) -> p c o", c=ICS),
                                 blob_slice(OFF_WD + (s_ + 1) * SEG * D, SEG, D))

    out_toks = []
    for tt in range(NTT):
        wait("y", acc_fin[tt])
        out_toks.append(dma("y", out_own.ap()[128 * tt:128 * (tt + 1), :],
                            accum[:, D * tt:D * (tt + 1)]))
    for t in out_toks:
        wait("y", t)

    with nc.Block() as block:
        @block.tensor
        def _(eng):
            for f in ops["t"]:
                f(eng)

        @block.vector
        def _(eng):
            for f in ops["v"]:
                f(eng)

        @block.scalar
        def _(eng):
            for f in ops["s"]:
                f(eng)

        @block.sync
        def _(eng):
            for f in ops["y"]:
                f(eng)

        @block.gpsimd
        def _(eng):
            for f in ops["g"]:
                f(eng)

    stack.close()
    n_inst = sum(len(v) for v in ops.values())
    return nc, n_inst


_PROG_CACHE = {}


def _get_prog(key_cfg, debug=False):
    key = (tuple(sorted(key_cfg.items())), debug)
    if key not in _PROG_CACHE:
        _PROG_CACHE[key] = build_program(make_cfg(**key_cfg), debug=debug)
    return _PROG_CACHE[key]


def stage_inputs(cfg, x, Wq, bq, Wk, bk, Wv, bv, Wo, bo, w_gate_up, w_down):
    S, D, NH, INTER = cfg["S"], cfg["D"], cfg["NH"], cfg["INTER"]
    OCW, TOK, SEG, NSEG, BW = cfg["OCW"], cfg["TOK"], cfg["SEG"], cfg["NSEG"], cfg["BW"]
    f = np.float32
    x = np.asarray(x, dtype=f).reshape(S, D)
    WqT = np.ascontiguousarray(np.asarray(Wq, dtype=f).T)
    WkT = np.ascontiguousarray(np.asarray(Wk, dtype=f).T)
    WvT = np.ascontiguousarray(np.asarray(Wv, dtype=f).T)
    WoT = np.ascontiguousarray(np.asarray(Wo, dtype=f).T)
    WguT = np.ascontiguousarray(np.asarray(w_gate_up, dtype=f).T)
    WdT = np.ascontiguousarray(np.asarray(w_down, dtype=f).T)
    bq = np.asarray(bq, dtype=f).reshape(D)
    bk = np.asarray(bk, dtype=f).reshape(D)
    bv = np.asarray(bv, dtype=f).reshape(D)
    bo = np.asarray(bo, dtype=f).reshape(1, D)

    parts = [WoT.ravel()]
    for s_ in range(NSEG):
        seg = np.concatenate([WguT[:, SEG * s_:SEG * (s_ + 1)],
                              WguT[:, INTER + SEG * s_:INTER + SEG * (s_ + 1)]], axis=1)
        parts.append(np.ascontiguousarray(seg).ravel())
    for s_ in range(NSEG):
        parts.append(np.ascontiguousarray(WdT[SEG * s_:SEG * (s_ + 1), :]).ravel())
    blob = np.concatenate(parts).reshape(NCORE, BW)
    x_c = np.ascontiguousarray(x)

    in_maps = []
    for c in range(NCORE):
        sl = slice(OCW * c, OCW * (c + 1))
        in_maps.append({
            "x_own": np.ascontiguousarray(x[TOK * c:TOK * (c + 1)]),
            "x_full": x_c,
            "wq_t": np.ascontiguousarray(WqT[:, sl]),
            "wk_t": np.ascontiguousarray(WkT[:, sl]),
            "wv_t": np.ascontiguousarray(WvT[:, sl]),
            "bq_c": np.ascontiguousarray(bq[sl].reshape(OCW, 1)),
            "bk_c": np.ascontiguousarray(bk[sl].reshape(OCW, 1)),
            "bv_c": np.ascontiguousarray(bv[sl].reshape(1, OCW)),
            "bo_c": bo,
            "wblob": np.ascontiguousarray(blob[c:c + 1]),
        })
    return in_maps


def kernel(L0_attn_hidden_states, L0_attn_Wq, L0_attn_bq, L0_attn_Wk, L0_attn_bk,
           L0_attn_Wv, L0_attn_bv, L0_attn_Wo, L0_attn_bo,
           L0_ffn_w_gate_up, L0_ffn_w_down):
    key_cfg = dict(S=8192, D=2048, NH=16, INTER=8192)
    cfg = make_cfg(**key_cfg)
    nc, _ = _get_prog(key_cfg)
    in_maps = stage_inputs(cfg, L0_attn_hidden_states, L0_attn_Wq, L0_attn_bq,
                           L0_attn_Wk, L0_attn_bk, L0_attn_Wv, L0_attn_bv,
                           L0_attn_Wo, L0_attn_bo, L0_ffn_w_gate_up, L0_ffn_w_down)
    res = run_bass_kernel_spmd(nc, in_maps, core_ids=list(range(NCORE)))
    out = np.concatenate([res.results[c]["out_own"] for c in range(NCORE)], axis=0)
    return out.reshape(1, cfg["S"], cfg["D"]).astype(np.float32)


# version 24
# speedup vs baseline: 1.1072x; 1.0469x over previous
import numpy as np
import concourse.bass as bass
import concourse.mybir as mybir
from concourse.bass_utils import run_bass_kernel_spmd
from concourse.aot_env import allow_ici_all_to_all

F32 = mybir.dt.float32
BF16 = mybir.dt.bfloat16
AF = mybir.ActivationFunctionType
ALU = mybir.AluOpType
AX = mybir.AxisListType

NCORE = 8
HD = 128
EPS = 1e-5
ALPHA = 1.0 / float(np.sqrt(HD))


def make_cfg(S=8192, D=2048, NH=16, INTER=8192):
    c = dict(S=S, D=D, NH=NH, INTER=INTER)
    c["HPC"] = NH // NCORE
    c["OCW"] = c["HPC"] * HD
    c["TOK"] = S // NCORE
    c["NTT"] = c["TOK"] // 128
    c["TB"] = min(512, c["TOK"])
    c["NB"] = S // c["TB"]
    c["NST"] = S // 512
    c["MC"] = D // 128
    c["NOC"] = D // 512
    c["SEG"] = 512
    c["NSEG"] = INTER // c["SEG"]
    c["TS"] = min(512, c["TOK"])
    c["NTS"] = c["TOK"] // c["TS"]
    c["OFF_WO"] = 0
    c["OFF_GU"] = D * D
    c["OFF_WD"] = D * D + 2 * INTER * D
    c["BLOB"] = D * D + 2 * INTER * D + INTER * D
    assert c["BLOB"] % NCORE == 0
    c["BW"] = c["BLOB"] // NCORE
    assert D == NH * HD
    return c


class _Sem:
    __slots__ = ("h", "n")

    def __init__(self, h):
        self.h = h
        self.n = 0


def build_program(cfg, debug=False):
    S, D, NH, INTER = cfg["S"], cfg["D"], cfg["NH"], cfg["INTER"]
    HPC, OCW, TOK, NTT = cfg["HPC"], cfg["OCW"], cfg["TOK"], cfg["NTT"]
    TB, NB, NST, MC, NOC = cfg["TB"], cfg["NB"], cfg["NST"], cfg["MC"], cfg["NOC"]
    SEG, NSEG, TS, NTS = cfg["SEG"], cfg["NSEG"], cfg["TS"], cfg["NTS"]
    BW = cfg["BW"]
    NVT = S // 128
    VTPB = TB // 128
    ICS = SEG // 128

    nc = bass.Bass("TRN2", target_bir_lowering=False, debug=False)

    x_own = nc.dram_tensor("x_own", [TOK, D], F32, kind="ExternalInput")
    wq_t = nc.dram_tensor("wq_t", [D, OCW], F32, kind="ExternalInput")
    wk_t = nc.dram_tensor("wk_t", [D, OCW], F32, kind="ExternalInput")
    wv_t = nc.dram_tensor("wv_t", [D, OCW], F32, kind="ExternalInput")
    bq_c = nc.dram_tensor("bq_c", [OCW, 1], F32, kind="ExternalInput")
    bk_c = nc.dram_tensor("bk_c", [OCW, 1], F32, kind="ExternalInput")
    bv_c = nc.dram_tensor("bv_c", [1, OCW], F32, kind="ExternalInput")
    bo_c = nc.dram_tensor("bo_c", [1, D], F32, kind="ExternalInput")
    wblob = nc.dram_tensor("wblob", [1, BW], F32, kind="ExternalInput")
    out_own = nc.dram_tensor("out_own", [TOK, D], F32, kind="ExternalOutput")

    XAG = 2 if (TOK // 2) >= min(512, TOK) else 1
    XW = TOK // XAG
    ag_xn_in_p = [nc.dram_tensor("ag_xn_in%d" % p, [D, XW], BF16) for p in range(XAG)]
    ag_xn_out_p = [nc.dram_tensor("ag_xn_out%d" % p, [NCORE * D, XW], BF16,
                                  addr_space="Shared") for p in range(XAG)]
    xn_own_dram = nc.dram_tensor("xn_own_dram", [D, TOK], BF16)
    ag_w_in = nc.dram_tensor("ag_w_in", [1, BW], BF16)
    ag_w_out = nc.dram_tensor("ag_w_out", [NCORE, BW], BF16, addr_space="Shared")
    PH = 2 if TOK >= 1024 else 1
    a2a_in_p = [nc.dram_tensor("a2a_in%d" % p, [NCORE * OCW, TOK // PH], BF16)
                for p in range(PH)]
    a2a_out_p = [nc.dram_tensor("a2a_out%d" % p, [NCORE * OCW, TOK // PH], BF16)
                 for p in range(PH)]

    dbg = {}
    if debug:
        dbg["xn"] = nc.dram_tensor("dbg_xn", [D, TOK], BF16, kind="ExternalOutput")
        dbg["q"] = nc.dram_tensor("dbg_q", [OCW, S], BF16, kind="ExternalOutput")
        dbg["k"] = nc.dram_tensor("dbg_k", [OCW, S], BF16, kind="ExternalOutput")
        dbg["v"] = nc.dram_tensor("dbg_v", [S, OCW], BF16, kind="ExternalOutput")
        dbg["bc"] = nc.dram_tensor("dbg_bc", [128, HPC], F32, kind="ExternalOutput")

    from contextlib import ExitStack
    stack = ExitStack()
    en = stack.enter_context

    K_ = 1024
    AR_BYTES = 192 * K_
    arena = en(nc.sbuf_tensor("arena", [128, AR_BYTES // 4], F32))
    arf = arena[:, :]
    arb = arf.bitcast(BF16)

    def bfv(off, cols, parts=128):
        assert off % 32 == 0 and off + 2 * cols <= AR_BYTES, (off, cols)
        return arb[0:parts, off // 2: off // 2 + cols]

    def f32v(off, cols, parts=128):
        assert off % 32 == 0 and off + 4 * cols <= AR_BYTES, (off, cols)
        return arf[0:parts, off // 4: off // 4 + cols]

    o = 0
    qw_sb = bfv(o, MC * OCW); o += 2 * MC * OCW
    kw_sb = bfv(o, MC * OCW); o += 2 * MC * OCW
    vw_sb = bfv(o, MC * OCW); o += 2 * MC * OCW
    _qkv_end = o
    Q_sb = bfv(o, HPC * S); o += 2 * HPC * S
    K_sb = bfv(o, HPC * S); o += 2 * HPC * S
    V_sb = bfv(o, NVT * OCW); o += 2 * NVT * OCW
    _xns = o
    xn_blk = [bfv(_xns + i * 2 * MC * TB, MC * TB) for i in range(2)]
    o += 2 * 2 * MC * TB
    _attn_hi = o
    TAIL0 = AR_BYTES - 24 * K_
    assert _attn_hi <= TAIL0
    p_buf = [bfv(TAIL0 + i * K_, 512) for i in range(4)]
    masks = [bfv(TAIL0 + 4 * K_ + i * K_, 512) for i in range(4)]
    ho_buf = [bfv(TAIL0 + 8 * K_ + i * K_, 512) for i in range(2)]
    r_sb2 = [f32v(TAIL0 + 10 * K_ + i * 2 * K_, 512) for i in range(2)]

    xc8 = [f32v(_qkv_end + i * 4 * D, D) for i in range(NTT)]
    xn_fm_own = bfv(_qkv_end + NTT * 4 * D, MC * TOK)
    sq_dump = f32v(TAIL0 + 14 * K_, D)
    assert _qkv_end + NTT * 4 * D + 2 * MC * TOK <= TAIL0

    accum = f32v(0, NTT * D)
    _A = 4 * NTT * D
    ho_own = bfv(_A, MC * TOK)
    xn_own = bfv(_A, MC * TOK)
    _B = _A + 2 * MC * TOK
    gu_buf = [bfv(_B, MC * 2 * SEG),
              bfv(AR_BYTES - 4 * MC * SEG, MC * 2 * SEG)]
    _C = _B + 2 * MC * K_
    wo_buf = [bfv(_xns + i * MC * K_, MC * 512) for i in range(2)]
    _wo_end = _xns + 2 * MC * K_
    _x_off = _B if _B + 8 * D <= _xns else _wo_end
    x_t4 = [f32v(_x_off + i * 4 * D, D) for i in range(2)]
    bo_f = f32v(0, D, parts=1)
    wd_buf = bfv(_C, ICS * D)
    _E = _C + 8 * D
    h_sb = bfv(_E, ICS * TOK)
    _F = max(_E + 2 * ICS * TOK, _wo_end,
             _x_off + 8 * D if _x_off >= _wo_end else 0)
    bo_hi = bfv(_F, D, parts=1)
    bo_lo = bfv(_F + 2 * D, D, parts=1)
    assert _F + 4 * D <= AR_BYTES - 4 * MC * SEG, ("P4 map overflow", _F, AR_BYTES)

    gs_sb = en(nc.sbuf_tensor("gs_sb", [128, TS], BF16))[:, :]
    consts = en(nc.sbuf_tensor("consts", [128, 1280], BF16))
    ident = consts[:, 0:128]
    ones_col = consts[:, 128:129]
    ones_row = consts[0:1, 160:288]
    ones_mat = consts[:, 1024:1152]
    bv_hi = consts[0:1, 288:288 + OCW]
    bv_lo = consts[0:1, 544:544 + OCW]
    fsm = en(nc.sbuf_tensor("fsm", [128, 3072], F32))
    ones_f32 = fsm[0:1, 0:128]
    stats = fsm[0:1, 1664:1728]
    sc1 = fsm[0:1, 1728:1736]
    bq_sb = fsm[:, 1744:1744 + HPC]
    bk_sb = fsm[:, 1752:1752 + HPC]
    biasC = fsm[:, 1760:1760 + HPC]
    mst = fsm[:, 1776:1792]
    bv_f = fsm[0:1, 1792:1792 + OCW]
    bv_t1 = fsm[0:1, 384:384 + OCW]
    lnU = fsm[:, 640:640 + NTT]
    lnW = fsm[:, 672:672 + NTT]
    lnT1 = fsm[:, 704:704 + NTT]
    lnT2 = fsm[:, 736:736 + NTT]
    lnR = fsm[:, 768:768 + NTT]
    lnM = fsm[:, 800:800 + NTT]
    identF = fsm[:, 2304:2432]
    bo_t1 = fsm[0:1, 2432:2432 + 512]

    psA = en(nc.psum_tensor("psA", [128, 1024], F32))
    psB = en(nc.psum_tensor("psB", [128, 1024], F32))
    psm = [en(nc.psum_tensor("psm%d" % i, [128, 512], F32)) for i in range(4)]

    NCH = 6
    dch = {"y": [_Sem(en(nc.semaphore("sY%d" % i))) for i in range(NCH)],
           "g": [_Sem(en(nc.semaphore("sG%d" % i))) for i in range(NCH)]}
    _rr = {"y": 0, "g": 0}
    tE = _Sem(en(nc.semaphore("tE")))
    vE = _Sem(en(nc.semaphore("vE")))
    scE = _Sem(en(nc.semaphore("scE")))
    gE = _Sem(en(nc.semaphore("gE")))
    ccs = [_Sem(en(nc.semaphore("cc%d" % i))) for i in range(3)]

    ops = {k: [] for k in "tvsyg"}
    _wmax = {}

    def em(e, f):
        ops[e].append(f)

    def wait(e, tok):
        if not tok:
            return
        sem, val = tok
        if val <= 0:
            return
        key = (e, id(sem))
        if _wmax.get(key, 0) >= val:
            return
        _wmax[key] = val
        em(e, lambda eng, _s=sem.h, _t=val: eng.wait_ge(_s, _t))

    def tmax(*toks):
        best = 0
        for t in toks:
            if t and (not best or t[1] > best[1]):
                if best:
                    assert t[0] is best[0]
                best = t
        return best

    def dma(e, out, in_, **kw):
        ch = _rr[e] % NCH
        _rr[e] += 1
        sem = dch[e][ch]
        wait(e, (sem, sem.n))
        sem.n += 16
        em(e, lambda eng, _o=out, _i=in_, _s=sem.h, _k=dict(kw):
           eng.dma_start(out=_o, in_=_i, **_k).then_inc(_s, 16))
        return (sem, sem.n)

    def run(e, sem, meth, *a, **kw):
        sem.n += 1
        em(e, lambda eng, _m=meth, _a=a, _k=dict(kw), _s=sem.h:
           getattr(eng, _m)(*_a, **_k).then_inc(_s, 1))
        return (sem, sem.n)

    def run0(e, meth, *a, **kw):
        em(e, lambda eng, _m=meth, _a=a, _k=dict(kw): getattr(eng, _m)(*_a, **_k))

    def mm(out, lhsT, rhs, start, stop):
        run0("t", "matmul", out, lhsT, rhs, start=start, stop=stop,
             skip_group_check=True)

    def mm_i(out, lhsT, rhs, start, stop):
        return run("t", tE, "matmul", out, lhsT, rhs, start=start, stop=stop,
                   skip_group_check=True)


    def fence(wait_engine, q):
        for sem in dch[q]:
            wait(wait_engine, (sem, sem.n))

    qkv_toks = []
    for (w_in, w_sb) in ((wq_t, qw_sb), (wk_t, kw_sb), (wv_t, vw_sb)):
        qkv_toks.append(dma("g", w_sb.rearrange("p (c o) -> p c o", c=MC),
                            w_in.ap().rearrange("(c p) o -> p c o", p=128)))
    b_toks = [dma("y", bq_sb[:, :], bq_c.ap().rearrange("(c p) one -> p (c one)", p=128),
                  allow_slow_non_contiguous=True),
              dma("y", bk_sb[:, :], bk_c.ap().rearrange("(c p) one -> p (c one)", p=128),
                  allow_slow_non_contiguous=True),
              dma("y", bv_f[:, :], bv_c[:, :])]

    run("g", gE, "memset", ident, 1.0)
    run("g", gE, "memset", identF, 1.0)
    run("g", gE, "memset", consts[:, 128:160], 1.0)
    run("g", gE, "memset", consts[0:1, 160:288], 1.0)
    run("g", gE, "memset", ones_mat, 1.0)
    run("g", gE, "memset", fsm[0:1, 0:128], 1.0)
    for j in range(4):
        run("g", gE, "memset", masks[j], 1.0)
    wait("g", (gE, gE.n))
    run("g", gE, "affine_select", out=ident, in_=ident, compare_op=ALU.is_equal,
        fill=0.0, base=0, pattern=[[-1, 128]], channel_multiplier=1)
    run("g", gE, "affine_select", out=identF, in_=identF, compare_op=ALU.is_equal,
        fill=0.0, base=0, pattern=[[-1, 128]], channel_multiplier=1)
    for j in range(4):
        run("g", gE, "affine_select", out=masks[j], in_=masks[j],
            compare_op=ALU.is_ge, fill=0.0, base=-128 * j,
            pattern=[[1, 512]], channel_multiplier=-1)
    g_const = (gE, gE.n)

    for t in b_toks:
        wait("v", t)
    run0("v", "tensor_copy", bv_hi, bv_f)
    run0("v", "drain")
    run0("v", "tensor_copy", bv_t1, bv_hi)
    run0("v", "drain")
    run0("v", "tensor_sub", bv_t1, bv_f, bv_t1)
    run0("v", "drain")
    run0("v", "tensor_copy", bv_lo, bv_t1)
    bv_split = run("v", vE, "drain")

    xt_toks = {}
    sq_prev = 0
    for tt in range(NTT):
        xt_toks[tt] = dma("y", xc8[tt], x_own.ap()[128 * tt:128 * (tt + 1), :])
        wait("v", xt_toks[tt])
        run0("v", "reduce_sum", lnU[:, tt:tt + 1], xc8[tt], axis=AX.X)
        wait("s", xt_toks[tt])
        wait("s", sq_prev)
        sq_prev = run("s", scE, "activation", sq_dump, xc8[tt], AF.Square,
                      accum_out=lnW[:, tt:tt + 1])
    vA = run("v", vE, "drain")
    wait("v", sq_prev)
    run0("v", "tensor_scalar", lnT1, lnW, float(D), float(EPS) * D * D,
         ALU.mult, ALU.add)
    run0("v", "tensor_tensor", lnT2, lnU, lnU, ALU.mult)
    run0("v", "drain")
    run0("v", "tensor_sub", lnT1, lnT1, lnT2)
    vB1 = run("v", vE, "drain")
    wait("s", vB1)
    stB = run("s", scE, "activation", lnT2, lnT1, AF.Sqrt)
    wait("v", stB)
    run0("v", "reciprocal", lnT1, lnT2)
    run0("v", "drain")
    run0("v", "tensor_scalar_mul", lnR, lnT1, float(D))
    run0("v", "drain")
    run0("v", "scalar_tensor_tensor", lnM, lnU, -1.0 / D, lnR, ALU.mult, ALU.mult)
    vB = run("v", vE, "drain")
    ps_free = {0: 0, 1: 0}
    ev_done = {}
    ev_last = 0
    for tt in range(NTT):
        wait("s", vB)
        nt_tok = run("s", scE, "activation", xc8[tt], xc8[tt], AF.Identity,
                     scale=lnR[:, tt:tt + 1], bias=lnM[:, tt:tt + 1])
        psT = (psA, psB)[tt % 2]
        wait("t", nt_tok)
        wait("t", g_const)
        for half in range(MC // 8):
            wait("t", ps_free[tt % 2])
            tt2 = 0
            for c8 in range(8):
                c = 8 * half + c8
                tt2 = run("t", tE, "transpose", psT[:, 128 * c8:128 * (c8 + 1)],
                          xc8[tt][:, 128 * c:128 * (c + 1)], identF)
            wait("v", tt2)
            dstv = xn_fm_own.rearrange("p (c t) -> p c t", c=MC)[
                :, 8 * half:8 * half + 8, 128 * tt:128 * (tt + 1)]
            srcv = psT[:, 0:1024].rearrange("p (c t) -> p c t", c=8)
            ps_free[tt % 2] = run("v", vE, "tensor_copy", dstv, srcv)
            ev_last = ps_free[tt % 2]
        ev_done[tt] = ev_last

    cc_xn_p = []
    for xp in range(XAG):
        last_tile = ((xp + 1) * XW) // 128 - 1
        wait("y", ev_done[last_tile])
        shv = xn_fm_own.rearrange("p (c t) -> p c t", c=MC)[:, :, XW * xp:XW * (xp + 1)]
        t_ship = dma("y", ag_xn_in_p[xp].ap().rearrange("(c p) t -> p c t", p=128), shv)
        wait("g", t_ship)
        ccs[0].n += 1
        cc_xn_p.append((ccs[0], ccs[0].n))
        em("g", lambda g, _p=xp: g.collective_compute(
            "AllGather", ALU.bypass, replica_groups=[list(range(NCORE))],
            ins=[ag_xn_in_p[_p].ap().opt()],
            outs=[ag_xn_out_p[_p].ap().opt()]).then_inc(ccs[0].h, 1))
    cc_xn = cc_xn_p[-1]
    wait("y", ev_last)
    agxn_toks = [dma("y", xn_own_dram.ap().rearrange("(c p) t -> p c t", p=128),
                     xn_fm_own.rearrange("p (c t) -> p c t", c=MC))]
    if debug:
        agxn_toks.append(dma("y", dbg["xn"].ap().rearrange("(c p) t -> p c t", p=128),
                             xn_fm_own.rearrange("p (c t) -> p c t", c=MC)))
    for t in agxn_toks:
        wait("g", t)
    xnb_tok = {}
    xnb_free = {0: 0, 1: 0}
    qk_ev = {0: 0, 1: 0}
    v_ev = {0: 0, 1: 0}
    last_q_ev = 0
    last_v_ev = 0
    if XAG == 2:
        block_order = [b for b in range(NB) if ((b * TB) % TOK) < XW] + \
                      [b for b in range(NB) if ((b * TB) % TOK) >= XW]
    else:
        block_order = list(range(NB))
    sq_free = {0: 0, 1: 0, 2: 0, 3: 0}
    nrm_free = {0: 0, 1: 0}
    nsq = [0]
    npp = [0]
    pending_norm = []

    def flush_norms():
        while pending_norm:
            sqv, idx, sqt = pending_norm.pop(0)
            pp2 = npp[0] % 2
            npp[0] += 1
            wait("t", sqt)
            wait("t", nrm_free[pp2])
            psN = (psA, psB)[pp2][0:1, 0:512]
            mt2 = mm_i(psN[:, 0:TB] if TB < 512 else psN, ones_col, sqv,
                       start=True, stop=True)
            sq_free[nsq[0] % 4] = mt2
            nsq[0] += 1
            wait("v", mt2)
            nrm_free[pp2] = run("v", vE, "reduce_max", stats[0:1, idx:idx + 1],
                                (psA, psB)[pp2][0:1, 0:TB], axis=AX.X)
    for bi, b in enumerate(block_order):
        par = bi % 2
        shard = (b * TB) // TOK
        hh = (b * TB) % TOK
        xhalf = hh // XW
        wait("g", cc_xn_p[xhalf])
        src = ag_xn_out_p[xhalf].ap()[shard * D:(shard + 1) * D,
                                      hh % XW:hh % XW + TB] \
            .rearrange("(c p) t -> p c t", p=128)
        wait("g", xnb_free[par])
        xnb_tok[b] = dma("g", xn_blk[par].rearrange("p (c t) -> p c t", c=MC), src)

        flush_norms()
        wait("t", xnb_tok[b])
        if bi < 2:
            for t in qkv_toks:
                wait("t", t)
            wait("t", g_const)
            for t in b_toks:
                wait("t", t)
        xv = xn_blk[par].rearrange("p (c t) -> p c t", c=MC)
        for which, (wsb, bcol, dst) in enumerate(((qw_sb, bq_sb, Q_sb),
                                                  (kw_sb, bk_sb, K_sb))):
            wv = wsb.rearrange("p (c o) -> p c o", c=MC)
            for h in range(HPC):
                pp = (bi * 2 * HPC + which * HPC + h) % 2
                ps = psm[pp]
                wait("t", qk_ev[pp])
                mt = 0
                for m in range(MC):
                    if m < MC - 1:
                        mm(ps[:, 0:TB], wv[:, m, 128 * h:128 * (h + 1)], xv[:, m, :],
                           start=(m == 0), stop=False)
                    else:
                        mt = mm_i(ps[:, 0:TB], wv[:, m, 128 * h:128 * (h + 1)],
                                  xv[:, m, :], start=(m == 0), stop=True)
                wait("s", mt)
                qk_ev[pp] = run("s", scE, "activation",
                                dst[:, S * h + TB * b:S * h + TB * (b + 1)],
                                ps[:, 0:TB], AF.Identity, bias=bcol[:, h:h + 1])
                last_q_ev = qk_ev[pp]
                sqp = (nsq[0] + len(pending_norm)) % 4
                wait("s", qk_ev[pp])
                wait("s", sq_free[sqp])
                sqt = run("s", scE, "activation", p_buf[sqp][:, 0:TB],
                          dst[:, S * h + TB * b:S * h + TB * (b + 1)], AF.Square)
                pending_norm.append((p_buf[sqp][:, 0:TB],
                                     (2 * h + which) * 16 + b, sqt))
        wv = vw_sb.rearrange("p (c o) -> p c o", c=MC)
        npair = (VTPB + 1) // 2
        for pair in range(npair):
            pp = 2 + (bi * npair + pair) % 2
            ps = psm[pp]
            wait("t", v_ev[pp - 2])
            wait("t", bv_split)
            nj = min(2, VTPB - 2 * pair)
            mt = 0
            for j in range(nj):
                t4 = 2 * pair + j
                for m in range(MC):
                    mm(ps[:, OCW * j:OCW * (j + 1)], xv[:, m, 128 * t4:128 * (t4 + 1)],
                       wv[:, m, :], start=(m == 0), stop=False)
                mm(ps[:, OCW * j:OCW * (j + 1)], ones_row, bv_hi, start=False, stop=False)
                mt = mm_i(ps[:, OCW * j:OCW * (j + 1)], ones_row, bv_lo,
                          start=False, stop=True)
            wait("v", mt)
            base = OCW * (VTPB * b + 2 * pair)
            v_ev[pp - 2] = run("v", vE, "tensor_copy",
                               V_sb[:, base:base + nj * OCW], ps[:, 0:nj * OCW])
            last_v_ev = v_ev[pp - 2]
        xnb_free[par] = (tE, tE.n)

    t_blob = dma("g", ag_w_in[:, :], wblob[:, :])
    wait("g", t_blob)
    _a2a0_hole = [None]

    def _wait_a2a0(g):
        for (h_, v_) in _a2a0_hole[0]:
            g.wait_ge(h_, v_)
    em("g", _wait_a2a0)
    ccs[2].n += 1
    cc_a2a0 = (ccs[2], ccs[2].n)

    def _emit_a2a0(g):
        with allow_ici_all_to_all():
            g.collective_compute(
                "AllToAll", ALU.bypass, replica_groups=[list(range(NCORE))],
                ins=[a2a_in_p[0].ap().opt()],
                outs=[a2a_out_p[0].ap().opt()]).then_inc(ccs[2].h, 1)
    em("g", _emit_a2a0)
    _wag_hole = [None]
    em("g", lambda g: g.wait_ge(*_wag_hole[0]))
    ccs[1].n = 1
    cc_w = (ccs[1], 1)
    em("g", lambda g: g.collective_compute(
        "AllGather", ALU.bypass, replica_groups=[list(range(NCORE))],
        ins=[ag_w_in.ap().opt()], outs=[ag_w_out.ap().opt()]).then_inc(ccs[1].h, 1))

    p2_t_end = (tE, tE.n)
    blobv = ag_w_out.ap().rearrange("r w -> (r w)")

    def blob_slice(off, rows, cols):
        return blobv[off:off + rows * cols].rearrange("(c p o) -> p c o", p=128, o=cols)

    OFF_WO, OFF_GU, OFF_WD = cfg["OFF_WO"], cfg["OFF_GU"], cfg["OFF_WD"]
    wait("g", cc_w)
    wait("g", p2_t_end)
    wo_tok = {0: dma("g", wo_buf[0].rearrange("p (c o) -> p c o", c=MC),
                     blob_slice(OFF_WO, D, D)[:, :, 0:512])}
    if NOC > 1:
        wo_tok[1] = dma("g", wo_buf[1].rearrange("p (c o) -> p c o", c=MC),
                        blob_slice(OFF_WO, D, D)[:, :, 512:1024])

    if debug:
        wait("y", last_q_ev)
        wait("y", last_v_ev)
        dma("y", dbg["q"].ap().rearrange("(h p) t -> p h t", p=128),
            Q_sb.rearrange("p (h t) -> p h t", h=HPC))
        dma("y", dbg["k"].ap().rearrange("(h p) t -> p h t", p=128),
            K_sb.rearrange("p (h t) -> p h t", h=HPC))
        dma("y", dbg["v"].ap().rearrange("(n p) o -> p n o", p=128),
            V_sb.rearrange("p (n o) -> p n o", n=NVT))

    flush_norms()
    wait("v", tmax(nrm_free[0], nrm_free[1]))
    run0("v", "drain")
    biasC_tok = 0
    for h in range(HPC):
        run0("v", "reduce_max", sc1[0:1, 0:1],
             stats[0:1, 2 * h * 16:2 * h * 16 + NB], axis=AX.X)
        run0("v", "drain")
        run0("v", "reduce_max", sc1[0:1, 1:2],
             stats[0:1, (2 * h + 1) * 16:(2 * h + 1) * 16 + NB], axis=AX.X)
        run0("v", "drain")
        vt = run("v", vE, "tensor_tensor", sc1[0:1, 2:3], sc1[0:1, 0:1],
                 sc1[0:1, 1:2], ALU.mult)
        wait("s", vt)
        st = run("s", scE, "activation", sc1[0:1, 3:4], sc1[0:1, 2:3], AF.Sqrt)
        wait("v", st)
        vt = run("v", vE, "tensor_scalar_mul", sc1[0:1, 4:5], sc1[0:1, 3:4], -ALPHA)
        wait("t", vt)
        wait("t", last_v_ev)
        mt = mm_i(psm[2][:, 0:1], ones_f32, sc1[0:1, 4:5], start=True, stop=True)
        wait("v", mt)
        biasC_tok = run("v", vE, "tensor_copy", biasC[:, h:h + 1], psm[2][:, 0:1])
        run0("v", "drain")
    p2norm_t = (tE, tE.n)
    if debug:
        wait("y", biasC_tok)
        dma("y", dbg["bc"].ap(), biasC)

    wait("y", p2_t_end)
    bo_tok0 = dma("y", bo_f[:, :], bo_c[:, :])
    wait("v", bo_tok0)
    run0("v", "tensor_copy", bo_hi, bo_f)
    run0("v", "drain")
    for q in range(D // 512):
        run0("v", "tensor_copy", bo_t1, bo_hi[0:1, 512 * q:512 * (q + 1)])
        run0("v", "drain")
        run0("v", "tensor_sub", bo_t1, bo_f[0:1, 512 * q:512 * (q + 1)], bo_t1)
        run0("v", "drain")
        run0("v", "tensor_copy", bo_lo[0:1, 512 * q:512 * (q + 1)], bo_t1)
        run0("v", "drain")
    bo_split = run("v", vE, "drain")

    scq = [psA[:, 0:512], psA[:, 512:1024], psB[:, 0:512], psB[:, 512:1024]]
    exp_tok = {q: 0 for q in range(4)}
    pbuf_cons = {q: p2norm_t for q in range(4)}
    sc_cons = {q: nrm_free[q // 2] for q in range(4)}
    av_free = {0: last_v_ev, 1: last_v_ev}
    sum_free = {0: biasC_tok, 1: biasC_tok}
    ho_free = {0: [], 1: []}
    wait("s", biasC_tok)
    wait("t", last_q_ev)
    wait("t", last_v_ev)
    a2a_ship = {p: [] for p in range(PH)}
    TOKC = min(TOK // PH, 512) if PH == 2 else min(TOK, 512)
    LA = 3
    TLA = 4

    if PH == 2:
        strip_order = [i for i in range(NST - 2, -1, -2)] + \
                      [i for i in range(NST - 1, 0, -2)]
    else:
        strip_order = list(range(NST))
    jobs = []
    for i in strip_order:
        nt = 4 * (i + 1)
        for h in range(HPC):
            for kt in range(nt):
                jobs.append((i, kt, h))
    njobs = len(jobs)
    avt_h = {}
    tails = {}

    def emit_sc(g):
        i, kt, h = jobs[g]
        q4 = g % 4
        qs = Q_sb[:, S * h + 512 * i:S * h + 512 * (i + 1)]
        wait("t", exp_tok[q4])
        wait("t", sc_cons[q4])
        mt = mm_i(scq[q4], K_sb[:, S * h + 128 * kt:S * h + 128 * (kt + 1)],
                  qs, start=True, stop=True)
        if g % 2 == 0:
            return
        qe, qo = (g - 1) % 4, q4
        pj = bfv(TAIL0 + ((g - 1) % 4) * K_, 1024)
        psrc = (psA, psB)[(g % 4) // 2][:, 0:1024]
        wait("s", mt)
        wait("s", pbuf_cons[qe])
        wait("s", pbuf_cons[qo])
        et = run("s", scE, "activation", pj, psrc, AF.Exp,
                 bias=biasC[:, h:h + 1], scale=ALPHA)
        exp_tok[qe] = exp_tok[qo] = et
        sc_cons[qe] = sc_cons[qo] = 0
        k0 = jobs[g - 1][1]
        if kt >= 4 * i or k0 >= 4 * i:
            wait("v", et)
            for (qq, kk) in ((qe, k0), (qo, kt)):
                if kk >= 4 * i:
                    run0("v", "tensor_tensor", p_buf[qq], p_buf[qq],
                         masks[kk - 4 * i], ALU.mult)
            gt = run("v", vE, "drain")
            exp_tok[qe] = exp_tok[qo] = gt

    def emit_av(g):
        i, kt, h = jobs[g]
        q4 = g % 4
        nt = 4 * (i + 1)
        wait("t", exp_tok[q4])
        if kt == 0:
            wait("t", av_free[h])
            wait("t", sum_free[h])
        mm(psm[h][:, 0:512],
           V_sb[:, OCW * kt + 128 * h:OCW * kt + 128 * (h + 1)],
           p_buf[q4], start=(kt == 0), stop=(kt == nt - 1))
        avt = mm_i(psm[2 + h][:, 0:512], ones_mat, p_buf[q4],
                   start=(kt == 0), stop=(kt == nt - 1))
        pbuf_cons[q4] = avt
        if kt == nt - 1:
            avt_h[(i, h)] = avt
            tails[g + TLA] = (i, h)

    def emit_tail(i, h):
        hb = (i * HPC + h) % 2
        wait("v", avt_h[(i, h)])
        for t in ho_free[hb]:
            wait("v", t)
        vtr = run("v", vE, "reciprocal", r_sb2[hb], psm[2 + h][:, 0:512])
        sum_free[h] = vtr
        run0("v", "drain")
        vt = run("v", vE, "tensor_tensor", ho_buf[hb], psm[h][:, 0:512],
                 r_sb2[hb], ALU.mult)
        av_free[h] = vt
        wait("y", vt)
        ho_free[hb] = []
        for u in range(512 // TOKC):
            g_t = 512 * i + u * TOKC
            rows0 = OCW * (g_t // TOK) + 128 * h
            hp = (g_t % TOK) // (TOK // PH)
            col0 = g_t % (TOK // PH)
            tok = dma("y", a2a_in_p[hp].ap()[rows0:rows0 + 128, col0:col0 + TOKC],
                      ho_buf[hb][:, u * TOKC:(u + 1) * TOKC])
            ho_free[hb].append(tok)
            a2a_ship[hp].append(tok)

    wag_gate = [None]
    xfree_gate = [None]
    XSAFE = 4
    _remaining = set(strip_order)
    _done_ct = {}
    for i in strip_order:
        _done_ct[i] = 4 * (i + 1) * HPC
    _per_strip_left = dict(_done_ct)
    for g in range(njobs + TLA + 1):
        if g in tails:
            i_, h_ = tails.pop(g)
            emit_tail(i_, h_)
        if g < njobs:
            emit_sc(g)
        if g - LA >= 0 and g - LA < njobs:
            emit_av(g - LA)
            i_av = jobs[g - LA][0]
            _per_strip_left[i_av] -= 1
            if _per_strip_left[i_av] == 0:
                _remaining.discard(i_av)
                if xfree_gate[0] is None and all(i2 < XSAFE for i2 in _remaining):
                    xfree_gate[0] = (tE, tE.n)
        if g == njobs // 3:
            wag_gate[0] = (tE, tE.n)
    p3_t_end = (tE, tE.n)
    if xfree_gate[0] is None:
        xfree_gate[0] = p3_t_end
    _wag_hole[0] = (wag_gate[0][0].h, wag_gate[0][1])
    _a2a0_hole[0] = [(t_[0].h, t_[1]) for t_ in a2a_ship[0]]

    wait("g", cc_w)
    cc_a2a_p = [cc_a2a0]
    for p in range(1, PH):
        for t in a2a_ship[p]:
            wait("g", t)
        ccs[2].n += 1
        cc_a2a_p.append((ccs[2], ccs[2].n))

        def _emit_a2a(g, _p=p):
            with allow_ici_all_to_all():
                g.collective_compute(
                    "AllToAll", ALU.bypass, replica_groups=[list(range(NCORE))],
                    ins=[a2a_in_p[_p].ap().opt()],
                    outs=[a2a_out_p[_p].ap().opt()]).then_inc(ccs[2].h, 1)
        em("g", _emit_a2a)
    cc_a2a = cc_a2a_p[-1]

    wait("g", p3_t_end)
    gu_tok = {0: dma("g", gu_buf[1].rearrange("p (c f) -> p c f", c=MC),
                     blob_slice(OFF_GU, D, 2 * SEG))}
    gu_home = {0: 1}
    wait("y", p3_t_end)
    ho_own_toks = []
    for p in range(PH):
        wait("y", cc_a2a_p[p])
        ho_own_toks.append(
            dma("y", ho_own.rearrange("p (c t) -> p c t", c=MC)[
                :, :, (TOK // PH) * p:(TOK // PH) * (p + 1)],
                a2a_out_p[p].ap().rearrange("(c p) t -> p c t", p=128)))

    x4_done = {}
    acc_done = {}
    for tt in range(NTT):
        par = tt % 2
        wait("y", x4_done.get(tt - 2, 0))
        wait("y", xfree_gate[0])
        x4_tok = dma("y", x_t4[par], x_own.ap()[128 * tt:128 * (tt + 1), :])
        wait("v", x4_tok)
        wait("v", p3_t_end)
        if tt == 0:
            wait("v", bo_split)
        acc_done[tt] = run("v", vE, "tensor_copy",
                           accum[:, D * tt:D * (tt + 1)], x_t4[par])
        x4_done[tt] = acc_done[tt]
    run0("v", "drain")
    acc_init = acc_done[NTT - 1]

    hov = ho_own.rearrange("p (c t) -> p c t", c=MC)
    op_ev = {0: acc_init, 1: acc_init}
    NHLF = min(PH, 2)
    tt_half = [[tt for tt in range(NTT) if (tt * 128) // max(1, TOK // NHLF) == hf]
               for hf in range(NHLF)]
    use_seq = [(hf, oc) for hf in range(NHLF) for oc in range(NOC)]
    wo_use_tok = {}
    wo_cons = {}
    for ui, (hf, oc) in enumerate(use_seq):
        if ui < 2:
            wo_use_tok[ui] = wo_tok[min(oc, max(wo_tok))]
    for ui, (hf, oc) in enumerate(use_seq):
        par = ui % 2
        wov = wo_buf[par].rearrange("p (c o) -> p c o", c=MC)
        for tt in tt_half[hf]:
            pp = (ui * NTT + tt) % 2
            ps = psm[pp]
            wait("t", wo_use_tok[ui])
            wait("t", ho_own_toks[min(len(ho_own_toks) - 1, hf)])
            wait("t", op_ev[pp])
            wait("t", bo_split)
            for c in range(MC):
                mm(ps[:, 0:512], hov[:, c, 128 * tt:128 * (tt + 1)], wov[:, c, :],
                   start=(c == 0), stop=False)
            mm(ps[:, 0:512], ones_row, bo_hi[0:1, 512 * oc:512 * (oc + 1)],
               start=False, stop=False)
            mt = mm_i(ps[:, 0:512], ones_row, bo_lo[0:1, 512 * oc:512 * (oc + 1)],
                      start=False, stop=True)
            wait("v", mt)
            wait("v", acc_done[tt])
            op_ev[pp] = run("v", vE, "tensor_tensor",
                            accum[:, D * tt + 512 * oc:D * tt + 512 * (oc + 1)],
                            accum[:, D * tt + 512 * oc:D * tt + 512 * (oc + 1)],
                            ps[:, 0:512], ALU.add)
        wo_cons[par] = (tE, tE.n)
        if ui + 2 < len(use_seq):
            nhf, noc = use_seq[ui + 2]
            wait("g", wo_cons[par])
            wo_use_tok[ui + 2] = dma(
                "g", wo_buf[par].rearrange("p (c o) -> p c o", c=MC),
                blob_slice(OFF_WO, D, D)[:, :, 512 * noc:512 * (noc + 1)])
    oproj_t_end = (tE, tE.n)
    acc_guard = run("v", vE, "drain")

    wait("g", oproj_t_end)
    wait("g", acc_guard)
    xn_own_tok = dma("g", xn_own.rearrange("p (c t) -> p c t", c=MC),
                     xn_own_dram.ap().rearrange("(c p) t -> p c t", p=128))
    wd_tok = {0: dma("g", wd_buf.rearrange("p (c o) -> p c o", c=ICS),
                     blob_slice(OFF_WD, SEG, D))}
    if NSEG > 1:
        gu_tok[1] = dma("g", gu_buf[0].rearrange("p (c f) -> p c f", c=MC),
                        blob_slice(OFF_GU + D * 2 * SEG, D, 2 * SEG))
        gu_home[1] = 0

    xnv = xn_own.rearrange("p (c t) -> p c t", c=MC)
    g_free = 0
    u_free = acc_guard
    gs_free = 0
    d_ev = {0: acc_guard, 1: acc_guard, 2: acc_guard, 3: acc_guard}
    down_wm = {-1: oproj_t_end}
    acc_fin = {}
    gu_cons = {}
    hmul = 0
    for s_ in range(NSEG):
        home = gu_home[s_]
        guv = gu_buf[home].rearrange("p (c f) -> p c f", c=MC)
        wdv = wd_buf.rearrange("p (c o) -> p c o", c=ICS)
        wait("t", gu_tok[s_])
        wait("t", xn_own_tok)
        for ts in range(NTS):
            for ic in range(ICS):
                wait("t", g_free)
                wait("t", d_ev[2])
                wait("t", d_ev[3])
                mtg = 0
                for m in range(MC):
                    if m < MC - 1:
                        mm(psm[2][:, 0:TS], guv[:, m, 128 * ic:128 * (ic + 1)],
                           xnv[:, m, TS * ts:TS * (ts + 1)], start=(m == 0), stop=False)
                    else:
                        mtg = mm_i(psm[2][:, 0:TS], guv[:, m, 128 * ic:128 * (ic + 1)],
                                   xnv[:, m, TS * ts:TS * (ts + 1)], start=(m == 0),
                                   stop=True)
                wait("t", u_free)
                wait("t", d_ev[2])
                wait("t", d_ev[3])
                mtu = 0
                for m in range(MC):
                    if m < MC - 1:
                        mm(psm[3][:, 0:TS], guv[:, m, SEG + 128 * ic:SEG + 128 * (ic + 1)],
                           xnv[:, m, TS * ts:TS * (ts + 1)], start=(m == 0), stop=False)
                    else:
                        mtu = mm_i(psm[3][:, 0:TS],
                                   guv[:, m, SEG + 128 * ic:SEG + 128 * (ic + 1)],
                                   xnv[:, m, TS * ts:TS * (ts + 1)], start=(m == 0),
                                   stop=True)
                wait("s", mtg)
                wait("s", gs_free)
                g_free = run("s", scE, "activation", gs_sb, psm[2][:, 0:TS], AF.Silu)
                wait("v", g_free)
                wait("v", mtu)
                wait("v", down_wm[s_ - 1])
                hmul = run("v", vE, "tensor_tensor",
                           h_sb[:, TOK * ic + TS * ts:TOK * ic + TS * (ts + 1)],
                           gs_sb, psm[3][:, 0:TS], ALU.mult)
                u_free = hmul
                gs_free = hmul
        gu_cons[s_] = (tE, tE.n)
        hv = h_sb.rearrange("p (c t) -> p c t", c=ICS)
        wait("t", wd_tok[s_])
        wait("t", hmul)
        for tt in range(NTT):
            for oc2 in range(NOC):
                pp = (tt * NOC + oc2) % 4
                ps = psm[pp]
                wait("t", d_ev[pp])
                mt = 0
                for ic in range(ICS):
                    if ic < ICS - 1:
                        mm(ps[:, 0:512], hv[:, ic, 128 * tt:128 * (tt + 1)],
                           wdv[:, ic, 512 * oc2:512 * (oc2 + 1)],
                           start=(ic == 0), stop=False)
                    else:
                        mt = mm_i(ps[:, 0:512], hv[:, ic, 128 * tt:128 * (tt + 1)],
                                  wdv[:, ic, 512 * oc2:512 * (oc2 + 1)],
                                  start=(ic == 0), stop=True)
                wait("v", mt)
                d_ev[pp] = run("v", vE, "tensor_tensor",
                               accum[:, D * tt + 512 * oc2:D * tt + 512 * (oc2 + 1)],
                               accum[:, D * tt + 512 * oc2:D * tt + 512 * (oc2 + 1)],
                               ps[:, 0:512], ALU.add)
                if s_ == NSEG - 1 and oc2 == NOC - 1:
                    acc_fin[tt] = d_ev[pp]
        down_wm[s_] = (tE, tE.n)
        run0("v", "drain")
        if s_ + 2 < NSEG:
            wait("g", gu_cons[s_])
            gu_tok[s_ + 2] = dma("g", gu_buf[home].rearrange("p (c f) -> p c f", c=MC),
                                 blob_slice(OFF_GU + (s_ + 2) * D * 2 * SEG, D, 2 * SEG))
            gu_home[s_ + 2] = home
        if s_ + 1 < NSEG:
            wait("g", down_wm[s_])
            wd_tok[s_ + 1] = dma("g", wd_buf.rearrange("p (c o) -> p c o", c=ICS),
                                 blob_slice(OFF_WD + (s_ + 1) * SEG * D, SEG, D))

    out_toks = []
    for tt in range(NTT):
        wait("y", acc_fin[tt])
        out_toks.append(dma("y", out_own.ap()[128 * tt:128 * (tt + 1), :],
                            accum[:, D * tt:D * (tt + 1)]))
    for t in out_toks:
        wait("y", t)

    with nc.Block() as block:
        @block.tensor
        def _(eng):
            for f in ops["t"]:
                f(eng)

        @block.vector
        def _(eng):
            for f in ops["v"]:
                f(eng)

        @block.scalar
        def _(eng):
            for f in ops["s"]:
                f(eng)

        @block.sync
        def _(eng):
            for f in ops["y"]:
                f(eng)

        @block.gpsimd
        def _(eng):
            for f in ops["g"]:
                f(eng)

    stack.close()
    n_inst = sum(len(v) for v in ops.values())
    return nc, n_inst


_PROG_CACHE = {}


def _get_prog(key_cfg, debug=False):
    key = (tuple(sorted(key_cfg.items())), debug)
    if key not in _PROG_CACHE:
        _PROG_CACHE[key] = build_program(make_cfg(**key_cfg), debug=debug)
    return _PROG_CACHE[key]


def stage_inputs(cfg, x, Wq, bq, Wk, bk, Wv, bv, Wo, bo, w_gate_up, w_down):
    S, D, NH, INTER = cfg["S"], cfg["D"], cfg["NH"], cfg["INTER"]
    OCW, TOK, SEG, NSEG, BW = cfg["OCW"], cfg["TOK"], cfg["SEG"], cfg["NSEG"], cfg["BW"]
    f = np.float32
    x = np.asarray(x, dtype=f).reshape(S, D)
    WqT = np.ascontiguousarray(np.asarray(Wq, dtype=f).T)
    WkT = np.ascontiguousarray(np.asarray(Wk, dtype=f).T)
    WvT = np.ascontiguousarray(np.asarray(Wv, dtype=f).T)
    WoT = np.ascontiguousarray(np.asarray(Wo, dtype=f).T)
    WguT = np.ascontiguousarray(np.asarray(w_gate_up, dtype=f).T)
    WdT = np.ascontiguousarray(np.asarray(w_down, dtype=f).T)
    bq = np.asarray(bq, dtype=f).reshape(D)
    bk = np.asarray(bk, dtype=f).reshape(D)
    bv = np.asarray(bv, dtype=f).reshape(D)
    bo = np.asarray(bo, dtype=f).reshape(1, D)

    parts = [WoT.ravel()]
    for s_ in range(NSEG):
        seg = np.concatenate([WguT[:, SEG * s_:SEG * (s_ + 1)],
                              WguT[:, INTER + SEG * s_:INTER + SEG * (s_ + 1)]], axis=1)
        parts.append(np.ascontiguousarray(seg).ravel())
    for s_ in range(NSEG):
        parts.append(np.ascontiguousarray(WdT[SEG * s_:SEG * (s_ + 1), :]).ravel())
    blob = np.concatenate(parts).reshape(NCORE, BW)

    in_maps = []
    for c in range(NCORE):
        sl = slice(OCW * c, OCW * (c + 1))
        in_maps.append({
            "x_own": np.ascontiguousarray(x[TOK * c:TOK * (c + 1)]),
            "wq_t": np.ascontiguousarray(WqT[:, sl]),
            "wk_t": np.ascontiguousarray(WkT[:, sl]),
            "wv_t": np.ascontiguousarray(WvT[:, sl]),
            "bq_c": np.ascontiguousarray(bq[sl].reshape(OCW, 1)),
            "bk_c": np.ascontiguousarray(bk[sl].reshape(OCW, 1)),
            "bv_c": np.ascontiguousarray(bv[sl].reshape(1, OCW)),
            "bo_c": bo,
            "wblob": np.ascontiguousarray(blob[c:c + 1]),
        })
    return in_maps


def kernel(L0_attn_hidden_states, L0_attn_Wq, L0_attn_bq, L0_attn_Wk, L0_attn_bk,
           L0_attn_Wv, L0_attn_bv, L0_attn_Wo, L0_attn_bo,
           L0_ffn_w_gate_up, L0_ffn_w_down):
    key_cfg = dict(S=8192, D=2048, NH=16, INTER=8192)
    cfg = make_cfg(**key_cfg)
    nc, _ = _get_prog(key_cfg)
    in_maps = stage_inputs(cfg, L0_attn_hidden_states, L0_attn_Wq, L0_attn_bq,
                           L0_attn_Wk, L0_attn_bk, L0_attn_Wv, L0_attn_bv,
                           L0_attn_Wo, L0_attn_bo, L0_ffn_w_gate_up, L0_ffn_w_down)
    res = run_bass_kernel_spmd(nc, in_maps, core_ids=list(range(NCORE)))
    out = np.concatenate([res.results[c]["out_own"] for c in range(NCORE)], axis=0)
    return out.reshape(1, cfg["S"], cfg["D"]).astype(np.float32)
